# version 06 (final)
# speedup vs baseline: 1.0210x; 1.0197x over previous
import numpy as np
import ml_dtypes
from contextlib import ExitStack

import concourse.bass as bass
import concourse.mybir as mybir
from concourse.bass_utils import run_bass_kernel_spmd

F32 = mybir.dt.float32
BF16 = mybir.dt.bfloat16
I32 = mybir.dt.int32
U8 = mybir.dt.uint8
AF = mybir.ActivationFunctionType
ALU = mybir.AluOpType
AX = mybir.AxisListType
BF = ml_dtypes.bfloat16

B, S, D = 4, 4096, 1024
H, HD = 16, 64
LD = 256
TOK = 2048
NSLOT, SLOT = 4, 512
EXT = [2, 4, 6, 8]
Q0 = {0: [0, 1536, 2048, 3584], 1: [512, 1024, 2560, 3072]}
NEG = -1.0e30
TOPK = 512
IDXH, IDXD = 4, 64
FFI = 3072
EPS = 1e-5
DC = D // 128

ENGS = ("sync", "tensor", "scalar", "vector", "gpsimd")



class Buf:
    __slots__ = ("name", "w", "r", "kind", "wg")

    def __init__(self, name, kind="s"):
        self.name = name
        self.kind = kind
        self.w = None
        self.r = []
        self.wg = {}


class Step:
    __slots__ = ("eng", "fn", "deps", "idx", "inc_ord", "amt", "tag", "semo", "gsem",
                 "gidx")

    def __init__(self, eng, fn, deps, amt, tag, semo=False, gsem=None):
        self.eng, self.fn, self.deps, self.amt, self.tag = eng, fn, deps, amt, tag
        self.inc_ord = None
        self.semo = semo
        self.gsem = gsem

    @property
    def key(self):
        return self.gsem or self.eng


class Sched:

    def __init__(self, pr):
        self.pr = pr
        self.steps = {e: [] for e in ENGS}
        self.all = []
        self.barriers = []

    def buf(self, name, kind="s"):
        return Buf(name, kind)

    def barrier(self):
        self.barriers.append({e: len(self.steps[e]) for e in ENGS})

    def step(self, eng, fn, reads=(), writes=(), tag="", dma=False, collective=False,
             gsem=None):
        deps = set()
        for b_ in reads:
            if b_.w is not None:
                deps.add(b_.w)
            deps.update(b_.wg.values())
        for b_ in writes:
            if b_.w is not None:
                deps.add(b_.w)
            deps.update(b_.wg.values())
            for rd in b_.r:
                deps.add(rd)
        if dma and not collective and gsem is None:
            sb = ([b_ for b_ in writes if b_.kind == "s"]
                  or [b_ for b_ in reads if b_.kind == "s"]
                  or list(writes) or list(reads))
            gsem = "g_" + sb[0].name if sb else "g_misc"
        st = Step(eng, fn, deps, 16 if (dma and not collective) else 1, tag,
                  semo=collective, gsem=gsem)
        st.idx = (eng, len(self.steps[eng]))
        st.gidx = len(self.all)
        self.steps[eng].append(st)
        self.all.append(st)
        for b_ in reads:
            b_.r.append(st)
        for b_ in writes:
            b_.w = st
            b_.r = []
            if dma:
                b_.wg[gsem] = st
            else:
                b_.wg.clear()
        return st

    def emit(self, block, sems, final_outputs=()):
        pr = self.pr
        import bisect as _bisect
        bars_by_eng = {e: sorted(b[e] for b in self.barriers) for e in ENGS}
        gsegs = {}
        for e in ENGS:
            for i, st in enumerate(self.steps[e]):
                if st.gsem and st.gsem != "cc":
                    seg = _bisect.bisect_right(bars_by_eng[e], i)
                    gsegs.setdefault(st.gsem, set()).add(seg)
        pool = []
        assign = {}
        for g in sorted(gsegs):
            for bi, used in enumerate(pool):
                if used.isdisjoint(gsegs[g]):
                    assign[g] = bi
                    used |= gsegs[g]
                    break
            else:
                assign[g] = len(pool)
                pool.append(set(gsegs[g]))
        for st in self.all:
            if st.gsem and st.gsem != "cc":
                st.gsem = f"P{assign[st.gsem]}"
        gnames = sorted({st.gsem for st in self.all if st.gsem})
        for gn in gnames:
            sems[gn] = pr.stack.enter_context(pr.nc.semaphore("gsem_" + gn))
        keys = list(ENGS) + gnames

        needed = set()
        for st in self.all:
            if st.amt == 16 or st.semo or st.gsem:
                needed.add(st)
            for dp in st.deps:
                if dp.eng != st.eng or dp.amt == 16 or dp.semo:
                    needed.add(dp)
        for bar in self.barriers:
            for e in ENGS:
                if bar[e] > 0:
                    needed.add(self.steps[e][bar[e] - 1])
        for e in ENGS:
            if self.steps[e]:
                needed.add(self.steps[e][-1])
        cum = {k: 0 for k in keys}
        for st in self.all:
            if st in needed:
                cum[st.key] += st.amt
                st.inc_ord = cum[st.key]
        totals = dict(cum)

        bar_cnt = []
        for bar in self.barriers:
            cnt = {k: 0 for k in keys}
            for e in ENGS:
                for st in self.steps[e][:bar[e]]:
                    if st in needed:
                        cnt[st.key] += st.amt
            bar_cnt.append(cnt)

        barrier_at = {e: {} for e in ENGS}
        for bi, bar in enumerate(self.barriers):
            for e in ENGS:
                barrier_at[e].setdefault(bar[e], []).append(bi)

        def run_engine(ename):
            def body(eng):
                watermark = {k: 0 for k in keys}
                last_drain = -1
                for i, st in enumerate(self.steps[ename]):
                    for bi in barrier_at[ename].get(i, ()):
                        for k in keys:
                            v = bar_cnt[bi][k]
                            if v > watermark[k]:
                                watermark[k] = v
                                eng.wait_ge(sems[k], v)
                    need_drain = False
                    for dp in st.deps:
                        if (dp.eng == ename and dp.amt != 16 and not dp.semo
                                and dp.gsem is None):
                            if dp.idx[1] > last_drain and ename != "tensor":
                                need_drain = True
                        else:
                            v = dp.inc_ord
                            assert v is not None, (st.tag, dp.tag)
                            if v > watermark[dp.key]:
                                watermark[dp.key] = v
                                eng.wait_ge(sems[dp.key], v)
                    if need_drain:
                        eng.drain()
                        last_drain = i - 1
                    inst = st.fn(eng)
                    if st.inc_ord is not None:
                        assert inst is not None, st.tag
                        inst.then_inc(sems[st.key], st.amt)
            return body

        for e in ENGS:
            if self.steps[e]:
                getattr(block, e)(run_engine(e))

        def fin(eng):
            for k in keys:
                if k != "sync" and totals[k]:
                    eng.wait_ge(sems[k], totals[k])
            if totals["sync"]:
                eng.wait_ge(sems["sync"], totals["sync"])
        block.sync(fin)


class Prog:
    def __init__(self):
        self.nc = bass.Bass("TRN2", target_bir_lowering=False, debug=False)
        self.stack = ExitStack()
        self.sch = Sched(self)
        self._sb_names = set()
        self.ns = ""

    def din(self, name, shape, dt=F32):
        return self.nc.dram_tensor(self.ns + name, list(shape), dt,
                                   kind="ExternalInput").ap()

    def dout(self, name, shape, dt=F32):
        return self.nc.dram_tensor(self.ns + name, list(shape), dt,
                                   kind="ExternalOutput").ap()

    def dram(self, name, shape, dt=F32):
        return self.nc.dram_tensor(self.ns + name, list(shape), dt)

    def sb(self, st, name, shape, dt=F32):
        name = self.ns + name
        assert name not in self._sb_names, name
        self._sb_names.add(name)
        return st.enter_context(self.nc.sbuf_tensor(name, list(shape), dt))

    def ps(self, st, name, shape, dt=F32):
        return st.enter_context(self.nc.psum_tensor(self.ns + name, list(shape), dt))



def _own_rows(parity):
    return np.concatenate([np.arange(q0, q0 + SLOT) for q0 in Q0[parity]])


def _causal_masks(parity, layer):
    out = np.zeros((NSLOT, 2, SLOT, SLOT), np.float32)
    for s_ in range(NSLOT):
        q0 = Q0[parity][s_]
        k_base = (EXT[s_] - 2) * SLOT
        q = q0 + np.arange(SLOT)[None, :]
        k = k_base + np.arange(2 * SLOT)[:, None]
        m = np.where(k <= q, 0.0, NEG).astype(np.float32)
        if layer == 'l1' and q0 == 0:
            m = np.where(k < SLOT, 0.0, NEG).astype(np.float32)
        out[s_, 0] = m[:SLOT]
        out[s_, 1] = m[SLOT:]
    return np.ascontiguousarray(out.astype(BF))


_IDENT_BF = np.eye(128, dtype=BF)
_IDENT_F32 = np.eye(128, dtype=np.float32)
_ONES_BF = np.ones((128, 1), dtype=BF)
_ONES_F32 = np.ones((128, 1), dtype=np.float32)


def _unshard(per_core_arrays):
    x = np.zeros((B, S, D), np.float32)
    for c in range(8):
        x[c // 2][_own_rows(c % 2)] = per_core_arrays[c]
    return x



class Ctx:
    pass


class _UseBlock:

    def __init__(self, block):
        self.block = block

    def __enter__(self):
        return self.block

    def __exit__(self, *a):
        return False


def emit_consts(pr, st):
    cx = Ctx()
    cx.ident_bf_d = pr.din("ident_bf", [128, 128], BF16)
    cx.ident_f_d = pr.din("ident_f", [128, 128], F32)
    cx.ones_bf_d = pr.din("ones_bf", [128, 1], BF16)
    cx.ones_row_d = pr.din("ones_row", [1, 128], F32)
    cx.ones_f_d = pr.din("ones_f", [128, 1], F32)
    cx.ident_bf = pr.sb(st, "c_ident_bf", [128, 128], BF16)
    cx.ident_f = pr.sb(st, "c_ident_f", [128, 128], F32)
    cx.ones_bf = pr.sb(st, "c_ones_bf", [128, 1], BF16)
    cx.ones_row = pr.sb(st, "c_ones_row", [1, 128], F32)
    cx.ones_f = pr.sb(st, "c_ones_f", [128, 1], F32)
    sch = pr.sch
    cx.b_ident_bf, cx.b_ident_f = sch.buf("ident_bf"), sch.buf("ident_f")
    cx.b_ones_bf, cx.b_ones_f = sch.buf("ones_bf"), sch.buf("ones_f")
    cx.b_ones_row = sch.buf("ones_row")
    for t, d_, b_ in ((cx.ident_bf, cx.ident_bf_d, cx.b_ident_bf),
                     (cx.ident_f, cx.ident_f_d, cx.b_ident_f),
                     (cx.ones_bf, cx.ones_bf_d, cx.b_ones_bf),
                     (cx.ones_row, cx.ones_row_d, cx.b_ones_row),
                     (cx.ones_f, cx.ones_f_d, cx.b_ones_f)):
        sch.step("sync", lambda e, t=t, d=d_: e.dma_start(out=t[:], in_=d[:]),
                 writes=[b_], dma=True, tag="const")
    return cx


def const_inputs():
    return {"ident_bf": _IDENT_BF, "ident_f": _IDENT_F32,
            "ones_bf": _ONES_BF, "ones_f": _ONES_F32,
            "ones_row": np.ones((1, 128), np.float32)}


def emit_ln_to_fm(pr, st, cx, name, src_dram, n_tiles, fm_chunks, fm_col0,
                  keep_tm=None, src_buf=None):
    sch = pr.sch
    xb = [pr.sb(st, f"{name}_x{j}", [128, D], F32) for j in range(2)]
    bxb = [sch.buf(f"{name}_x{j}") for j in range(2)]
    ssq = [pr.sb(st, f"{name}_q{j}", [128, 1], F32) for j in range(2)]
    bssq = [sch.buf(f"{name}_q{j}") for j in range(2)]
    sscr = pr.sb(st, f"{name}_scr", [128, D], F32)
    bscr = sch.buf(f"{name}_scr")
    stat = [pr.sb(st, f"{name}_st{j}", [128, 4], F32) for j in range(2)]
    bstat = [sch.buf(f"{name}_st{j}") for j in range(2)]
    lnt = [pr.sb(st, f"{name}_l{j}", [128, D], BF16) for j in range(2)]
    blnt = [sch.buf(f"{name}_l{j}") for j in range(2)]
    ptr = [pr.ps(st, f"{name}_p{j}", [128, D], BF16) for j in range(2)]
    bptr = [sch.buf(f"{name}_p{j}") for j in range(2)]

    for i in range(n_tiles):
        j = i % 2
        x, bx = xb[j], bxb[j]
        sq, bsq = ssq[j], bssq[j]
        stt, bst = stat[j], bstat[j]
        ln, bln = lnt[j], blnt[j]
        pt, bpt = ptr[j], bptr[j]
        mean, var, rstd, nmr = (stt[:, 0:1], stt[:, 1:2], stt[:, 2:3], stt[:, 3:4])
        sch.step("sync", lambda e, x=x, i=i: e.dma_start(out=x[:], in_=src_dram(i)),
                 writes=[bx], dma=True, tag=f"{name}_ld{i}")
        sch.step("scalar", lambda e, x=x, sq=sq: e.activation(
            out=sscr[:], in_=x[:], func=AF.Square, accum_out=sq[:]),
            reads=[bx], writes=[bscr, bsq], tag=f"{name}_sq{i}")
        sch.step("vector", lambda e, x=x, mean=mean: e.tensor_reduce(
            mean, x[:], axis=AX.X, op=ALU.add), reads=[bx], writes=[bst],
            tag=f"{name}_sum{i}")
        sch.step("vector", lambda e, mean=mean: e.tensor_scalar(
            out=mean, in0=mean, scalar1=1.0 / D, scalar2=None, op0=ALU.mult),
            reads=[bst], writes=[bst], tag=f"{name}_mn{i}")
        sch.step("vector", lambda e, var=var, sq=sq: e.tensor_scalar(
            out=var, in0=sq[:], scalar1=1.0 / D, scalar2=None, op0=ALU.mult),
            reads=[bsq, bst], writes=[bst], tag=f"{name}_v1{i}")
        sch.step("vector", lambda e, nmr=nmr, mean=mean: e.tensor_tensor(
            out=nmr, in0=mean, in1=mean, op=ALU.mult),
            reads=[bst], writes=[bst], tag=f"{name}_m2{i}")
        sch.step("vector", lambda e, var=var, nmr=nmr: e.tensor_tensor(
            out=var, in0=var, in1=nmr, op=ALU.subtract),
            reads=[bst], writes=[bst], tag=f"{name}_v2{i}")
        sch.step("vector", lambda e, var=var: e.tensor_scalar(
            out=var, in0=var, scalar1=EPS, scalar2=None, op0=ALU.add),
            reads=[bst], writes=[bst], tag=f"{name}_v3{i}")
        sch.step("scalar", lambda e, rstd=rstd, var=var: e.activation(
            out=rstd, in_=var, func=AF.Ln), reads=[bst], writes=[bst],
            tag=f"{name}_ln{i}")
        sch.step("scalar", lambda e, rstd=rstd: e.activation(
            out=rstd, in_=rstd, func=AF.Exp, scale=-0.5),
            reads=[bst], writes=[bst], tag=f"{name}_ex{i}")
        sch.step("vector", lambda e, nmr=nmr, mean=mean, rstd=rstd:
                 e.scalar_tensor_tensor(out=nmr, in0=mean, scalar=-1.0,
                                        in1=rstd, op0=ALU.mult, op1=ALU.mult),
                 reads=[bst], writes=[bst], tag=f"{name}_nm{i}")
        sch.step("scalar", lambda e, ln=ln, x=x, rstd=rstd, nmr=nmr:
                 e.activation(out=ln[:], in_=x[:], func=AF.Identity,
                              bias=nmr, scale=rstd),
                 reads=[bx, bst], writes=[bln], tag=f"{name}_ap{i}")
        if keep_tm is not None:
            dst_t, dst_b = keep_tm
            sch.step("vector", lambda e, ln=ln, i=i: e.tensor_copy(
                out=dst_t[:, i * D:(i + 1) * D], in_=ln[:]),
                reads=[bln], writes=[dst_b], tag=f"{name}_tm{i}")
        for jj in range(DC):
            sch.step("tensor", lambda e, ln=ln, pt=pt, jj=jj: e.transpose(
                pt[:, jj * 128:(jj + 1) * 128], ln[:, jj * 128:(jj + 1) * 128],
                cx.ident_bf[:]),
                reads=[bln, cx.b_ident_bf], writes=[bpt], tag=f"{name}_tr{i}_{jj}")
        for jj in range(DC):
            fmt, fmb = fm_chunks[jj]
            c0 = fm_col0 + i * 128
            sch.step("vector", lambda e, pt=pt, jj=jj, fmt=fmt, c0=c0:
                     e.tensor_copy(out=fmt[:, c0:c0 + 128],
                                   in_=pt[:, jj * 128:(jj + 1) * 128]),
                     reads=[bpt], writes=[fmb], tag=f"{name}_ev{i}_{jj}")


def emit_mm_chain(pr, sch, name, out_ps, bout, lhs_list, rhs_list, reads,
                  extra_first=None):
    n = len(lhs_list)
    for i in range(n):
        first = (i == 0 and extra_first is None)
        sch.step("tensor", lambda e, i=i, first=first:
                 e.matmul(out_ps, lhs_list[i], rhs_list[i],
                          start=first, stop=(i == n - 1)),
                 reads=list(reads), writes=[bout], tag=f"{name}_mm{i}")



def build_stage_a(dbg=False, spill_out=False, fx=None):
    if fx is None:
        pr = Prog()
        st = pr.stack
    else:
        pr = fx["pr"]
        st = fx["st"]
    sch = pr.sch
    nc = pr.nc
    XR = [fx["x_in_buf"]] if fx else []
    XW = [fx["x_out_buf"]] if fx else []

    x_own_d = fx["x_in"] if fx else pr.din("x_own", [TOK, D])
    x_bat_d = pr.din("x_batch", [S, D])
    wq_d = pr.din("WqT", [D, H * HD])
    wkvd_d = pr.din("Wkvd", [D, LD])
    wku_d = pr.din("WkuT", [LD, H * HD])
    wvu_d = pr.din("WvuT", [LD, H * HD])
    wout_d = pr.din("WoutT", [H * HD, D])
    mask_d = pr.din("mask", [NSLOT, 2, SLOT, SLOT], BF16)
    mk = pr.dout if (dbg or spill_out) else pr.dram
    k_sp = mk("k_sp", [32, 8, 128, 128], BF16)
    v_sp = mk("v_sp", [32, 128, 16 * 65], BF16)
    xo_d = fx["x_out"] if fx else pr.dout("x_out", [TOK, D])

    if fx is None:
        sems = {e: pr.stack.enter_context(nc.semaphore("sem_" + e)) for e in ENGS}
        blk_cm = nc.Block()
    else:
        sems = fx["sems"]
        blk_cm = _UseBlock(fx["block"])

    with blk_cm as block:
        cx = fx["cx"] if fx else emit_consts(pr, st)

        wq = pr.sb(st, "wq", [128, DC * 1024], BF16); b_wq = sch.buf("wq")
        wkvd = pr.sb(st, "wkvd", [128, DC * LD], BF16); b_wkvd = sch.buf("wkvd")
        wku = pr.sb(st, "wku", [128, 2 * 1024], BF16); b_wku = sch.buf("wku")
        wvu = pr.sb(st, "wvu", [128, 2 * 1024], BF16); b_wvu = sch.buf("wvu")
        wout = pr.sb(st, "wout", [128, DC * 1024], BF16); b_wout = sch.buf("wout")
        for c in range(DC):
            sch.step("gpsimd", lambda e, c=c: e.dma_start(
                out=wq[:, c * 1024:(c + 1) * 1024], in_=wq_d[c * 128:(c + 1) * 128, :]),
                writes=[b_wq], dma=True, tag=f"wq{c}")
            sch.step("gpsimd", lambda e, c=c: e.dma_start(
                out=wkvd[:, c * LD:(c + 1) * LD], in_=wkvd_d[c * 128:(c + 1) * 128, :]),
                writes=[b_wkvd], dma=True, tag=f"wkvd{c}")
            sch.step("gpsimd", lambda e, c=c: e.dma_start(
                out=wout[:, c * 1024:(c + 1) * 1024], in_=wout_d[c * 128:(c + 1) * 128, :]),
                writes=[b_wout], dma=True, tag=f"wout{c}")
        for l in range(2):
            sch.step("gpsimd", lambda e, l=l: e.dma_start(
                out=wku[:, l * 1024:(l + 1) * 1024], in_=wku_d[l * 128:(l + 1) * 128, :]),
                writes=[b_wku], dma=True, tag=f"wku{l}")
            sch.step("gpsimd", lambda e, l=l: e.dma_start(
                out=wvu[:, l * 1024:(l + 1) * 1024], in_=wvu_d[l * 128:(l + 1) * 128, :]),
                writes=[b_wvu], dma=True, tag=f"wvu{l}")

        lat = [pr.sb(st, f"lat{l}", [128, S], BF16) for l in range(2)]
        b_lat = [sch.buf(f"lat{l}") for l in range(2)]
        qsb = [pr.sb(st, f"qsb{c}", [128, TOK], BF16) for c in range(DC)]
        b_qsb = [sch.buf(f"qsb{c}") for c in range(DC)]
        nrm = pr.sb(st, "nrm", [1, 16], F32)
        b_nrm = sch.buf("nrm")
        knm = pr.sb(st, "knm", [1, 16], F32)
        b_knm = sch.buf("knm")
        qnm = pr.sb(st, "qnm", [1, 16], F32)
        b_qnm = sch.buf("qnm")
        ustars = pr.sb(st, "ustars", [128, 16], F32)
        b_ustars = sch.buf("ustars")

        ph13 = ExitStack()
        gfm = [[pr.sb(ph13, f"gfm{r}_{c}", [128, 512], BF16) for c in range(DC)]
               for r in range(2)]
        b_gfm = [sch.buf(f"gfm{r}") for r in range(2)]
        xb = [pr.sb(ph13, f"lnx{j}", [128, D], F32) for j in range(2)]
        bxb = [sch.buf(f"lnx{j}") for j in range(2)]
        ssq = [pr.sb(ph13, f"lnq{j}", [128, 1], F32) for j in range(2)]
        bssq = [sch.buf(f"lnq{j}") for j in range(2)]
        sscr = pr.sb(ph13, "lnscr", [128, D], F32)
        bscr = sch.buf("lnscr")
        stat = [pr.sb(ph13, f"lnst{j}", [128, 4], F32) for j in range(2)]
        bstat = [sch.buf(f"lnst{j}") for j in range(2)]
        lnt = [pr.sb(ph13, f"lnl{j}", [128, D], BF16) for j in range(2)]
        blnt = [sch.buf(f"lnl{j}") for j in range(2)]
        ptr = [pr.ps(ph13, f"lnp{j}", [128, D], BF16) for j in range(2)]
        bptr = [sch.buf(f"lnp{j}") for j in range(2)]
        acc = [pr.ps(ph13, f"acc{j}", [128, 512], F32) for j in range(2)]
        b_acc = [sch.buf(f"acc{j}") for j in range(2)]
        psn = [pr.ps(ph13, f"psn{j}", [1, 512], F32) for j in range(2)]
        b_psn = [sch.buf(f"psn{j}") for j in range(2)]

        def ln_tile(i, src_ap, act_scale=None, xrd=()):
            j = i % 2
            x, bx, sq, bsq = xb[j], bxb[j], ssq[j], bssq[j]
            stt, bst, ln, bln = stat[j], bstat[j], lnt[j], blnt[j]
            mean, var, rstd, nmr = stt[:, 0:1], stt[:, 1:2], stt[:, 2:3], stt[:, 3:4]
            sch.step("sync", lambda e: e.dma_start(out=x[:], in_=src_ap),
                     reads=list(xrd), writes=[bx], dma=True, tag=f"lnld{i}")
            sch.step("scalar", lambda e: e.activation(
                out=sscr[:], in_=x[:], func=AF.Square, accum_out=sq[:]),
                reads=[bx], writes=[bscr, bsq], tag=f"lnsq{i}")
            sch.step("vector", lambda e: e.tensor_reduce(mean, x[:], axis=AX.X, op=ALU.add),
                     reads=[bx], writes=[bst], tag=f"lnsum{i}")
            sch.step("vector", lambda e: e.tensor_scalar(
                out=mean, in0=mean, scalar1=1.0 / D, scalar2=None, op0=ALU.mult),
                reads=[bst], writes=[bst], tag=f"lnmn{i}")
            sch.step("vector", lambda e: e.tensor_scalar(
                out=var, in0=sq[:], scalar1=1.0 / D, scalar2=None, op0=ALU.mult),
                reads=[bsq, bst], writes=[bst], tag=f"lnv1{i}")
            sch.step("vector", lambda e: e.tensor_tensor(
                out=nmr, in0=mean, in1=mean, op=ALU.mult),
                reads=[bst], writes=[bst], tag=f"lnm2{i}")
            sch.step("vector", lambda e: e.tensor_tensor(
                out=var, in0=var, in1=nmr, op=ALU.subtract),
                reads=[bst], writes=[bst], tag=f"lnv2{i}")
            sch.step("vector", lambda e: e.tensor_scalar(
                out=var, in0=var, scalar1=EPS, scalar2=None, op0=ALU.add),
                reads=[bst], writes=[bst], tag=f"lnv3{i}")
            sch.step("scalar", lambda e: e.activation(out=rstd, in_=var, func=AF.Ln),
                     reads=[bst], writes=[bst], tag=f"lnln{i}")
            sch.step("scalar", lambda e: e.activation(
                out=rstd, in_=rstd, func=AF.Exp, scale=-0.5),
                reads=[bst], writes=[bst], tag=f"lnex{i}")
            sch.step("vector", lambda e: e.scalar_tensor_tensor(
                out=nmr, in0=mean, scalar=-1.0, in1=rstd, op0=ALU.mult, op1=ALU.mult),
                reads=[bst], writes=[bst], tag=f"lnnm{i}")
            sch.step("scalar", lambda e: e.activation(
                out=ln[:], in_=x[:], func=AF.Identity, bias=nmr, scale=rstd),
                reads=[bx, bst], writes=[bln], tag=f"lnap{i}")
            return ln, bln

        def transpose_to(i, ln, bln, dst_list, dst_buf, col0):
            j = i % 2
            pt, bpt = ptr[j], bptr[j]
            for c in range(DC):
                sch.step("tensor", lambda e, c=c: e.transpose(
                    pt[:, c * 128:(c + 1) * 128], ln[:, c * 128:(c + 1) * 128],
                    cx.ident_bf[:]),
                    reads=[bln, cx.b_ident_bf], writes=[bpt], tag=f"tr{i}_{c}")
            for c in range(DC):
                sch.step("vector", lambda e, c=c: e.tensor_copy(
                    out=dst_list[c][:, col0:col0 + 128],
                    in_=pt[:, c * 128:(c + 1) * 128]),
                    reads=[bpt], writes=[dst_buf], tag=f"ev{i}_{c}")

        ti = 0
        for g in range(8):
            r = g % 2
            for u in range(4):
                ln, bln = ln_tile(ti, x_bat_d[(g * 4 + u) * 128:(g * 4 + u + 1) * 128, :])
                transpose_to(ti, ln, bln, gfm[r], b_gfm[r], u * 128)
                ti += 1
            for l in range(2):
                pa, ba = acc[(g * 2 + l) % 2], b_acc[(g * 2 + l) % 2]
                for c in range(DC):
                    sch.step("tensor", lambda e, c=c, l=l, r=r, pa=pa: e.matmul(
                        pa[:], wkvd[:, c * LD + l * 128:c * LD + (l + 1) * 128],
                        gfm[r][c][:], start=(c == 0), stop=(c == DC - 1)),
                        reads=[b_wkvd, b_gfm[r]], writes=[ba], tag=f"latm{g}_{l}_{c}")
                sch.step("scalar", lambda e, l=l, g=g, pa=pa: e.activation(
                    out=lat[l][:, g * 512:(g + 1) * 512], in_=pa[:], func=AF.Identity),
                    reads=[ba], writes=[b_lat[l]], tag=f"late{g}_{l}")
        for qt in range(4):
            r = ti % 2
            for u in range(4):
                ln, bln = ln_tile(ti, x_own_d[(qt * 4 + u) * 128:(qt * 4 + u + 1) * 128, :],
                                  xrd=XR)
                transpose_to(ti, ln, bln, gfm[r], b_gfm[r], u * 128)
                ti += 1
            for hc in range(DC):
                pa, ba = acc[(qt * DC + hc) % 2], b_acc[(qt * DC + hc) % 2]
                for c in range(DC):
                    sch.step("tensor", lambda e, c=c, hc=hc, r=r, pa=pa: e.matmul(
                        pa[:], wq[:, c * 1024 + hc * 128:c * 1024 + (hc + 1) * 128],
                        gfm[r][c][:], start=(c == 0), stop=(c == DC - 1)),
                        reads=[b_wq, b_gfm[r]], writes=[ba], tag=f"qm{qt}_{hc}_{c}")
                sch.step("scalar", lambda e, hc=hc, qt=qt, pa=pa: e.activation(
                    out=qsb[hc][:, qt * 512:(qt + 1) * 512], in_=pa[:],
                    func=AF.Identity, scale=0.125),
                    reads=[ba], writes=[b_qsb[hc]], tag=f"qe{qt}_{hc}")

        kst = [pr.sb(ph13, f"kst{j}", [128, 512], BF16) for j in range(2)]
        b_kst = [sch.buf(f"kst{j}") for j in range(2)]
        ksq = [pr.sb(ph13, f"ksq{j}", [128, 512], BF16) for j in range(2)]
        b_ksq = [sch.buf(f"ksq{j}") for j in range(2)]
        b_ksp = sch.buf("k_sp", kind="d")
        b_vspb = sch.buf("v_sp", kind="d")
        sch.step("vector", lambda e: e.memset(nrm[:], 0.0), writes=[b_nrm], tag="nrm0")
        sch.step("vector", lambda e: e.memset(knm[:], 0.0), writes=[b_knm], tag="knm0")
        sch.step("vector", lambda e: e.memset(qnm[:], 0.0), writes=[b_qnm], tag="qnm0")
        for kt in range(8):
            for hc in range(DC):
                j = (kt * DC + hc) % 2
                pa, ba = acc[j], b_acc[j]
                for l in range(2):
                    sch.step("tensor", lambda e, l=l, hc=hc, kt=kt, pa=pa: e.matmul(
                        pa[:], wku[:, l * 1024 + hc * 128:l * 1024 + (hc + 1) * 128],
                        lat[l][:, kt * 512:(kt + 1) * 512],
                        start=(l == 0), stop=(l == 1)),
                        reads=[b_wku, b_lat[l]], writes=[ba], tag=f"km{kt}_{hc}_{l}")
                sch.step("scalar", lambda e, j=j, pa=pa: e.activation(
                    out=kst[j][:], in_=pa[:], func=AF.Identity),
                    reads=[ba], writes=[b_kst[j]], tag=f"ke{kt}_{hc}")
                sch.step("scalar", lambda e, j=j: e.activation(
                    out=ksq[j][:], in_=kst[j][:], func=AF.Square),
                    reads=[b_kst[j]], writes=[b_ksq[j]], tag=f"kq{kt}_{hc}")
                for par in range(2):
                    pn, bn = psn[par], b_psn[par]
                    sch.step("tensor", lambda e, j=j, par=par, pn=pn: e.matmul(
                        pn[:], cx.ones_bf[par * 64:(par + 1) * 64, :],
                        ksq[j][par * 64:(par + 1) * 64, :],
                        start=True, stop=True, skip_group_check=True),
                        reads=[b_ksq[j], cx.b_ones_bf], writes=[bn], tag=f"kn{kt}_{hc}_{par}")
                for par in range(2):
                    hh = hc * 2 + par
                    pn, bn = psn[par], b_psn[par]
                    sch.step("vector", lambda e, pn=pn: e.reduce_max(
                        nrm[0:1, 1:2], pn[0:1, :], axis=AX.X),
                        reads=[bn], writes=[b_nrm], tag=f"knr{kt}_{hc}_{par}")
                    sch.step("vector", lambda e, hh=hh: e.tensor_tensor(
                        out=knm[0:1, hh:hh + 1], in0=knm[0:1, hh:hh + 1],
                        in1=nrm[0:1, 1:2], op=ALU.max),
                        reads=[b_nrm, b_knm], writes=[b_knm], tag=f"knx{kt}_{hc}_{par}")
                sch.step("sync", lambda e, j=j, kt=kt, hc=hc: e.dma_start(
                    out=k_sp[kt * 4:(kt + 1) * 4, hc].rearrange("a p c -> p a c"),
                    in_=kst[j][:].rearrange("p (a c) -> p a c", c=128)),
                    reads=[b_kst[j]], writes=[b_ksp], dma=True, tag=f"ks{kt}_{hc}")


        vst = [pr.sb(ph13, f"vst{j}", [128, 16 * 65], BF16) for j in range(2)]
        b_vst = [sch.buf(f"vst{j}") for j in range(2)]
        for kt in range(32):
            j = kt % 2
            sch.step("vector", lambda e, j=j: e.memset(
                vst[j][:].rearrange("p (h c) -> p h c", c=65)[:, :, 64:65], 1.0),
                writes=[b_vst[j]], tag=f"vo{kt}")
            for half in range(2):
                pa, ba = acc[half], b_acc[half]
                for l in range(2):
                    sch.step("tensor", lambda e, l=l, kt=kt, half=half, pa=pa: e.matmul(
                        pa[:], lat[l][:, kt * 128:(kt + 1) * 128],
                        wvu[:, l * 1024 + half * 512:l * 1024 + (half + 1) * 512],
                        start=(l == 0), stop=(l == 1)),
                        reads=[b_wvu, b_lat[l]], writes=[ba], tag=f"vm{kt}_{half}_{l}")
                sch.step("scalar", lambda e, j=j, half=half, pa=pa: e.activation(
                    out=vst[j][:].rearrange("p (h c) -> p h c", c=65)
                    [:, half * 8:(half + 1) * 8, 0:64],
                    in_=pa[:].rearrange("p (h c) -> p h c", c=64),
                    func=AF.Identity),
                    reads=[ba], writes=[b_vst[j]], tag=f"ve{kt}_{half}")
            sch.step("sync", lambda e, j=j, kt=kt: e.dma_start(
                out=v_sp[kt], in_=vst[j][:]),
                reads=[b_vst[j]], writes=[b_vspb], dma=True, tag=f"vs{kt}")

        qq = [pr.sb(ph13, f"qq{j}", [128, 512], BF16) for j in range(2)]
        b_qq = [sch.buf(f"qq{j}") for j in range(2)]
        for qt in range(4):
            for hc in range(DC):
                j = (qt * DC + hc) % 2
                sch.step("vector", lambda e, j=j, hc=hc, qt=qt: e.tensor_tensor(
                    out=qq[j][:], in0=qsb[hc][:, qt * 512:(qt + 1) * 512],
                    in1=qsb[hc][:, qt * 512:(qt + 1) * 512], op=ALU.mult),
                    reads=[b_qsb[hc]], writes=[b_qq[j]], tag=f"qsq{qt}_{hc}")
                for par in range(2):
                    pn, bn = psn[par], b_psn[par]
                    sch.step("tensor", lambda e, j=j, par=par, pn=pn: e.matmul(
                        pn[:], cx.ones_bf[par * 64:(par + 1) * 64, :],
                        qq[j][par * 64:(par + 1) * 64, :],
                        start=True, stop=True, skip_group_check=True),
                        reads=[b_qq[j], cx.b_ones_bf], writes=[bn], tag=f"qn{qt}_{hc}_{par}")
                for par in range(2):
                    hh = hc * 2 + par
                    pn, bn = psn[par], b_psn[par]
                    sch.step("vector", lambda e, pn=pn: e.reduce_max(
                        nrm[0:1, 2:3], pn[0:1, :], axis=AX.X),
                        reads=[bn], writes=[b_nrm], tag=f"qnr{qt}_{hc}_{par}")
                    sch.step("vector", lambda e, hh=hh: e.tensor_tensor(
                        out=qnm[0:1, hh:hh + 1], in0=qnm[0:1, hh:hh + 1],
                        in1=nrm[0:1, 2:3], op=ALU.max),
                        reads=[b_nrm, b_qnm], writes=[b_qnm], tag=f"qnx{qt}_{hc}_{par}")
        sch.step("vector", lambda e: e.tensor_tensor(
            out=nrm[0:1, 0:16], in0=qnm[0:1, :], in1=knm[0:1, :], op=ALU.mult),
            reads=[b_qnm, b_knm], writes=[b_nrm], tag="u2")
        sch.step("scalar", lambda e: e.activation(
            out=nrm[0:1, 0:16], in_=nrm[0:1, 0:16], func=AF.Sqrt),
            reads=[b_nrm], writes=[b_nrm], tag="usqrt")
        psb = pr.ps(ph13, "psb", [128, 512], F32)
        b_psb = sch.buf("psb")
        sch.step("tensor", lambda e: e.matmul(
            psb[:, 0:16], cx.ones_row[0:1, :],
            nrm[0:1, 0:16], start=True, stop=True, skip_group_check=True),
            reads=[b_nrm, cx.b_ones_row], writes=[b_psb], tag="ubc")
        sch.step("vector", lambda e: e.tensor_scalar(
            out=ustars[:], in0=psb[:, 0:16], scalar1=-1.0, scalar2=None, op0=ALU.mult),
            reads=[b_psb], writes=[b_ustars], tag="uneg")

        if dbg:
            dbg_lat = pr.dout("dbg_lat", [128, S], BF16)
            dbg_q = pr.dout("dbg_q", [128, TOK], BF16)
            dbg_nrm = pr.dout("dbg_nrm", [1, 16])
            dbg_us = pr.dout("dbg_us", [128, 1])
            sch.step("sync", lambda e: e.dma_start(out=dbg_lat[:], in_=lat[0][:]),
                     reads=[b_lat[0]], dma=True, tag="dbg1")
            sch.step("sync", lambda e: e.dma_start(out=dbg_q[:], in_=qsb[0][:]),
                     reads=[b_qsb[0]], dma=True, tag="dbg2")
            sch.step("sync", lambda e: e.dma_start(out=dbg_nrm[:], in_=nrm[:]),
                     reads=[b_nrm], dma=True, tag="dbg3")
            sch.step("sync", lambda e: e.dma_start(out=dbg_us[:], in_=ustars[:, 0:1]),
                     reads=[b_ustars], dma=True, tag="dbg4")
        if fx is not None and fx.get("dbga"):
            dbgu = pr.dout("dbgu", [128, 16])
            dbgk = pr.dout("dbgk", [1, 16])
            dbgq2 = pr.dout("dbgq2", [1, 16])
            dbglt = pr.dout("dbglt", [128, S], BF16)
            sch.step("sync", lambda e: e.dma_start(out=dbgu[:, :], in_=ustars[:]),
                     reads=[b_ustars], dma=True, tag="dbgu")
            sch.step("sync", lambda e: e.dma_start(out=dbgk[:, :], in_=knm[0:1, :]),
                     reads=[b_knm], dma=True, tag="dbgk")
            sch.step("sync", lambda e: e.dma_start(out=dbgq2[:, :], in_=qnm[0:1, :]),
                     reads=[b_qnm], dma=True, tag="dbgq2")
            sch.step("sync", lambda e: e.dma_start(out=dbglt[:, :], in_=lat[0][:]),
                     reads=[b_lat[0]], dma=True, tag="dbglt")

        ph13.close()
        sch.barrier()

        ph4 = ExitStack()
        qk = [pr.ps(ph4, f"qk{j}", [128, 512], F32) for j in range(2)]
        b_qk = [sch.buf(f"qk{j}") for j in range(2)]
        pv = [pr.ps(ph4, f"pv{j}", [65, 512], F32) for j in range(4)]
        b_pv = [sch.buf(f"pv{j}") for j in range(4)]
        ptbf = pr.ps(ph4, "ptbf", [128, D], BF16)
        b_ptbf = sch.buf("ptbf")
        spare = pr.ps(ph4, "spare", [128, 512], F32)
        b_spare = sch.buf("spare")

        msk = pr.sb(st, "msk", [128, 8 * 512], BF16)
        b_msk = sch.buf("msk")
        ktl = [pr.sb(st, f"ktl{r}", [128, 256], BF16) for r in range(3)]
        b_ktl = [sch.buf(f"ktl{r}") for r in range(3)]
        vtl = [pr.sb(st, f"vtl{r}", [128, 4 * 65], BF16) for r in range(3)]
        b_vtl = [sch.buf(f"vtl{r}") for r in range(3)]
        pt_t = [[pr.sb(st, f"pt{hl}_{j}", [128, 512], BF16) for j in range(2)]
                for hl in range(4)]
        b_pt = [[sch.buf(f"pt{hl}_{j}") for j in range(2)] for hl in range(4)]
        yfm = [pr.sb(st, f"yfm{c}", [128, 512], BF16) for c in range(DC)]
        b_yfm = [sch.buf(f"yfm{c}") for c in range(DC)]
        rsum = pr.sb(st, "rsum", [1, 512], F32)
        b_rsum = sch.buf("rsum")
        ytmp = pr.sb(st, "ytmp", [64, 512], F32)
        b_ytmp = sch.buf("ytmp")
        ao = [pr.sb(st, f"ao{c}", [128, 512], BF16) for c in range(DC)]
        b_ao = [sch.buf(f"ao{c}") for c in range(DC)]
        aot = [pr.sb(st, f"aot{j}", [128, D], BF16) for j in range(2)]
        b_aot = [sch.buf(f"aot{j}") for j in range(2)]
        estat = [pr.sb(st, f"est{j}", [128, 4], F32) for j in range(2)]
        b_estat = [sch.buf(f"est{j}") for j in range(2)]
        essq = [pr.sb(st, f"esq{j}", [128, 1], F32) for j in range(2)]
        b_essq = [sch.buf(f"esq{j}") for j in range(2)]
        escr = pr.sb(st, "escr", [128, D], F32)
        b_escr = sch.buf("escr")
        lnres = [pr.sb(st, f"lnr{j}", [128, D], F32) for j in range(2)]
        b_lnres = [sch.buf(f"lnr{j}") for j in range(2)]
        xrow = [pr.sb(st, f"xr{j}", [128, D], F32) for j in range(2)]
        b_xrow = [sch.buf(f"xr{j}") for j in range(2)]
        xout = [pr.sb(st, f"xo{j}", [128, D], F32) for j in range(2)]
        b_xout = [sch.buf(f"xo{j}") for j in range(2)]

        for s_ in range(NSLOT):
            Ek = EXT[s_] * 4
            q0c = s_ * 512
            sch.step("sync", lambda e, s_=s_: e.dma_start(
                out=msk[:].rearrange("p (m q) -> p m q", q=512),
                in_=mask_d[s_].rearrange("u (j p) q -> p (u j) q", p=128)),
                writes=[b_msk], dma=True, tag=f"mld{s_}")
            for hg in range(4):
                for kt in range(Ek):
                    r = kt % 3
                    sch.step("sync", lambda e, r=r, kt=kt, hg=hg: e.dma_start(
                        out=ktl[r][:].rearrange("p (a c) -> p a c", c=128),
                        in_=k_sp[kt, hg * 2:(hg + 1) * 2].rearrange("a p c -> p a c")),
                        reads=[b_ksp], writes=[b_ktl[r]], dma=True, tag=f"ktl{s_}_{hg}_{kt}")
                    sch.step("sync", lambda e, r=r, kt=kt, hg=hg: e.dma_start(
                        out=vtl[r][:], in_=v_sp[kt, :, hg * 4 * 65:(hg + 1) * 4 * 65]),
                        reads=[b_vspb], writes=[b_vtl[r]], dma=True, tag=f"vtl{s_}_{hg}_{kt}")
                    masked = kt >= Ek - 8
                    mcol = (kt - (Ek - 8)) * 512
                    for hl in range(4):
                        h = hg * 4 + hl
                        jq = hl % 2
                        pqk, bqk = qk[jq], b_qk[jq]
                        if masked:
                            sch.step("tensor", lambda e, pqk=pqk, mcol=mcol: e.matmul(
                                pqk[:], cx.ident_bf[:], msk[:, mcol:mcol + 512],
                                start=True, stop=False, skip_group_check=True),
                                reads=[b_msk, cx.b_ident_bf], writes=[bqk],
                                tag=f"bad{s_}_{hg}_{kt}_{hl}")
                        sch.step("tensor", lambda e, pqk=pqk, h=h, hl=hl, q0c=q0c,
                                 kt=kt, masked=masked: e.matmul(
                            pqk[:], ktl[kt % 3][(h % 2) * 64:(h % 2) * 64 + 64,
                                                (hl // 2) * 128:(hl // 2) * 128 + 128],
                            qsb[h // 2][(h % 2) * 64:(h % 2) * 64 + 64, q0c:q0c + 512],
                            start=(not masked), stop=True, skip_group_check=True),
                            reads=[b_ktl[kt % 3], b_qsb[h // 2]], writes=[bqk],
                            tag=f"qk{s_}_{hg}_{kt}_{hl}")
                        ptt, bptt = pt_t[hl][kt % 2], b_pt[hl][kt % 2]
                        sch.step("scalar", lambda e, pqk=pqk, ptt=ptt, h=h: e.activation(
                            out=ptt[:], in_=pqk[:], func=AF.Exp, bias=ustars[:, h:h + 1]),
                            reads=[bqk, b_ustars], writes=[bptt],
                            tag=f"ex{s_}_{hg}_{kt}_{hl}")
                        ppv, bpv = pv[hl], b_pv[hl]
                        sch.step("tensor", lambda e, ppv=ppv, hl=hl, kt=kt, ptt=ptt,
                                 Ek=Ek: e.matmul(ppv[:], vtl[kt % 3][:, hl * 65:(hl + 1) * 65],
                                          ptt[:], start=(kt == 0), stop=(kt == Ek - 1),
                                          skip_group_check=True),
                                 reads=[b_vtl[kt % 3], bptt], writes=[bpv],
                                 tag=f"pv{s_}_{hg}_{kt}_{hl}")
                for hl in range(4):
                    h = hg * 4 + hl
                    ppv, bpv = pv[hl], b_pv[hl]
                    sch.step("vector", lambda e, ppv=ppv: e.reciprocal(
                        rsum[0:1, :], ppv[64:65, :]),
                        reads=[bpv], writes=[b_rsum], tag=f"rc{s_}_{h}")
                    sch.step("tensor", lambda e: e.matmul(
                        spare[0:64, :], cx.ones_row[0:1, 0:64],
                        rsum[0:1, :], start=True, stop=True, skip_group_check=True),
                        reads=[b_rsum, cx.b_ones_row], writes=[b_spare], tag=f"bc{s_}_{h}")
                    sch.step("vector", lambda e, ppv=ppv: e.tensor_copy(
                        out=ytmp[:], in_=ppv[0:64, :]),
                        reads=[bpv], writes=[b_ytmp], tag=f"yc{s_}_{h}")
                    hc, hb = h // 2, (h % 2) * 64
                    sch.step("vector", lambda e, hc=hc, hb=hb: e.tensor_tensor(
                        out=yfm[hc][hb:hb + 64, :], in0=ytmp[:], in1=spare[0:64, :],
                        op=ALU.mult),
                        reads=[b_ytmp, b_spare], writes=[b_yfm[hc]], tag=f"yn{s_}_{h}")
            for dm in range(DC):
                j = dm % 2
                pw, bw = qk[j], b_qk[j]
                for hc in range(DC):
                    sch.step("tensor", lambda e, pw=pw, hc=hc, dm=dm: e.matmul(
                        pw[:], wout[:, hc * 1024 + dm * 128:hc * 1024 + (dm + 1) * 128],
                        yfm[hc][:], start=(hc == 0), stop=(hc == DC - 1)),
                        reads=[b_wout, b_yfm[hc]], writes=[bw], tag=f"wo{s_}_{dm}_{hc}")
                sch.step("scalar", lambda e, pw=pw, dm=dm: e.activation(
                    out=ao[dm][:], in_=pw[:], func=AF.Identity),
                    reads=[bw], writes=[b_ao[dm]], tag=f"we{s_}_{dm}")
            for qs in range(4):
                j = qs % 2
                for dm in range(DC):
                    sch.step("tensor", lambda e, dm=dm, qs=qs: e.transpose(
                        ptbf[:, dm * 128:(dm + 1) * 128],
                        ao[dm][:, qs * 128:(qs + 1) * 128], cx.ident_bf[:]),
                        reads=[b_ao[dm], cx.b_ident_bf], writes=[b_ptbf],
                        tag=f"aot{s_}_{qs}_{dm}")
                sch.step("vector", lambda e, j=j: e.tensor_copy(
                    out=aot[j][:], in_=ptbf[:]),
                    reads=[b_ptbf], writes=[b_aot[j]], tag=f"aoe{s_}_{qs}")
                stt, bst = estat[j], b_estat[j]
                sq, bsq = essq[j], b_essq[j]
                mean, var, rstd, nmr = stt[:, 0:1], stt[:, 1:2], stt[:, 2:3], stt[:, 3:4]
                sch.step("scalar", lambda e, j=j, sq=sq: e.activation(
                    out=escr[:], in_=aot[j][:], func=AF.Square, accum_out=sq[:]),
                    reads=[b_aot[j]], writes=[b_escr, bsq], tag=f"esq{s_}_{qs}")
                sch.step("vector", lambda e, j=j, mean=mean: e.tensor_reduce(
                    mean, aot[j][:], axis=AX.X, op=ALU.add),
                    reads=[b_aot[j]], writes=[bst], tag=f"esm{s_}_{qs}")
                sch.step("vector", lambda e, mean=mean: e.tensor_scalar(
                    out=mean, in0=mean, scalar1=1.0 / D, scalar2=None, op0=ALU.mult),
                    reads=[bst], writes=[bst], tag=f"emn{s_}_{qs}")
                sch.step("vector", lambda e, var=var, sq=sq: e.tensor_scalar(
                    out=var, in0=sq[:], scalar1=1.0 / D, scalar2=None, op0=ALU.mult),
                    reads=[bsq, bst], writes=[bst], tag=f"ev1{s_}_{qs}")
                sch.step("vector", lambda e, nmr=nmr, mean=mean: e.tensor_tensor(
                    out=nmr, in0=mean, in1=mean, op=ALU.mult),
                    reads=[bst], writes=[bst], tag=f"em2{s_}_{qs}")
                sch.step("vector", lambda e, var=var, nmr=nmr: e.tensor_tensor(
                    out=var, in0=var, in1=nmr, op=ALU.subtract),
                    reads=[bst], writes=[bst], tag=f"ev2{s_}_{qs}")
                sch.step("vector", lambda e, var=var: e.tensor_scalar(
                    out=var, in0=var, scalar1=EPS, scalar2=None, op0=ALU.add),
                    reads=[bst], writes=[bst], tag=f"ev3{s_}_{qs}")
                sch.step("scalar", lambda e, rstd=rstd, var=var: e.activation(
                    out=rstd, in_=var, func=AF.Ln),
                    reads=[bst], writes=[bst], tag=f"eln{s_}_{qs}")
                sch.step("scalar", lambda e, rstd=rstd: e.activation(
                    out=rstd, in_=rstd, func=AF.Exp, scale=-0.5),
                    reads=[bst], writes=[bst], tag=f"eex{s_}_{qs}")
                sch.step("vector", lambda e, nmr=nmr, mean=mean, rstd=rstd:
                         e.scalar_tensor_tensor(out=nmr, in0=mean, scalar=-1.0,
                                                in1=rstd, op0=ALU.mult, op1=ALU.mult),
                         reads=[bst], writes=[bst], tag=f"enm{s_}_{qs}")
                sch.step("scalar", lambda e, j=j, rstd=rstd, nmr=nmr: e.activation(
                    out=lnres[j][:], in_=aot[j][:], func=AF.Identity,
                    bias=nmr, scale=rstd),
                    reads=[b_aot[j], bst], writes=[b_lnres[j]], tag=f"eap{s_}_{qs}")
                row0 = s_ * 512 + qs * 128
                sch.step("sync", lambda e, j=j, row0=row0: e.dma_start(
                    out=xrow[j][:], in_=x_own_d[row0:row0 + 128, :]),
                    reads=list(XR), writes=[b_xrow[j]], dma=True, tag=f"exl{s_}_{qs}")
                sch.step("vector", lambda e, j=j: e.tensor_tensor(
                    out=xout[j][:], in0=lnres[j][:], in1=xrow[j][:], op=ALU.add),
                    reads=[b_lnres[j], b_xrow[j]], writes=[b_xout[j]], tag=f"eadd{s_}_{qs}")
                sch.step("sync", lambda e, j=j, row0=row0: e.dma_start(
                    out=xo_d[row0:row0 + 128, :], in_=xout[j][:]),
                    reads=[b_xout[j]], writes=list(XW), dma=True, tag=f"exs{s_}_{qs}")
        ph4.close()

        if fx is None:
            sch.emit(block, sems)
    return pr


def stage_a_core_inputs(inputs, core):
    b, p = core // 2, core % 2
    x = np.asarray(inputs["x_full"], np.float32)
    d = {
        "x_own": np.ascontiguousarray(x[b][_own_rows(p)]),
        "x_batch": np.ascontiguousarray(x[b]),
        "WqT": np.ascontiguousarray(inputs["L0_attn_Wq"].reshape(D, H * HD).astype(np.float32)),
        "Wkvd": np.ascontiguousarray(inputs["L0_attn_Wkv_down"].astype(np.float32)),
        "WkuT": np.ascontiguousarray(inputs["L0_attn_Wk_up"].reshape(LD, H * HD).astype(np.float32)),
        "WvuT": np.ascontiguousarray(inputs["L0_attn_Wv_up"].reshape(LD, H * HD).astype(np.float32)),
        "WoutT": np.ascontiguousarray(inputs["L0_attn_Wout"].reshape(H * HD, D).astype(np.float32)),
        "mask": _causal_masks(p, 'l0'),
    }
    d.update(const_inputs())
    return d



class LNKit:

    def __init__(self, pr, st, name, nring=2):
        self.pr, self.name = pr, name
        sch = pr.sch
        self.ssq = [pr.sb(st, f"{name}_q{j}", [128, 1], F32) for j in range(nring)]
        self.bssq = [sch.buf(f"{name}_q{j}") for j in range(nring)]
        self.scr = pr.sb(st, f"{name}_scr", [128, D], F32)
        self.bscr = sch.buf(f"{name}_scr")
        self.stat = [pr.sb(st, f"{name}_st{j}", [128, 4], F32) for j in range(nring)]
        self.bstat = [sch.buf(f"{name}_st{j}") for j in range(nring)]
        self.n = nring

    def emit(self, i, x_tile, bx, out_tile, bout, tagp):
        pr, sch = self.pr, self.pr.sch
        j = i % self.n
        sq, bsq = self.ssq[j], self.bssq[j]
        stt, bst = self.stat[j], self.bstat[j]
        mean, var, rstd, nmr = stt[:, 0:1], stt[:, 1:2], stt[:, 2:3], stt[:, 3:4]
        sch.step("scalar", lambda e: e.activation(
            out=self.scr[:], in_=x_tile[:], func=AF.Square, accum_out=sq[:]),
            reads=[bx], writes=[self.bscr, bsq], tag=f"{tagp}sq")
        sch.step("vector", lambda e: e.tensor_reduce(mean, x_tile[:], axis=AX.X, op=ALU.add),
                 reads=[bx], writes=[bst], tag=f"{tagp}sum")
        sch.step("vector", lambda e: e.tensor_scalar(
            out=mean, in0=mean, scalar1=1.0 / D, scalar2=None, op0=ALU.mult),
            reads=[bst], writes=[bst], tag=f"{tagp}mn")
        sch.step("vector", lambda e: e.tensor_scalar(
            out=var, in0=sq[:], scalar1=1.0 / D, scalar2=None, op0=ALU.mult),
            reads=[bsq, bst], writes=[bst], tag=f"{tagp}v1")
        sch.step("vector", lambda e: e.tensor_tensor(
            out=nmr, in0=mean, in1=mean, op=ALU.mult),
            reads=[bst], writes=[bst], tag=f"{tagp}m2")
        sch.step("vector", lambda e: e.tensor_tensor(
            out=var, in0=var, in1=nmr, op=ALU.subtract),
            reads=[bst], writes=[bst], tag=f"{tagp}v2")
        sch.step("vector", lambda e: e.tensor_scalar(
            out=var, in0=var, scalar1=EPS, scalar2=None, op0=ALU.add),
            reads=[bst], writes=[bst], tag=f"{tagp}v3")
        sch.step("scalar", lambda e: e.activation(out=rstd, in_=var, func=AF.Ln),
                 reads=[bst], writes=[bst], tag=f"{tagp}ln")
        sch.step("scalar", lambda e: e.activation(
            out=rstd, in_=rstd, func=AF.Exp, scale=-0.5),
            reads=[bst], writes=[bst], tag=f"{tagp}ex")
        sch.step("vector", lambda e: e.scalar_tensor_tensor(
            out=nmr, in0=mean, scalar=-1.0, in1=rstd, op0=ALU.mult, op1=ALU.mult),
            reads=[bst], writes=[bst], tag=f"{tagp}nm")
        sch.step("scalar", lambda e: e.activation(
            out=out_tile[:], in_=x_tile[:], func=AF.Identity, bias=nmr, scale=rstd),
            reads=[bx, bst], writes=[bout], tag=f"{tagp}ap")


def emit_absmax_to_scalar(pr, sch, cx, colmax, b_colmax, dst_ap, b_dst, psb, b_psb, tagp):
    sch.step("tensor", lambda e: e.transpose(
        psb[0:1, 0:128], colmax[:], cx.ident_f[:]),
        reads=[b_colmax, cx.b_ident_f], writes=[b_psb], tag=f"{tagp}tr")
    sch.step("vector", lambda e: e.reduce_max(dst_ap, psb[0:1, 0:128], axis=AX.X),
             reads=[b_psb], writes=[b_dst], tag=f"{tagp}rm")


def build_stage_b(upto=99, b3parts='all', fx=None):
    if fx is None:
        pr = Prog()
        st = pr.stack
    else:
        pr = fx["pr"]
        st = fx["st"]
    sch = pr.sch
    nc = pr.nc
    XR = [fx["x_in_buf"]] if fx else []
    XW = [fx["x_out_buf"]] if fx else []

    x_own_d = fx["x_in"] if fx else pr.din("x_own", [TOK, D])
    wg_d = pr.din("w_gate", [FFI, D])
    wu_d = pr.din("w_up", [FFI, D])
    wd_d = pr.din("w_down", [D, FFI])
    xo_d = fx["x_out"] if fx else pr.dout("x_out", [TOK, D])
    ln_dr = pr.dram("ln_dr", [TOK, D])
    up_dr = pr.dram("up_dr", [FFI, TOK], BF16)
    gt_dr = pr.dram("gt_dr", [FFI, TOK], BF16)
    o_dr = pr.dram("o_dr", [FFI, TOK], BF16)
    cc_in = pr.dram("cc_in", [1, 16])
    cc_out = pr.dram("cc_out", [1, 16])
    b_lndr = sch.buf("ln_dr", kind="d"); b_updr = sch.buf("up_dr", kind="d")
    b_gtdr = sch.buf("gt_dr", kind="d"); b_odr = sch.buf("o_dr", kind="d")
    b_ccin = sch.buf("cc_in", kind="d"); b_ccout = sch.buf("cc_out", kind="d")

    IC = FFI // 128
    if fx is None:
        sems = {e: pr.stack.enter_context(nc.semaphore("sem_" + e)) for e in ENGS}
        blk_cm = nc.Block()
    else:
        sems = fx["sems"]
        blk_cm = _UseBlock(fx["block"])

    with blk_cm as block:
        cx = fx["cx"] if fx else emit_consts(pr, st)
        mx = pr.sb(st, "mx", [128, 10], F32)
        b_mx = sch.buf("mx")
        sc_r = pr.sb(st, "sc_r", [1, 16], F32)
        b_scr_ = sch.buf("sc_r")
        scb = pr.sb(st, "scb", [128, 8], F32)
        b_scb = sch.buf("scb")
        ccs = pr.sb(st, "ccs", [1, 16], F32)
        b_ccs = sch.buf("ccs")
        wdT = [pr.sb(st, f"wdT{r}", [128, D], BF16) for r in range(IC)]
        b_wdT = [sch.buf(f"wdT{r}") for r in range(IC)]
        lnk = LNKit(pr, st, "blk")
        sch.step("vector", lambda e: e.memset(mx[:], 0.0), writes=[b_mx], tag="mx0")
        sch.step("vector", lambda e: e.memset(ccs[:], 0.0), writes=[b_ccs], tag="ccs0")
        xb = [pr.sb(st, f"bx{j}", [128, D], F32) for j in range(2)]
        bxb = [sch.buf(f"bx{j}") for j in range(2)]
        lnf = [pr.sb(st, f"blnf{j}", [128, D], F32) for j in range(2)]
        b_lnf = [sch.buf(f"blnf{j}") for j in range(2)]
        xi32 = [pr.sb(st, f"xi{j}", [128, 512], I32) for j in range(2)]
        b_xi32 = [sch.buf(f"xi{j}") for j in range(2)]
        xhat = [pr.sb(st, f"xh{j}", [128, D], BF16) for j in range(2)]
        b_xhat = [sch.buf(f"xh{j}") for j in range(2)]
        xfm = [pr.sb(st, f"xfm{c}", [128, 512], BF16) for c in range(DC)]
        b_xfm = sch.buf("xfm")
        ugst = [pr.sb(st, f"ugst{j}", [128, 512], BF16) for j in range(2)]
        b_ugst = [sch.buf(f"ugst{j}") for j in range(2)]
        tbf = [pr.sb(st, f"tbf{j}", [128, 512], BF16) for j in range(2)]
        b_tbf = [sch.buf(f"tbf{j}") for j in range(2)]
        ti32 = [pr.sb(st, f"ti32{j}", [128, 512], I32) for j in range(2)]
        b_ti32 = [sch.buf(f"ti32{j}") for j in range(2)]
        tf32 = [pr.sb(st, f"tf32{j}", [128, 512], F32) for j in range(2)]
        b_tf32 = [sch.buf(f"tf32{j}") for j in range(2)]

        pst = pr.ps(st, "pst", [128, D], BF16)
        b_pst = sch.buf("pst")
        ps2 = [pr.ps(st, f"ps2_{j}", [128, 512], F32) for j in range(2)]
        b_ps2 = [sch.buf(f"ps2_{j}") for j in range(2)]

        def make_dq(stk, width, pfx, nring=2):
            wnat = [pr.sb(stk, f"{pfx}nat{j}", [128, width], F32) for j in range(nring)]
            wi32 = [pr.sb(stk, f"{pfx}i32{j}", [128, width], I32) for j in range(nring)]
            wbf = [pr.sb(stk, f"{pfx}bf{j}", [128, width], BF16) for j in range(nring)]
            wsc = [pr.sb(stk, f"{pfx}sc{j}", [128, width // 32], F32) for j in range(nring)]
            wri = [pr.sb(stk, f"{pfx}ri{j}", [128, width // 32], F32) for j in range(nring)]
            bufs = {k: [sch.buf(f"{pfx}{k}{j}") for j in range(nring)]
                    for k in ("nat", "i32", "bf", "sc", "ri")}
            def dequant_tile(idx, src_ap, tagp):
                j = idx % nring
                nb = width // 32
                wn, bwn = wnat[j], bufs["nat"][j]
                wi, bwi = wi32[j], bufs["i32"][j]
                wb, bwb = wbf[j], bufs["bf"][j]
                ws, bws = wsc[j], bufs["sc"][j]
                wr, bwr = wri[j], bufs["ri"][j]
                sch.step("sync", lambda e: e.dma_start(out=wn[:], in_=src_ap),
                         writes=[bwn], dma=True, tag=f"{tagp}ld")
                w3 = wn[:].rearrange("p (b k) -> p b k", k=32)
                sch.step("vector", lambda e: e.reduce_max(
                    ws[:], w3, axis=AX.X, apply_absolute_value=True),
                    reads=[bwn], writes=[bws], tag=f"{tagp}am")
                sch.step("vector", lambda e: e.tensor_scalar(
                    out=ws[:], in0=ws[:], scalar1=1.0 / 7.0, scalar2=1e-12,
                    op0=ALU.mult, op1=ALU.max),
                    reads=[bws], writes=[bws], tag=f"{tagp}sc")
                sch.step("vector", lambda e: e.reciprocal(wr[:], ws[:]),
                         reads=[bws], writes=[bwr], tag=f"{tagp}rc")
                rb = wr[:].to_broadcast([128, nb, 32])
                sch.step("vector", lambda e: e.tensor_tensor(
                    out=wi[:].rearrange("p (b k) -> p b k", k=32),
                    in0=w3, in1=rb, op=ALU.mult),
                    reads=[bwn, bwr], writes=[bwi], tag=f"{tagp}dv")
                sb_ = ws[:].to_broadcast([128, nb, 32])
                sch.step("vector", lambda e: e.tensor_tensor(
                    out=wb[:].rearrange("p (b k) -> p b k", k=32),
                    in0=wi[:].rearrange("p (b k) -> p b k", k=32),
                    in1=sb_, op=ALU.mult),
                    reads=[bwi, bws], writes=[bwb], tag=f"{tagp}ml")
                return wb, bwb
            return dequant_tile

        ph0d = ExitStack()
        _B = upto
        dqd = make_dq(ph0d, FFI, "d")
        for r in range(DC):
            wb, bwb = dqd(r, wd_d[r * 128:(r + 1) * 128, :], f"dqd{r}_")
            for c in range(IC):
                sch.step("tensor", lambda e, c=c, wb=wb, bwb=bwb: e.transpose(
                    pst[:, (c % 8) * 128:((c % 8) + 1) * 128],
                    wb[:, c * 128:(c + 1) * 128], cx.ident_bf[:]),
                    reads=[bwb, cx.b_ident_bf], writes=[b_pst], tag=f"tqd{r}_{c}")
                sch.step("vector", lambda e, c=c, r=r: e.tensor_copy(
                    out=wdT[c][:, r * 128:(r + 1) * 128],
                    in_=pst[:, (c % 8) * 128:((c % 8) + 1) * 128]),
                    reads=[b_pst], writes=[b_wdT[c]], tag=f"ted{r}_{c}")
        ph0d.close()

        if _B < 1:
            sch.step("sync", lambda e: e.dma_start(out=xo_d[0:128, :], in_=wdT[0][:]),
                     reads=[b_wdT[0]], dma=True, tag="early_out") if False else None
            sch.step("scalar", lambda e: e.activation(out=lnf[0][:], in_=lnf[0][:], func=AF.Identity) , writes=[b_lnf[0]], tag="noop")
            sch.step("sync", lambda e: e.dma_start(out=xo_d[0:128, :], in_=lnf[0][:]),
                     reads=[b_lnf[0]], dma=True, tag="early_o2")
            sch.emit(block, sems)
            return pr
        ph0 = ExitStack()
        wuT = [pr.sb(ph0, f"wuT{c}", [128, FFI], BF16) for c in range(DC)]
        b_wuT = [sch.buf(f"wuT{c}") for c in range(DC)]
        wgT = [pr.sb(ph0, f"wgT{c}", [128, FFI], BF16) for c in range(DC)]
        b_wgT = [sch.buf(f"wgT{c}") for c in range(DC)]
        dqu = make_dq(ph0, D, "u", nring=1)
        for wsel, (wdr, wT, bwT) in enumerate(
                ((wu_d, wuT, b_wuT), (wg_d, wgT, b_wgT))):
            for r in range(IC):
                wb, bwb = dqu(wsel * IC + r, wdr[r * 128:(r + 1) * 128, :],
                              f"dq{wsel}_{r}_")
                for c in range(DC):
                    sch.step("tensor", lambda e, c=c, wb=wb, bwb=bwb: e.transpose(
                        pst[:, c * 128:(c + 1) * 128], wb[:, c * 128:(c + 1) * 128],
                        cx.ident_bf[:]),
                        reads=[bwb, cx.b_ident_bf], writes=[b_pst], tag=f"tq{wsel}_{r}_{c}")
                for c in range(DC):
                    sch.step("vector", lambda e, c=c, r=r, wT=wT, bwT=bwT: e.tensor_copy(
                        out=wT[c][:, r * 128:(r + 1) * 128],
                        in_=pst[:, c * 128:(c + 1) * 128]),
                        reads=[b_pst], writes=[bwT[c]], tag=f"te{wsel}_{r}_{c}")

        if _B < 2:
            sch.step("sync", lambda e: e.dma_start(out=xo_d[0:128, 0:FFI // 2], in_=wuT[0][:, 0:FFI // 2]),
                     reads=[b_wuT[0]], dma=True, tag="early_o2")
            sch.emit(block, sems)
            ph0.close()
            return pr
        for i in range(16):
            j = i % 2
            sch.step("sync", lambda e, i=i, j=j: e.dma_start(
                out=xb[j][:], in_=x_own_d[i * 128:(i + 1) * 128, :]),
                reads=list(XR), writes=[bxb[j]], dma=True, tag=f"b1ld{i}")
            lnk.emit(i, xb[j], bxb[j], lnf[j], b_lnf[j], f"b1_{i}")
            sch.step("vector", lambda e, j=j: e.reduce_max(
                mx[:, 1:2], lnf[j][:], axis=AX.X, apply_absolute_value=True),
                reads=[b_lnf[j]], writes=[b_mx], tag=f"b1rm{i}")
            sch.step("vector", lambda e: e.tensor_tensor(
                out=mx[:, 0:1], in0=mx[:, 0:1], in1=mx[:, 1:2], op=ALU.max),
                reads=[b_mx], writes=[b_mx], tag=f"b1mx{i}")
            sch.step("sync", lambda e, i=i, j=j: e.dma_start(
                out=ln_dr[i * 128:(i + 1) * 128, :], in_=lnf[j][:]),
                reads=[b_lnf[j]], writes=[b_lndr], dma=True, tag=f"b1sp{i}")

        if _B < 3:
            sch.step("sync", lambda e: e.dma_start(out=xo_d[0:128, :], in_=lnf[1][:]),
                     reads=[b_lnf[1]], dma=True, tag="early_o3")
            sch.emit(block, sems)
            ph0.close()
            return pr
        psb = ps2[0]
        b_psb = b_ps2[0]
        emit_absmax_to_scalar(pr, sch, cx, mx[:, 0:1], b_mx, sc_r[0:1, 0:1], b_scr_,
                              psb, b_psb, "b1s")
        sch.step("vector", lambda e: e.tensor_copy(out=ccs[0:1, 0:1], in_=sc_r[0:1, 0:1]),
                 reads=[b_scr_], writes=[b_ccs], tag="cc1c")
        sch.step("gpsimd", lambda e: e.dma_start(out=cc_in[0:1, :], in_=ccs[0:1, :]),
                 reads=[b_ccs], writes=[b_ccin], dma=True, tag="cc1i")
        sch.step("gpsimd", lambda e: e.collective_compute(
            "AllReduce", ALU.max, replica_groups=[list(range(8))],
            ins=[cc_in.ap().opt()], outs=[cc_out.ap().opt()]),
            reads=[b_ccin], writes=[b_ccout], collective=True, tag="cc1",
            gsem="cc")
        sch.step("gpsimd", lambda e: e.dma_start(out=ccs[0:1, 1:2], in_=cc_out[0:1, 0:1]),
            reads=[b_ccout], writes=[b_ccs], dma=True, tag="cc1o")
        sch.step("vector", lambda e: e.tensor_scalar(
            out=sc_r[0:1, 2:3], in0=ccs[0:1, 1:2], scalar1=1.0 / 32767.0,
            scalar2=1e-12, op0=ALU.mult, op1=ALU.max),
            reads=[b_ccs], writes=[b_scr_], tag="sx")
        sch.step("vector", lambda e: e.reciprocal(sc_r[0:1, 3:4], sc_r[0:1, 2:3]),
                 reads=[b_scr_], writes=[b_scr_], tag="rsx")
        sch.step("tensor", lambda e: e.matmul(
            psb[:, 0:2], cx.ones_row[0:1, :], sc_r[0:1, 2:4],
            start=True, stop=True, skip_group_check=True),
            reads=[b_scr_, cx.b_ones_row], writes=[b_psb], tag="sxb")
        sch.step("vector", lambda e: e.tensor_copy(out=scb[:, 0:2], in_=psb[:, 0:2]),
                 reads=[b_psb], writes=[b_scb], tag="sxe")

        if _B < 4:
            sch.step("sync", lambda e: e.dma_start(out=xo_d[0:128, 0:8], in_=scb[:, 0:8]),
                     reads=[b_scb], dma=True, tag="early_o4")
            sch.emit(block, sems)
            ph0.close()
            return pr
        for tt in range(4):
            for u in range(4):
                i = tt * 4 + u
                j = i % 2
                sch.step("sync", lambda e, i=i, j=j: e.dma_start(
                    out=xb[j][:], in_=ln_dr[i * 128:(i + 1) * 128, :]),
                    reads=[b_lndr], writes=[bxb[j]], dma=True, tag=f"b3ld{i}")
                for hf in range(2):
                    sch.step("vector", lambda e, j=j, hf=hf: e.tensor_scalar(
                        out=xi32[j][:], in0=xb[j][:, hf * 512:(hf + 1) * 512],
                        scalar1=scb[:, 1:2], scalar2=None, op0=ALU.mult),
                        reads=[bxb[j], b_scb], writes=[b_xi32[j]], tag=f"b3q{i}_{hf}")
                    sch.step("vector", lambda e, j=j, hf=hf: e.tensor_scalar(
                        out=xhat[j][:, hf * 512:(hf + 1) * 512], in0=xi32[j][:],
                        scalar1=scb[:, 0:1], scalar2=None, op0=ALU.mult),
                        reads=[b_xi32[j], b_scb], writes=[b_xhat[j]], tag=f"b3s{i}_{hf}")
                for c in range(DC):
                    sch.step("tensor", lambda e, c=c, j=j: e.transpose(
                        pst[:, c * 128:(c + 1) * 128], xhat[j][:, c * 128:(c + 1) * 128],
                        cx.ident_bf[:]),
                        reads=[b_xhat[j], cx.b_ident_bf], writes=[b_pst], tag=f"b3t{i}_{c}")
                for c in range(DC):
                    sch.step("vector", lambda e, c=c, u=u: e.tensor_copy(
                        out=xfm[c][:, u * 128:(u + 1) * 128],
                        in_=pst[:, c * 128:(c + 1) * 128]),
                        reads=[b_pst], writes=[b_xfm], tag=f"b3e{i}_{c}")
            for ic in range(IC if b3parts != 'x' else 0):
                for wsel, (wT, bwT, dr, bdr, mcol) in enumerate(
                        ((wuT, b_wuT, up_dr, b_updr, 2), (wgT, b_wgT, gt_dr, b_gtdr, 4))):
                    j2 = (ic * 2 + wsel) % 2
                    pa, ba = ps2[j2], b_ps2[j2]
                    for c in range(DC):
                        sch.step("tensor", lambda e, c=c, ic=ic, wT=wT, pa=pa: e.matmul(
                            pa[:], wT[c][:, ic * 128:(ic + 1) * 128], xfm[c][:],
                            start=(c == 0), stop=(c == DC - 1), skip_group_check=True),
                            reads=[bwT[c], b_xfm], writes=[ba], tag=f"b3m{tt}_{ic}_{wsel}_{c}")
                    stg, bstg = ugst[wsel], b_ugst[wsel]
                    sch.step("scalar", lambda e, pa=pa, stg=stg: e.activation(
                        out=stg[:], in_=pa[:], func=AF.Identity),
                        reads=[ba], writes=[bstg], tag=f"b3v{tt}_{ic}_{wsel}")
                    if b3parts in ('xmr', 'all'):
                        sch.step("vector", lambda e, stg=stg, mcol=mcol: e.reduce_max(
                            mx[:, mcol + 1:mcol + 2], stg[:], axis=AX.X,
                            apply_absolute_value=True),
                            reads=[bstg], writes=[b_mx], tag=f"b3r{tt}_{ic}_{wsel}")
                        sch.step("vector", lambda e, mcol=mcol: e.tensor_tensor(
                            out=mx[:, mcol:mcol + 1], in0=mx[:, mcol:mcol + 1],
                            in1=mx[:, mcol + 1:mcol + 2], op=ALU.max),
                            reads=[b_mx], writes=[b_mx], tag=f"b3x{tt}_{ic}_{wsel}")
                    if b3parts in ('xms', 'all'):
                        sch.step("sync", lambda e, stg=stg, ic=ic, tt=tt, dr=dr: e.dma_start(
                            out=dr[ic * 128:(ic + 1) * 128, tt * 512:(tt + 1) * 512],
                            in_=stg[:]),
                            reads=[bstg], writes=[bdr], dma=True, tag=f"b3s{tt}_{ic}_{wsel}")
        ph0.close()

        if _B < 5:
            sch.step("sync", lambda e: e.dma_start(out=xo_d[0:128, 0:8], in_=mx[:, 0:8]),
                     reads=[b_mx], dma=True, tag="early_o5")
            sch.emit(block, sems)
            return pr
        emit_absmax_to_scalar(pr, sch, cx, mx[:, 2:3], b_mx, sc_r[0:1, 4:5], b_scr_,
                              psb, b_psb, "b3su")
        emit_absmax_to_scalar(pr, sch, cx, mx[:, 4:5], b_mx, sc_r[0:1, 5:6], b_scr_,
                              psb, b_psb, "b3sg")
        sch.step("vector", lambda e: e.tensor_copy(out=ccs[0:1, 0:2], in_=sc_r[0:1, 4:6]),
                 reads=[b_scr_], writes=[b_ccs], tag="cc2c")
        sch.step("gpsimd", lambda e: e.dma_start(out=cc_in[0:1, :], in_=ccs[0:1, :]),
                 reads=[b_ccs], writes=[b_ccin], dma=True, tag="cc2i")
        sch.step("gpsimd", lambda e: e.collective_compute(
            "AllReduce", ALU.max, replica_groups=[list(range(8))],
            ins=[cc_in.ap().opt()], outs=[cc_out.ap().opt()]),
            reads=[b_ccin], writes=[b_ccout], collective=True, tag="cc2",
            gsem="cc")
        sch.step("gpsimd", lambda e: e.dma_start(out=ccs[0:1, 2:4], in_=cc_out[0:1, 0:2]),
                 reads=[b_ccout], writes=[b_ccs], dma=True, tag="cc2o")
        sch.step("vector", lambda e: e.tensor_scalar(
            out=sc_r[0:1, 6:8], in0=ccs[0:1, 2:4], scalar1=1.0 / 32767.0,
            scalar2=1e-12, op0=ALU.mult, op1=ALU.max),
            reads=[b_ccs], writes=[b_scr_], tag="sug")
        sch.step("vector", lambda e: e.reciprocal(sc_r[0:1, 8:10], sc_r[0:1, 6:8]),
                 reads=[b_scr_], writes=[b_scr_], tag="rsug")
        sch.step("tensor", lambda e: e.matmul(
            psb[:, 0:4], cx.ones_row[0:1, :], sc_r[0:1, 6:10],
            start=True, stop=True, skip_group_check=True),
            reads=[b_scr_, cx.b_ones_row], writes=[b_psb], tag="sugb")
        sch.step("vector", lambda e: e.tensor_copy(out=scb[:, 2:6], in_=psb[:, 0:4]),
                 reads=[b_psb], writes=[b_scb], tag="suge")

        if _B < 6:
            sch.step("sync", lambda e: e.dma_start(out=xo_d[0:128, 0:8], in_=scb[:, 0:8]),
                     reads=[b_scb], dma=True, tag="early_o6")
            sch.emit(block, sems)
            return pr
        ph4 = ExitStack()
        g2sb = [pr.sb(ph4, f"g2_{ic}", [128, TOK], BF16) for ic in range(IC)]
        b_g2sb = [sch.buf(f"g2_{ic}") for ic in range(IC)]
        for ic in range(IC):
            for tt in range(4):
                i = ic * 4 + tt
                j = i % 2
                sch.step("sync", lambda e, ic=ic, tt=tt, j=j: e.dma_start(
                    out=tbf[j][:], in_=gt_dr[ic * 128:(ic + 1) * 128,
                                            tt * 512:(tt + 1) * 512]),
                    reads=[b_gtdr], writes=[b_tbf[j]], dma=True, tag=f"b4ld{i}")
                sch.step("vector", lambda e, j=j: e.tensor_scalar(
                    out=ti32[j][:], in0=tbf[j][:], scalar1=scb[:, 5:6], scalar2=None,
                    op0=ALU.mult),
                    reads=[b_tbf[j], b_scb], writes=[b_ti32[j]], tag=f"b4q{i}")
                sch.step("vector", lambda e, j=j: e.tensor_scalar(
                    out=tf32[0][:], in0=ti32[j][:], scalar1=scb[:, 3:4], scalar2=None,
                    op0=ALU.mult),
                    reads=[b_ti32[j], b_scb], writes=[b_tf32[0]], tag=f"b4g{i}")
                sch.step("scalar", lambda e, j=j: e.activation(
                    out=tf32[1][:], in_=tf32[0][:], func=AF.Sigmoid),
                    reads=[b_tf32[0]], writes=[b_tf32[1]], tag=f"b4sg{i}")
                sch.step("vector", lambda e, j=j: e.tensor_scalar(
                    out=ti32[j][:], in0=tf32[1][:], scalar1=65536.0, scalar2=None,
                    op0=ALU.mult),
                    reads=[b_tf32[1]], writes=[b_ti32[j]], tag=f"b4si{i}")
                sch.step("vector", lambda e, j=j: e.tensor_scalar(
                    out=tf32[1][:], in0=ti32[j][:], scalar1=1.0 / 65536.0,
                    scalar2=65535.0 / 65536.0, op0=ALU.mult, op1=ALU.min),
                    reads=[b_ti32[j]], writes=[b_tf32[1]], tag=f"b4sq{i}")
                sch.step("vector", lambda e, j=j, ic=ic, tt=tt: e.tensor_tensor(
                    out=g2sb[ic][:, tt * 512:(tt + 1) * 512], in0=tf32[0][:],
                    in1=tf32[1][:], op=ALU.mult),
                    reads=[b_tf32[0], b_tf32[1]], writes=[b_g2sb[ic]], tag=f"b4g2{i}")
                sch.step("vector", lambda e, ic=ic, tt=tt: e.reduce_max(
                    mx[:, 7:8], g2sb[ic][:, tt * 512:(tt + 1) * 512], axis=AX.X,
                    apply_absolute_value=True),
                    reads=[b_g2sb[ic]], writes=[b_mx], tag=f"b4rm{i}")
                sch.step("vector", lambda e: e.tensor_tensor(
                    out=mx[:, 6:7], in0=mx[:, 6:7], in1=mx[:, 7:8], op=ALU.max),
                    reads=[b_mx], writes=[b_mx], tag=f"b4mx{i}")

        if _B < 7:
            sch.step("sync", lambda e: e.dma_start(out=xo_d[0:128, 0:8], in_=mx[:, 0:8]),
                     reads=[b_mx], dma=True, tag="early_o7")
            sch.emit(block, sems)
            ph4.close()
            return pr
        emit_absmax_to_scalar(pr, sch, cx, mx[:, 6:7], b_mx, sc_r[0:1, 10:11], b_scr_,
                              psb, b_psb, "b4s")
        sch.step("vector", lambda e: e.tensor_copy(out=ccs[0:1, 0:1], in_=sc_r[0:1, 10:11]),
                 reads=[b_scr_], writes=[b_ccs], tag="cc3c")
        sch.step("gpsimd", lambda e: e.dma_start(out=cc_in[0:1, :], in_=ccs[0:1, :]),
                 reads=[b_ccs], writes=[b_ccin], dma=True, tag="cc3i")
        sch.step("gpsimd", lambda e: e.collective_compute(
            "AllReduce", ALU.max, replica_groups=[list(range(8))],
            ins=[cc_in.ap().opt()], outs=[cc_out.ap().opt()]),
            reads=[b_ccin], writes=[b_ccout], collective=True, tag="cc3",
            gsem="cc")
        sch.step("gpsimd", lambda e: e.dma_start(out=ccs[0:1, 2:3], in_=cc_out[0:1, 0:1]),
                 reads=[b_ccout], writes=[b_ccs], dma=True, tag="cc3o")
        sch.step("vector", lambda e: e.tensor_scalar(
            out=sc_r[0:1, 11:12], in0=ccs[0:1, 2:3], scalar1=1.0 / 32767.0,
            scalar2=1e-12, op0=ALU.mult, op1=ALU.max),
            reads=[b_ccs], writes=[b_scr_], tag="sg2")
        sch.step("vector", lambda e: e.reciprocal(sc_r[0:1, 12:13], sc_r[0:1, 11:12]),
                 reads=[b_scr_], writes=[b_scr_], tag="rsg2")
        sch.step("tensor", lambda e: e.matmul(
            psb[:, 0:2], cx.ones_row[0:1, :], sc_r[0:1, 11:13],
            start=True, stop=True, skip_group_check=True),
            reads=[b_scr_, cx.b_ones_row], writes=[b_psb], tag="sg2b")
        sch.step("vector", lambda e: e.tensor_copy(out=scb[:, 6:8], in_=psb[:, 0:2]),
                 reads=[b_psb], writes=[b_scb], tag="sg2e")

        if _B < 8:
            sch.step("sync", lambda e: e.dma_start(out=xo_d[0:128, 0:8], in_=scb[:, 0:8]),
                     reads=[b_scb], dma=True, tag="early_o8")
            sch.emit(block, sems)
            ph4.close()
            return pr
        for ic in range(IC):
            for tt in range(4):
                i = ic * 4 + tt
                j = i % 2
                sch.step("vector", lambda e, j=j, ic=ic, tt=tt: e.tensor_scalar(
                    out=ti32[j][:], in0=g2sb[ic][:, tt * 512:(tt + 1) * 512],
                    scalar1=scb[:, 7:8], scalar2=None, op0=ALU.mult),
                    reads=[b_g2sb[ic], b_scb], writes=[b_ti32[j]], tag=f"b5q{i}")
                sch.step("vector", lambda e, j=j: e.tensor_scalar(
                    out=tf32[0][:], in0=ti32[j][:], scalar1=scb[:, 6:7], scalar2=None,
                    op0=ALU.mult),
                    reads=[b_ti32[j], b_scb], writes=[b_tf32[0]], tag=f"b5g{i}")
                sch.step("sync", lambda e, ic=ic, tt=tt, j=j: e.dma_start(
                    out=tbf[j][:], in_=up_dr[ic * 128:(ic + 1) * 128,
                                            tt * 512:(tt + 1) * 512]),
                    reads=[b_updr], writes=[b_tbf[j]], dma=True, tag=f"b5ld{i}")
                sch.step("vector", lambda e, j=j: e.tensor_scalar(
                    out=ti32[j][:], in0=tbf[j][:], scalar1=scb[:, 4:5], scalar2=None,
                    op0=ALU.mult),
                    reads=[b_tbf[j], b_scb, b_tf32[0]], writes=[b_ti32[j]], tag=f"b5uq{i}")
                sch.step("vector", lambda e, j=j: e.tensor_scalar(
                    out=tf32[1][:], in0=ti32[j][:], scalar1=scb[:, 2:3], scalar2=None,
                    op0=ALU.mult),
                    reads=[b_ti32[j], b_scb], writes=[b_tf32[1]], tag=f"b5us{i}")
                sch.step("vector", lambda e, j=j: e.tensor_tensor(
                    out=tbf[j][:], in0=tf32[0][:], in1=tf32[1][:], op=ALU.mult),
                    reads=[b_tf32[0], b_tf32[1]], writes=[b_tbf[j]], tag=f"b5o{i}")
                sch.step("vector", lambda e, j=j: e.reduce_max(
                    mx[:, 9:10], tbf[j][:], axis=AX.X, apply_absolute_value=True),
                    reads=[b_tbf[j]], writes=[b_mx], tag=f"b5rm{i}")
                sch.step("vector", lambda e: e.tensor_tensor(
                    out=mx[:, 8:9], in0=mx[:, 8:9], in1=mx[:, 9:10], op=ALU.max),
                    reads=[b_mx], writes=[b_mx], tag=f"b5mx{i}")
                sch.step("sync", lambda e, j=j, ic=ic, tt=tt: e.dma_start(
                    out=o_dr[ic * 128:(ic + 1) * 128, tt * 512:(tt + 1) * 512],
                    in_=tbf[j][:]),
                    reads=[b_tbf[j]], writes=[b_odr], dma=True, tag=f"b5sp{i}")
        ph4.close()

        if _B < 9:
            sch.step("sync", lambda e: e.dma_start(out=xo_d[0:128, 0:8], in_=mx[:, 0:8]),
                     reads=[b_mx], dma=True, tag="early_o9")
            sch.emit(block, sems)
            return pr
        emit_absmax_to_scalar(pr, sch, cx, mx[:, 8:9], b_mx, sc_r[0:1, 13:14], b_scr_,
                              psb, b_psb, "b5s")
        sch.step("vector", lambda e: e.tensor_copy(out=ccs[0:1, 0:1], in_=sc_r[0:1, 13:14]),
                 reads=[b_scr_], writes=[b_ccs], tag="cc4c")
        sch.step("gpsimd", lambda e: e.dma_start(out=cc_in[0:1, :], in_=ccs[0:1, :]),
                 reads=[b_ccs], writes=[b_ccin], dma=True, tag="cc4i")
        sch.step("gpsimd", lambda e: e.collective_compute(
            "AllReduce", ALU.max, replica_groups=[list(range(8))],
            ins=[cc_in.ap().opt()], outs=[cc_out.ap().opt()]),
            reads=[b_ccin], writes=[b_ccout], collective=True, tag="cc4",
            gsem="cc")
        sch.step("gpsimd", lambda e: e.dma_start(out=ccs[0:1, 2:3], in_=cc_out[0:1, 0:1]),
                 reads=[b_ccout], writes=[b_ccs], dma=True, tag="cc4o")
        sch.step("vector", lambda e: e.tensor_scalar(
            out=sc_r[0:1, 14:15], in0=ccs[0:1, 2:3], scalar1=1.0 / 32767.0,
            scalar2=1e-12, op0=ALU.mult, op1=ALU.max),
            reads=[b_ccs], writes=[b_scr_], tag="so")
        sch.step("vector", lambda e: e.reciprocal(sc_r[0:1, 15:16], sc_r[0:1, 14:15]),
                 reads=[b_scr_], writes=[b_scr_], tag="rso")
        sch.step("tensor", lambda e: e.matmul(
            psb[:, 0:2], cx.ones_row[0:1, :], sc_r[0:1, 14:16],
            start=True, stop=True, skip_group_check=True),
            reads=[b_scr_, cx.b_ones_row], writes=[b_psb], tag="sob")
        sch.step("vector", lambda e: e.tensor_copy(out=scb[:, 8:10] if False else scb[:, 6:8],
                                                   in_=psb[:, 0:2]),
                 reads=[b_psb], writes=[b_scb], tag="soe")

        ph6 = ExitStack()
        oq = [pr.sb(ph6, f"oq{ic}", [128, 512], BF16) for ic in range(IC)]
        b_oq = [sch.buf(f"oq{ic}") for ic in range(IC)]
        aod = [pr.sb(ph6, f"aod{c}", [128, 512], BF16) for c in range(DC)]
        b_aod = [sch.buf(f"aod{c}") for c in range(DC)]
        aotb = [pr.sb(ph6, f"aotb{j}", [128, D], BF16) for j in range(2)]
        b_aotb = [sch.buf(f"aotb{j}") for j in range(2)]
        lnob = [pr.sb(ph6, f"lnob{j}", [128, D], F32) for j in range(2)]
        b_lnob = [sch.buf(f"lnob{j}") for j in range(2)]
        xrw = [pr.sb(ph6, f"xrw{j}", [128, D], F32) for j in range(2)]
        b_xrw = [sch.buf(f"xrw{j}") for j in range(2)]
        xou = [pr.sb(ph6, f"xou{j}", [128, D], F32) for j in range(2)]
        b_xou = [sch.buf(f"xou{j}") for j in range(2)]
        for tt in range(4):
            for ic in range(IC):
                i = tt * IC + ic
                j = i % 2
                sch.step("sync", lambda e, ic=ic, tt=tt, j=j: e.dma_start(
                    out=tbf[j][:], in_=o_dr[ic * 128:(ic + 1) * 128,
                                            tt * 512:(tt + 1) * 512]),
                    reads=[b_odr], writes=[b_tbf[j]], dma=True, tag=f"b6ld{i}")
                sch.step("vector", lambda e, j=j: e.tensor_scalar(
                    out=ti32[j][:], in0=tbf[j][:], scalar1=scb[:, 7:8], scalar2=None,
                    op0=ALU.mult),
                    reads=[b_tbf[j], b_scb], writes=[b_ti32[j]], tag=f"b6q{i}")
                sch.step("vector", lambda e, j=j, ic=ic: e.tensor_scalar(
                    out=oq[ic][:], in0=ti32[j][:], scalar1=scb[:, 6:7], scalar2=None,
                    op0=ALU.mult),
                    reads=[b_ti32[j], b_scb], writes=[b_oq[ic]], tag=f"b6s{i}")
            for dm in range(DC):
                j2 = dm % 2
                pa, ba = ps2[j2], b_ps2[j2]
                for ic in range(IC):
                    sch.step("tensor", lambda e, ic=ic, dm=dm, pa=pa: e.matmul(
                        pa[:], wdT[ic][:, dm * 128:(dm + 1) * 128], oq[ic][:],
                        start=(ic == 0), stop=(ic == IC - 1), skip_group_check=True),
                        reads=[b_wdT[ic], b_oq[ic]], writes=[ba], tag=f"b6m{tt}_{dm}_{ic}")
                sch.step("scalar", lambda e, pa=pa, dm=dm: e.activation(
                    out=aod[dm][:], in_=pa[:], func=AF.Identity),
                    reads=[ba], writes=[b_aod[dm]], tag=f"b6e{tt}_{dm}")
            for qs in range(4):
                j = qs % 2
                for dm in range(DC):
                    sch.step("tensor", lambda e, dm=dm, qs=qs: e.transpose(
                        pst[:, dm * 128:(dm + 1) * 128],
                        aod[dm][:, qs * 128:(qs + 1) * 128], cx.ident_bf[:]),
                        reads=[b_aod[dm], cx.b_ident_bf], writes=[b_pst],
                        tag=f"b6t{tt}_{qs}_{dm}")
                sch.step("vector", lambda e, j=j: e.tensor_copy(
                    out=aotb[j][:], in_=pst[:]),
                    reads=[b_pst], writes=[b_aotb[j]], tag=f"b6a{tt}_{qs}")
                lnk.emit(tt * 4 + qs, aotb[j], b_aotb[j], lnob[j], b_lnob[j],
                         f"b6l{tt}_{qs}")
                row0 = tt * 512 + qs * 128
                sch.step("sync", lambda e, j=j, row0=row0: e.dma_start(
                    out=xrw[j][:], in_=x_own_d[row0:row0 + 128, :]),
                    reads=list(XR), writes=[b_xrw[j]], dma=True, tag=f"b6x{tt}_{qs}")
                sch.step("vector", lambda e, j=j: e.tensor_tensor(
                    out=xou[j][:], in0=lnob[j][:], in1=xrw[j][:], op=ALU.add),
                    reads=[b_lnob[j], b_xrw[j]], writes=[b_xou[j]], tag=f"b6ad{tt}_{qs}")
                sch.step("sync", lambda e, j=j, row0=row0: e.dma_start(
                    out=xo_d[row0:row0 + 128, :], in_=xou[j][:]),
                    reads=[b_xou[j]], writes=list(XW), dma=True, tag=f"b6st{tt}_{qs}")
        ph6.close()

        if fx is None:
            sch.emit(block, sems)
    return pr


def stage_b_core_inputs(inputs, core, x_stage):
    b, p = core // 2, core % 2
    d = {
        "x_own": np.ascontiguousarray(x_stage[b][_own_rows(p)]),
        "w_gate": np.ascontiguousarray(inputs["L0_ffn_w_gate"].astype(np.float32)),
        "w_up": np.ascontiguousarray(inputs["L0_ffn_w_up"].astype(np.float32)),
        "w_down": np.ascontiguousarray(inputs["L0_ffn_w_down"].astype(np.float32)),
    }
    d.update(const_inputs())
    return d



def _causal_masks_qk(parity):
    out = np.zeros((NSLOT, SLOT, 2 * SLOT), np.float32)
    for s_ in range(NSLOT):
        q0 = Q0[parity][s_]
        kb = (EXT[s_] - 2) * SLOT
        q = q0 + np.arange(SLOT)[:, None]
        k = kb + np.arange(2 * SLOT)[None, :]
        if q0 == 0:
            m = np.where(k < SLOT, 0.0, NEG)
        else:
            m = np.where(k <= q, 0.0, NEG)
        out[s_] = m
    return np.ascontiguousarray(out.astype(BF))


def _theta_override(parity):
    o = np.zeros((TOK, 1), np.float32)
    if Q0[parity][0] == 0:
        o[:SLOT] = 1.0
    return o


def _rope_tables64(rows):
    inv = 1.0 / (10000.0 ** (np.arange(0, HD, 2, dtype=np.float64) / HD))
    fr = np.outer(np.asarray(rows, np.float64), inv)

    def build(t64):
        t = np.repeat(t64, 2, axis=1)
        hi = t.astype(np.float32)
        lo = (t - hi.astype(np.float64)).astype(np.float32)
        return np.ascontiguousarray(hi), np.ascontiguousarray(lo)
    chi, clo = build(np.cos(fr))
    shi, slo = build(np.sin(fr))
    return chi, clo, shi, slo


def build_stage_c(dbg=False, spill_out=False, fx=None):
    if fx is None:
        pr = Prog()
        st = pr.stack
    else:
        pr = fx["pr"]
        st = fx["st"]
    sch = pr.sch
    nc = pr.nc
    NIT = 26
    XR = [fx["x_in_buf"]] if fx else []
    XW = [fx["x_out_buf"]] if fx else []

    x_own_d = fx["x_in"] if fx else pr.din("x_own", [TOK, D])
    x_bat_d = None if fx else pr.din("x_batch", [S, D])
    wqT_d = pr.din("WqT", [D, D])
    wkT_d = pr.din("WkT", [D, D])
    wvT_d = pr.din("WvT", [D, D])
    woT_d = pr.din("WoT", [D, D])
    iqT_d = pr.din("iqT", [D, IDXH * IDXD])
    ikT_d = pr.din("ikT", [D, 2 * IDXD])
    iwT_d = pr.din("iwT", [D, IDXH])
    mqk_d = pr.din("mask_qk", [NSLOT, SLOT, 2 * SLOT], BF16)
    ovr_d = pr.din("th_ovr", [128, 16])
    rco_d = pr.din("rope_chi_o", [TOK, 64])
    rcl_d = pr.din("rope_clo_o", [TOK, 64])
    rso_d = pr.din("rope_shi_o", [TOK, 64])
    rsl_d = pr.din("rope_slo_o", [TOK, 64])
    rcbh_d = pr.din("rope_chi_b", [S, 64])
    rcbl_d = pr.din("rope_clo_b", [S, 64])
    rsbh_d = pr.din("rope_shi_b", [S, 64])
    rsbl_d = pr.din("rope_slo_b", [S, 64])
    mk = pr.dout if (dbg or spill_out) else pr.dram
    k_sp = mk("k_sp", [32, 8, 128, 128], BF16)
    v_sp = mk("v_sp", [32, 128, 16 * 65], BF16)
    b_ksp = sch.buf("k_sp", kind="d")
    b_vspb = sch.buf("v_sp", kind="d")
    xo_d = fx["x_out"] if fx else pr.dout("x_out", [TOK, D])

    if fx is None:
        sems = {e: pr.stack.enter_context(nc.semaphore("sem_" + e)) for e in ENGS}
        blk_cm = nc.Block()
    else:
        sems = fx["sems"]
        blk_cm = _UseBlock(fx["block"])

    with blk_cm as block:
        cx = fx["cx"] if fx else emit_consts(pr, st)
        lnk = LNKit(pr, st, "clk")

        qsb = [pr.sb(st, f"qsb{c}", [128, TOK], BF16) for c in range(DC)]
        b_qsb = [sch.buf(f"qsb{c}") for c in range(DC)]
        iqf = [pr.sb(st, f"iqf{c}", [128, TOK], BF16) for c in range(2)]
        b_iqf = [sch.buf(f"iqf{c}") for c in range(2)]
        ikf = pr.sb(st, "ikf", [128, S], BF16)
        b_ikf = sch.buf("ikf")
        iwt = pr.sb(st, "iwt", [128, 16 * IDXH], F32)
        b_iwt = sch.buf("iwt")
        wo = pr.sb(st, "wo", [128, DC * 1024], BF16)
        b_wo = sch.buf("wo")
        nrm = pr.sb(st, "nrm", [1, 16], F32)
        b_nrm = sch.buf("nrm")
        knm = pr.sb(st, "knm", [1, 16], F32)
        b_knm = sch.buf("knm")
        qnm = pr.sb(st, "qnm", [1, 16], F32)
        b_qnm = sch.buf("qnm")
        ustars = pr.sb(st, "ustars", [128, 16], F32)
        b_ustars = sch.buf("ustars")

        sch.step("vector", lambda e: e.memset(knm[:], 0.0), writes=[b_knm], tag="knm0")
        sch.step("vector", lambda e: e.memset(qnm[:], 0.0), writes=[b_qnm], tag="qnm0")
        for c in range(DC):
            sch.step("gpsimd", lambda e, c=c: e.dma_start(
                out=wo[:, c * 1024:(c + 1) * 1024], in_=woT_d[c * 128:(c + 1) * 128, :]),
                writes=[b_wo], dma=True, tag=f"wo{c}")
        ovf = pr.sb(st, "ovf", [128, 16], F32)
        b_ovf = sch.buf("ovf")
        sch.step("sync", lambda e: e.dma_start(out=ovf[:], in_=ovr_d[:]),
            writes=[b_ovf], dma=True, tag="ovrld")
        if fx:
            bfl_d = pr.din("bflags", [128, 4])
            bfl = pr.sb(st, "bfl", [128, 4], F32)
            b_bfl = sch.buf("bfl")
            sch.step("sync", lambda e: e.dma_start(out=bfl[:], in_=bfl_d[:]),
                     writes=[b_bfl], dma=True, tag="bflld")
            cand = pr.sb(st, "cand", [128, D], F32)
            b_cand = sch.buf("cand")

        phW = ExitStack()
        wq = pr.sb(phW, "wq", [128, DC * 1024], BF16); b_wq = sch.buf("wq")
        wk = pr.sb(phW, "wk", [128, DC * 1024], BF16); b_wk = sch.buf("wk")
        wv = pr.sb(phW, "wv", [128, DC * 1024], BF16); b_wv = sch.buf("wv")
        wiq = pr.sb(phW, "wiq", [128, DC * 256], BF16); b_wiq = sch.buf("wiq")
        wik = pr.sb(phW, "wik", [128, DC * 128], BF16); b_wik = sch.buf("wik")
        wiw = pr.sb(phW, "wiw", [128, DC * IDXH], BF16); b_wiw = sch.buf("wiw")
        for c in range(DC):
            for wt_, wd_, bw_, w_ in ((wq, wqT_d, b_wq, 1024), (wk, wkT_d, b_wk, 1024),
                                      (wv, wvT_d, b_wv, 1024), (wiq, iqT_d, b_wiq, 256),
                                      (wik, ikT_d, b_wik, 128), (wiw, iwT_d, b_wiw, IDXH)):
                sch.step("gpsimd", lambda e, c=c, wt_=wt_, wd_=wd_, w_=w_: e.dma_start(
                    out=wt_[:, c * w_:(c + 1) * w_], in_=wd_[c * 128:(c + 1) * 128, :]),
                    writes=[bw_], dma=True, tag=f"wld{c}_{w_}")
        rt_o = [pr.sb(phW, f"rto{i}", [128, 16 * 64], F32) for i in range(2)]
        b_rto = [sch.buf(f"rto{i}") for i in range(2)]
        rt_b = [pr.sb(phW, f"rtb{i}", [128, 32 * 64], F32) for i in range(2)]
        b_rtb = [sch.buf(f"rtb{i}") for i in range(2)]
        rtmp = [pr.sb(phW, "rtmp0", [128, 32 * 64], F32)]
        b_rtmp = [sch.buf("rtmp0")]
        for dst, bdst, hi_d, lo_d, ntile in (
                (rt_o[0], b_rto[0], rco_d, rcl_d, 16), (rt_o[1], b_rto[1], rso_d, rsl_d, 16),
                (rt_b[0], b_rtb[0], rcbh_d, rcbl_d, 32), (rt_b[1], b_rtb[1], rsbh_d, rsbl_d, 32)):
            sch.step("sync", lambda e, dst=dst, hi_d=hi_d, ntile=ntile: e.dma_start(
                out=dst[:].rearrange("p (a b) -> p a b", b=64)[:, 0:ntile],
                in_=hi_d[:].rearrange("(a p) b -> p a b", p=128)),
                writes=[bdst], dma=True, tag="rth")
            sch.step("sync", lambda e, lo_d=lo_d, ntile=ntile: e.dma_start(
                out=rtmp[0][:].rearrange("p (a b) -> p a b", b=64)[:, 0:ntile],
                in_=lo_d[:].rearrange("(a p) b -> p a b", p=128)),
                writes=[b_rtmp[0]], dma=True, tag="rtl")
            sch.step("vector", lambda e, dst=dst, ntile=ntile: e.tensor_tensor(
                out=dst[:, 0:ntile * 64], in0=dst[:, 0:ntile * 64],
                in1=rtmp[0][:, 0:ntile * 64], op=ALU.add),
                reads=[bdst, b_rtmp[0]], writes=[bdst], tag="rta")

        phC1 = ExitStack()
        gfm = [[pr.sb(phC1, f"gfm{r}_{c}", [128, 512], BF16) for c in range(DC)]
               for r in range(2)]
        b_gfm = [sch.buf(f"gfm{r}") for r in range(2)]
        xb = [pr.sb(phC1, f"cx{j}", [128, D], F32) for j in range(2)]
        bxb = [sch.buf(f"cx{j}") for j in range(2)]
        lnt = [pr.sb(phC1, f"cl{j}", [128, D], BF16) for j in range(2)]
        blnt = [sch.buf(f"cl{j}") for j in range(2)]
        ktm = [pr.sb(phC1, f"ktm{j}", [128, D], BF16) for j in range(2)]
        b_ktm = [sch.buf(f"ktm{j}") for j in range(2)]
        krot = pr.sb(phC1, "krot", [128, D], BF16)
        b_krot = sch.buf("krot")
        ksq = pr.sb(phC1, "ksq", [128, D], BF16)
        b_ksq = sch.buf("ksq")
        vst = [pr.sb(phC1, f"vst{j}", [128, 16 * 65], BF16) for j in range(2)]
        b_vst = [sch.buf(f"vst{j}") for j in range(2)]
        kst = pr.sb(phC1, "kst", [128, 512], BF16)
        b_kst = sch.buf("kst")
        ptr = [pr.ps(phC1, f"cp{j}", [128, D], BF16) for j in range(2)]
        bptr = [sch.buf(f"cp{j}") for j in range(2)]
        acc = [pr.ps(phC1, f"cacc{j}", [128, 512], F32) for j in range(2)]
        b_acc = [sch.buf(f"cacc{j}") for j in range(2)]
        psn = [pr.ps(phC1, f"cpsn{j}", [16, 512], F32) for j in range(2)]
        b_psn = [sch.buf(f"cpsn{j}") for j in range(2)]

        AG_P = [0, 1, 1, 0, 0, 1, 1, 0]
        AG_SLOT = [0, 0, 1, 1, 2, 2, 3, 3]

        def ln_tile_c(i, src_ap, batch_tile=None, xrd=()):
            j = i % 2
            if fx and batch_tile is not None:
                g, sub = batch_tile // 4, batch_tile % 4
                off = AG_SLOT[g] * 512 + sub * 128
                ag, bag = fx["ag_out"], fx["ag_out_buf"]
                for b4 in range(4):
                    row = (2 * b4 + AG_P[g]) * TOK + off
                    sch.step("sync", lambda e, b4=b4, row=row: e.dma_start(
                        out=cand[:] if b4 else xb[j][:], in_=ag[row:row + 128, :]),
                        reads=[bag], writes=[b_cand if b4 else bxb[j]], dma=True,
                        tag=f"cldc{i}_{b4}")
                    if b4 == 0:
                        sch.step("vector", lambda e: e.tensor_scalar(
                            out=xb[j][:], in0=xb[j][:], scalar1=bfl[:, 0:1],
                            scalar2=None, op0=ALU.mult),
                            reads=[bxb[j], b_bfl], writes=[bxb[j]], tag=f"cldm{i}_0")
                    else:
                        sch.step("vector", lambda e, b4=b4: e.scalar_tensor_tensor(
                            out=xb[j][:], in0=cand[:], scalar=bfl[:, b4:b4 + 1],
                            in1=xb[j][:], op0=ALU.mult, op1=ALU.add),
                            reads=[b_cand, b_bfl, bxb[j]], writes=[bxb[j]],
                            tag=f"cldm{i}_{b4}")
            else:
                sch.step("sync", lambda e: e.dma_start(out=xb[j][:], in_=src_ap),
                         reads=list(xrd), writes=[bxb[j]], dma=True, tag=f"cld{i}")
            lnk.emit(i, xb[j], bxb[j], lnt[j], blnt[j], f"c1_{i}")
            return lnt[j], blnt[j]

        def transpose_to_c(i, ln, bln, dst_list, dst_buf, col0):
            j = i % 2
            pt, bpt = ptr[j], bptr[j]
            for c in range(DC):
                sch.step("tensor", lambda e, c=c: e.transpose(
                    pt[:, c * 128:(c + 1) * 128], ln[:, c * 128:(c + 1) * 128],
                    cx.ident_bf[:]),
                    reads=[bln, cx.b_ident_bf], writes=[bpt], tag=f"ctr{i}_{c}")
            for c in range(DC):
                sch.step("vector", lambda e, c=c: e.tensor_copy(
                    out=dst_list[c][:, col0:col0 + 128],
                    in_=pt[:, c * 128:(c + 1) * 128]),
                    reads=[bpt], writes=[dst_buf], tag=f"cev{i}_{c}")

        def rope_tile(tm, btm, rcos, brcos, rsin, brsin, rc0, dst, bdst, scale_done,
                      tagp):
            v3 = tm[:].rearrange("p (h d) -> p h d", d=64)
            cosb = rcos[:, rc0:rc0 + 64].rearrange("p (a d) -> p a d", a=1).to_broadcast([128, H, 64])
            sinb = rsin[:, rc0:rc0 + 64].rearrange("p (a d) -> p a d", a=1).to_broadcast([128, H, 64])
            sch.step("vector", lambda e: e.tensor_scalar(
                out=krot[:].rearrange("p (h d) -> p h d", d=64)[:, :, 0:64:2],
                in0=v3[:, :, 1:64:2], scalar1=-1.0, scalar2=None, op0=ALU.mult),
                reads=[btm], writes=[b_krot], tag=f"{tagp}r1")
            sch.step("vector", lambda e: e.tensor_copy(
                out=krot[:].rearrange("p (h d) -> p h d", d=64)[:, :, 1:64:2],
                in_=v3[:, :, 0:64:2]),
                reads=[btm], writes=[b_krot], tag=f"{tagp}r2")
            sch.step("vector", lambda e: e.tensor_tensor(
                out=dst[:].rearrange("p (h d) -> p h d", d=64), in0=v3, in1=cosb,
                op=ALU.mult),
                reads=[btm, brcos], writes=[bdst], tag=f"{tagp}r3")
            sch.step("vector", lambda e: e.tensor_tensor(
                out=krot[:].rearrange("p (h d) -> p h d", d=64),
                in0=krot[:].rearrange("p (h d) -> p h d", d=64), in1=sinb,
                op=ALU.mult),
                reads=[b_krot, brsin], writes=[b_krot], tag=f"{tagp}r4")
            sch.step("vector", lambda e: e.tensor_tensor(
                out=dst[:], in0=dst[:], in1=krot[:], op=ALU.add),
                reads=[bdst, b_krot], writes=[bdst], tag=f"{tagp}r5")

        ti = 0
        for g in range(8):
            r = g % 2
            for u in range(4):
                ln, bln = ln_tile_c(
                    ti, None if fx else x_bat_d[(g * 4 + u) * 128:(g * 4 + u + 1) * 128, :],
                    batch_tile=g * 4 + u)
                transpose_to_c(ti, ln, bln, gfm[r], b_gfm[r], u * 128)
                ti += 1
            for u in range(4):
                kt128 = g * 4 + u
                jK = kt128 % 2
                for half in range(2):
                    pa, ba = acc[half], b_acc[half]
                    for c in range(DC):
                        sch.step("tensor", lambda e, c=c, u=u, half=half, r=r, pa=pa:
                                 e.matmul(pa[:], gfm[r][c][:, u * 128:(u + 1) * 128],
                                          wk[:, c * 1024 + half * 512:c * 1024 + (half + 1) * 512],
                                          start=(c == 0), stop=(c == DC - 1),
                                          skip_group_check=True),
                                 reads=[b_gfm[r], b_wk], writes=[ba], tag=f"ckm{kt128}_{half}_{c}")
                    sch.step("scalar", lambda e, half=half, jK=jK, pa=pa: e.activation(
                        out=ktm[jK][:, half * 512:(half + 1) * 512], in_=pa[:],
                        func=AF.Identity),
                        reads=[ba], writes=[b_ktm[jK]], tag=f"cke{kt128}_{half}")
                rope_tile(ktm[jK], b_ktm[jK], rt_b[0], b_rtb[0], rt_b[1], b_rtb[1],
                          kt128 * 64, ktm[jK], b_ktm[jK], False, f"ckr{kt128}_")
                sch.step("scalar", lambda e, jK=jK: e.activation(
                    out=ksq[:], in_=ktm[jK][:], func=AF.Square),
                    reads=[b_ktm[jK]], writes=[b_ksq], tag=f"cksq{kt128}")
                sch.step("vector", lambda e: e.reduce_max(
                    nrm[0:1, 0:1], ksq[0:1, 0:1], axis=AX.X),
                    reads=[b_ksq], writes=[b_nrm], tag=f"cknop{kt128}") if False else None
                sch.step("vector", lambda e, jK=jK: e.tensor_reduce(
                    lnk.scr[:, 0:16], ksq[:].rearrange("p (h d) -> p h d", d=64),
                    axis=AX.X, op=ALU.add),
                    reads=[b_ksq], writes=[lnk.bscr], tag=f"ckn{kt128}")
                pnn, bnn = psn[kt128 % 2], b_psn[kt128 % 2]
                sch.step("tensor", lambda e, pnn=pnn: e.transpose(
                    pnn[0:16, 0:128], krot[:, 0:16], cx.ident_f[:]),
                    reads=[b_krot, cx.b_ident_f], writes=[bnn], tag=f"cknt{kt128}") if False else None
                sch.step("tensor", lambda e, pnn=pnn: e.transpose(
                    pnn[0:16, 0:128], lnk.scr[:, 0:16], cx.ident_f[:]),
                    reads=[lnk.bscr, cx.b_ident_f], writes=[bnn], tag=f"cknt{kt128}")
                sch.step("vector", lambda e, pnn=pnn: e.reduce_max(
                    nrm[0:1, 1:2], pnn[0:16, 0:128].rearrange("a b -> () (a b)") if False
                    else pnn[0:1, 0:128], axis=AX.X),
                    reads=[bnn], writes=[b_nrm], tag=f"cknr2{kt128}") if False else None
                for hh in range(H):
                    sch.step("vector", lambda e, pnn=pnn, hh=hh: e.reduce_max(
                        nrm[0:1, 2:3], pnn[hh:hh + 1, 0:128], axis=AX.X),
                        reads=[bnn], writes=[b_nrm], tag=f"ckr{kt128}_{hh}") if False else None
                sch.step("vector", lambda e, pnn=pnn: e.tensor_copy(
                    out=lnk.scr[0:16, 0:128], in_=pnn[0:16, 0:128]),
                    reads=[bnn], writes=[lnk.bscr], tag=f"ckne{kt128}")
                sch.step("vector", lambda e: e.tensor_reduce(
                    lnk.scr[0:16, 128:129], lnk.scr[0:16, 0:128], axis=AX.X,
                    op=ALU.max),
                    reads=[lnk.bscr], writes=[lnk.bscr], tag=f"cknm{kt128}")
                sch.step("tensor", lambda e, pnn=pnn: e.transpose(
                    pnn[0:1, 0:16], lnk.scr[0:16, 128:129], cx.ident_f[0:16, 0:16]),
                    reads=[lnk.bscr, cx.b_ident_f], writes=[bnn], tag=f"cknt2{kt128}")
                sch.step("vector", lambda e, pnn=pnn: e.tensor_tensor(
                    out=knm[0:1, :], in0=knm[0:1, :], in1=pnn[0:1, 0:16], op=ALU.max),
                    reads=[bnn, b_knm], writes=[b_knm], tag=f"cknx{kt128}")
                jP = kt128 % 2
                pt, bpt = ptr[jP], bptr[jP]
                for c in range(DC):
                    sch.step("tensor", lambda e, c=c, jK=jK, pt=pt: e.transpose(
                        pt[:, c * 128:(c + 1) * 128],
                        ktm[jK][:, c * 128:(c + 1) * 128], cx.ident_bf[:]),
                        reads=[b_ktm[jK], cx.b_ident_bf], writes=[bpt], tag=f"cktr{kt128}_{c}")
                sch.step("vector", lambda e, pt=pt: e.tensor_copy(
                    out=kst[:, 0:512] if False else kst[:], in_=pt[:, 0:512]),
                    reads=[bpt], writes=[b_kst], tag=f"ckev{kt128}") if False else None
                sch.step("vector", lambda e, pt=pt: e.tensor_copy(
                    out=krot[:], in_=pt[:]),
                    reads=[bpt], writes=[b_krot], tag=f"ckev{kt128}")
                sch.step("sync", lambda e, kt128=kt128: e.dma_start(
                    out=k_sp[kt128].rearrange("a p c -> p a c"),
                    in_=krot[:].rearrange("p (a c) -> p a c", c=128)),
                    reads=[b_krot], writes=[b_ksp], dma=True, tag=f"cks{kt128}")
                jV = kt128 % 2
                sch.step("vector", lambda e, jV=jV: e.memset(
                    vst[jV][:].rearrange("p (h c) -> p h c", c=65)[:, :, 64:65], 1.0),
                    writes=[b_vst[jV]], tag=f"cvo{kt128}")
                for half in range(2):
                    pa, ba = acc[half], b_acc[half]
                    for c in range(DC):
                        sch.step("tensor", lambda e, c=c, u=u, half=half, r=r, pa=pa:
                                 e.matmul(pa[:], gfm[r][c][:, u * 128:(u + 1) * 128],
                                          wv[:, c * 1024 + half * 512:c * 1024 + (half + 1) * 512],
                                          start=(c == 0), stop=(c == DC - 1),
                                          skip_group_check=True),
                                 reads=[b_gfm[r], b_wv], writes=[ba], tag=f"cvm{kt128}_{half}_{c}")
                    sch.step("scalar", lambda e, half=half, jV=jV, pa=pa: e.activation(
                        out=vst[jV][:].rearrange("p (h c) -> p h c", c=65)
                        [:, half * 8:(half + 1) * 8, 0:64],
                        in_=pa[:].rearrange("p (h c) -> p h c", c=64),
                        func=AF.Identity),
                        reads=[ba], writes=[b_vst[jV]], tag=f"cve{kt128}_{half}")
                sch.step("sync", lambda e, jV=jV, kt128=kt128: e.dma_start(
                    out=v_sp[kt128], in_=vst[jV][:]),
                    reads=[b_vst[jV]], writes=[b_vspb], dma=True, tag=f"cvs{kt128}")
                pa, ba = acc[0], b_acc[0]
                for c in range(DC):
                    sch.step("tensor", lambda e, c=c, u=u, r=r, pa=pa: e.matmul(
                        pa[:, 0:128], gfm[r][c][:, u * 128:(u + 1) * 128],
                        wik[:, c * 128:(c + 1) * 128],
                        start=(c == 0), stop=(c == DC - 1), skip_group_check=True),
                        reads=[b_gfm[r], b_wik], writes=[ba], tag=f"cikm{kt128}_{c}")
                sch.step("scalar", lambda e, pa=pa: e.activation(
                    out=kst[:, 0:128], in_=pa[:, 0:128], func=AF.Identity),
                    reads=[ba], writes=[b_kst], tag=f"cike{kt128}")
                jP2 = (kt128 + 1) % 2
                pt2, bpt2 = ptr[jP2], bptr[jP2]
                sch.step("tensor", lambda e, pt2=pt2: e.transpose(
                    pt2[:, 0:128], kst[:, 0:128], cx.ident_bf[:]),
                    reads=[b_kst, cx.b_ident_bf], writes=[bpt2], tag=f"cikt{kt128}")
                sch.step("vector", lambda e, pt2=pt2, kt128=kt128: e.tensor_copy(
                    out=ikf[:, kt128 * 128:(kt128 + 1) * 128], in_=pt2[:, 0:128]),
                    reads=[bpt2], writes=[b_ikf], tag=f"cike2{kt128}")

        qtm = [pr.sb(phC1, f"qtm{j}", [128, D], BF16) for j in range(2)]
        b_qtm = [sch.buf(f"qtm{j}") for j in range(2)]
        for g in range(4):
            r = g % 2
            for u in range(4):
                ln, bln = ln_tile_c(ti, x_own_d[(g * 4 + u) * 128:(g * 4 + u + 1) * 128, :],
                                    xrd=XR)
                transpose_to_c(ti, ln, bln, gfm[r], b_gfm[r], u * 128)
                ti += 1
            for u in range(4):
                tt128 = g * 4 + u
                jQ = tt128 % 2
                for half in range(2):
                    pa, ba = acc[half], b_acc[half]
                    for c in range(DC):
                        sch.step("tensor", lambda e, c=c, u=u, half=half, r=r, pa=pa:
                                 e.matmul(pa[:], gfm[r][c][:, u * 128:(u + 1) * 128],
                                          wq[:, c * 1024 + half * 512:c * 1024 + (half + 1) * 512],
                                          start=(c == 0), stop=(c == DC - 1),
                                          skip_group_check=True),
                                 reads=[b_gfm[r], b_wq], writes=[ba], tag=f"cqm{tt128}_{half}_{c}")
                    sch.step("scalar", lambda e, half=half, jQ=jQ, pa=pa: e.activation(
                        out=qtm[jQ][:, half * 512:(half + 1) * 512], in_=pa[:],
                        func=AF.Identity, scale=0.125),
                        reads=[ba], writes=[b_qtm[jQ]], tag=f"cqe{tt128}_{half}")
                rope_tile(qtm[jQ], b_qtm[jQ], rt_o[0], b_rto[0], rt_o[1], b_rto[1],
                          tt128 * 64, qtm[jQ], b_qtm[jQ], True, f"cqr{tt128}_")
                sch.step("scalar", lambda e, jQ=jQ: e.activation(
                    out=ksq[:], in_=qtm[jQ][:], func=AF.Square),
                    reads=[b_qtm[jQ]], writes=[b_ksq], tag=f"cqsq{tt128}")
                sch.step("vector", lambda e: e.tensor_reduce(
                    lnk.scr[:, 0:16], ksq[:].rearrange("p (h d) -> p h d", d=64),
                    axis=AX.X, op=ALU.add),
                    reads=[b_ksq], writes=[lnk.bscr], tag=f"cqn{tt128}")
                pnn, bnn = psn[tt128 % 2], b_psn[tt128 % 2]
                sch.step("tensor", lambda e, pnn=pnn: e.transpose(
                    pnn[0:16, 0:128], lnk.scr[:, 0:16], cx.ident_f[:]),
                    reads=[lnk.bscr, cx.b_ident_f], writes=[bnn], tag=f"cqnt{tt128}")
                sch.step("vector", lambda e, pnn=pnn: e.tensor_copy(
                    out=lnk.scr[0:16, 0:128], in_=pnn[0:16, 0:128]),
                    reads=[bnn], writes=[lnk.bscr], tag=f"cqne{tt128}")
                sch.step("vector", lambda e: e.tensor_reduce(
                    lnk.scr[0:16, 128:129], lnk.scr[0:16, 0:128], axis=AX.X,
                    op=ALU.max),
                    reads=[lnk.bscr], writes=[lnk.bscr], tag=f"cqnm{tt128}")
                sch.step("tensor", lambda e, pnn=pnn: e.transpose(
                    pnn[0:1, 0:16], lnk.scr[0:16, 128:129], cx.ident_f[0:16, 0:16]),
                    reads=[lnk.bscr, cx.b_ident_f], writes=[bnn], tag=f"cqnt2{tt128}")
                sch.step("vector", lambda e, pnn=pnn: e.tensor_tensor(
                    out=qnm[0:1, :], in0=qnm[0:1, :], in1=pnn[0:1, 0:16], op=ALU.max),
                    reads=[bnn, b_qnm], writes=[b_qnm], tag=f"cqnx{tt128}")
                jP = tt128 % 2
                pt, bpt = ptr[jP], bptr[jP]
                for c in range(DC):
                    sch.step("tensor", lambda e, c=c, jQ=jQ, pt=pt: e.transpose(
                        pt[:, c * 128:(c + 1) * 128],
                        qtm[jQ][:, c * 128:(c + 1) * 128], cx.ident_bf[:]),
                        reads=[b_qtm[jQ], cx.b_ident_bf], writes=[bpt], tag=f"cqtr{tt128}_{c}")
                for c in range(DC):
                    sch.step("vector", lambda e, c=c, pt=pt, tt128=tt128: e.tensor_copy(
                        out=qsb[c][:, tt128 * 128:(tt128 + 1) * 128],
                        in_=pt[:, c * 128:(c + 1) * 128]),
                        reads=[bpt], writes=[b_qsb[c]], tag=f"cqev{tt128}_{c}")
                pa, ba = acc[0], b_acc[0]
                for c in range(DC):
                    sch.step("tensor", lambda e, c=c, u=u, r=r, pa=pa: e.matmul(
                        pa[:, 0:256], gfm[r][c][:, u * 128:(u + 1) * 128],
                        wiq[:, c * 256:(c + 1) * 256],
                        start=(c == 0), stop=(c == DC - 1), skip_group_check=True),
                        reads=[b_gfm[r], b_wiq], writes=[ba], tag=f"ciqm{tt128}_{c}")
                sch.step("scalar", lambda e, pa=pa: e.activation(
                    out=kst[:, 0:256], in_=pa[:, 0:256], func=AF.Identity),
                    reads=[ba], writes=[b_kst], tag=f"ciqe{tt128}")
                for c2 in range(2):
                    jP2 = (tt128 + 1) % 2
                    pt2, bpt2 = ptr[jP2], bptr[jP2]
                    sch.step("tensor", lambda e, c2=c2, pt2=pt2: e.transpose(
                        pt2[:, c2 * 128:(c2 + 1) * 128],
                        kst[:, c2 * 128:(c2 + 1) * 128], cx.ident_bf[:]),
                        reads=[b_kst, cx.b_ident_bf], writes=[bpt2], tag=f"ciqt{tt128}_{c2}")
                    sch.step("vector", lambda e, c2=c2, pt2=pt2, tt128=tt128: e.tensor_copy(
                        out=iqf[c2][:, tt128 * 128:(tt128 + 1) * 128],
                        in_=pt2[:, c2 * 128:(c2 + 1) * 128]),
                        reads=[bpt2], writes=[b_iqf[c2]], tag=f"ciqv{tt128}_{c2}")
                pa2, ba2 = acc[1], b_acc[1]
                for c in range(DC):
                    sch.step("tensor", lambda e, c=c, u=u, r=r, pa2=pa2: e.matmul(
                        pa2[:, 0:IDXH], gfm[r][c][:, u * 128:(u + 1) * 128],
                        wiw[:, c * IDXH:(c + 1) * IDXH],
                        start=(c == 0), stop=(c == DC - 1), skip_group_check=True),
                        reads=[b_gfm[r], b_wiw], writes=[ba2], tag=f"ciwm{tt128}_{c}")
                sch.step("vector", lambda e, pa2=pa2, tt128=tt128: e.tensor_copy(
                    out=iwt[:, tt128 * IDXH:(tt128 + 1) * IDXH], in_=pa2[:, 0:IDXH]),
                    reads=[ba2], writes=[b_iwt], tag=f"ciwe{tt128}")

        sch.step("vector", lambda e: e.tensor_tensor(
            out=nrm[0:1, 0:16], in0=qnm[0:1, :], in1=knm[0:1, :], op=ALU.mult),
            reads=[b_qnm, b_knm], writes=[b_nrm], tag="cu2")
        sch.step("scalar", lambda e: e.activation(
            out=nrm[0:1, 0:16], in_=nrm[0:1, 0:16], func=AF.Sqrt),
            reads=[b_nrm], writes=[b_nrm], tag="cusq")
        psb = acc[0]
        b_psb = b_acc[0]
        sch.step("tensor", lambda e: e.matmul(
            psb[:, 0:16], cx.ones_row[0:1, :], nrm[0:1, 0:16],
            start=True, stop=True, skip_group_check=True),
            reads=[b_nrm, cx.b_ones_row], writes=[b_psb], tag="cubc")
        sch.step("vector", lambda e: e.tensor_scalar(
            out=ustars[:], in0=psb[:, 0:16], scalar1=-1.0, scalar2=None, op0=ALU.mult),
            reads=[b_psb], writes=[b_ustars], tag="cuneg")
        phC1.close()
        phW.close()
        sch.barrier()

        phS = ExitStack()
        A_sb = [pr.sb(phS, "A0", [128, 4096], F32)]
        b_A = [sch.buf("A0")]
        btT = pr.sb(phS, "btT", [128, 32 * 512], BF16)
        b_btT = sch.buf("btT")
        msk_scr = pr.sb(phS, "mskscr", [128, 4096], U8)
        b_mskscr = sch.buf("mskscr")
        relu_sb = [pr.sb(phS, f"relu{j}", [128, 512], BF16) for j in range(2)]
        b_relu = [sch.buf(f"relu{j}") for j in range(2)]
        mskt = [pr.sb(phS, f"cmsk{j}", [128, 1024], BF16) for j in range(2)]
        b_mskt = [sch.buf(f"cmsk{j}") for j in range(2)]
        cans = [pr.sb(phS, f"can{j}", [128, 8], F32) for j in range(6)]
        b_cans = [sch.buf(f"can{j}") for j in range(6)]
        ths = []
        b_ths = []
        for j in range(4):
            ths.append(pr.sb(phS, f"ths{j}", [128, 8], F32))
            b_ths.append(sch.buf(f"ths{j}"))
        thu = [pr.sb(phS, f"thu{j}", [128, 2], U8) for j in range(4)]
        b_thu = [sch.buf(f"thu{j}") for j in range(4)]
        for j in range(6):
            sch.step("vector", lambda e, j=j: e.memset(cans[j][:], 1234.5),
                     writes=[b_cans[j]], tag=f"can{j}")
        negmil = pr.sb(phS, "negmil", [128, 1], F32)
        b_negmil = sch.buf("negmil")
        bbf = pr.sb(phS, "bbf", [128, 4096], BF16)
        b_bbf = sch.buf("bbf")
        ktl = [pr.sb(phS, f"cktl{r}", [128, 256], BF16) for r in range(3)]
        b_ktl = [sch.buf(f"cktl{r}") for r in range(3)]
        vtl = [pr.sb(phS, f"cvtl{r}", [128, 4 * 65], BF16) for r in range(3)]
        b_vtl = [sch.buf(f"cvtl{r}") for r in range(3)]
        pt_t = [[pr.sb(phS, f"cpt{hp}_{j}", [128, 1024], BF16) for j in range(2)]
                for hp in range(2)]
        b_pt = [[sch.buf(f"cpt{hp}_{j}") for j in range(2)] for hp in range(2)]
        yfm = [pr.sb(phS, f"cyfm{c}", [128, 512], BF16) for c in range(DC)]
        b_yfm = [sch.buf(f"cyfm{c}") for c in range(DC)]
        rsum = pr.sb(phS, "crsum", [1, 512], F32)
        b_rsum = sch.buf("crsum")
        ytmp = pr.sb(phS, "cytmp", [64, 512], F32)
        b_ytmp = sch.buf("cytmp")
        ao = [pr.sb(phS, f"cao{c}", [128, 512], BF16) for c in range(DC)]
        b_ao = [sch.buf(f"cao{c}") for c in range(DC)]
        aot = [pr.sb(phS, f"caot{j}", [128, D], BF16) for j in range(2)]
        b_aot = [sch.buf(f"caot{j}") for j in range(2)]
        lnres = [pr.sb(phS, f"clnr{j}", [128, D], F32) for j in range(2)]
        b_lnres = [sch.buf(f"clnr{j}") for j in range(2)]
        xrow = [pr.sb(phS, f"cxr{j}", [128, D], F32) for j in range(2)]
        b_xrow = [sch.buf(f"cxr{j}") for j in range(2)]
        xout = [pr.sb(phS, f"cxo{j}", [128, D], F32) for j in range(2)]
        b_xout = [sch.buf(f"cxo{j}") for j in range(2)]
        qkp = pr.ps(phS, "cqkp", [128, 1024], F32)
        qk = [qkp[:, 0:512], qkp[:, 512:1024]]
        b_qk = [sch.buf(f"cqk{j}") for j in range(2)]
        pv = [pr.ps(phS, f"cpv{j}", [65, 512], F32) for j in range(4)]
        b_pv = [sch.buf(f"cpv{j}") for j in range(4)]
        ptbf = pr.ps(phS, "cptbf", [128, D], BF16)
        b_ptbf = sch.buf("cptbf")
        spare = pr.ps(phS, "cspare", [128, 512], F32)
        b_spare = sch.buf("cspare")
        sch.step("vector", lambda e: e.memset(negmil[:], -1.0e6),
                 writes=[b_negmil], tag="negmil")
        upair = pr.sb(phS, "cupair", [128, 8], F32)
        b_upair = sch.buf("cupair")
        sch.step("vector", lambda e: e.tensor_reduce(
            upair[:], ustars[:].rearrange("p (a b) -> p a b", b=2), axis=AX.X,
            op=ALU.min),
            reads=[b_ustars], writes=[b_upair], tag="cupair")
        dbgth = dbg or (fx is not None and fx.get("dbgth"))
        if dbgth:
            dbg_th = pr.dout("dbg_th", [128, 16])
            dbg_ikx = pr.dout("dbg_ikx", [128, S], BF16)
            dbg_iqx = pr.dout("dbg_iqx", [128, TOK], BF16)
            dbg_iwx = pr.dout("dbg_iwx", [128, 16 * IDXH])
            dbg_A1 = pr.dout("dbg_A1", [128, 4096])
            dbg_Aall = pr.dout("dbg_Aall", [16, 128, 4096])
            dbg_ths = pr.dout("dbg_ths", [128, 16 * 8])
            dbg_can = pr.dout("dbg_can", [128, 48])
            thsnap = pr.sb(phS, "thsnap", [128, 16 * 8], F32)
            b_thsnap = sch.buf("thsnap")
            dbg_th_sb = pr.sb(phS, "dbgth", [128, 16], F32)
            b_dbgth = sch.buf("dbgth")
            sch.step("sync", lambda e: e.dma_start(out=dbg_ikx[:], in_=ikf[:]),
                     reads=[b_ikf], dma=True, tag="dbg_ikx")
            sch.step("sync", lambda e: e.dma_start(out=dbg_iqx[:], in_=iqf[0][:]),
                     reads=[b_iqf[0]], dma=True, tag="dbg_iqx")
            sch.step("sync", lambda e: e.dma_start(out=dbg_iwx[:], in_=iwt[:]),
                     reads=[b_iwt], dma=True, tag="dbg_iwx")
        if dbg:
            dbg_us = pr.dout("dbg_us", [128, 16])
            dbg_q = pr.dout("dbg_q", [128, TOK], BF16)
            dbg_ik = pr.dout("dbg_ik", [128, S], BF16)
            dbg_iq = pr.dout("dbg_iq", [128, TOK], BF16)
            dbg_iw = pr.dout("dbg_iw", [128, 16 * IDXH])
            dbg_A = pr.dout("dbg_A", [128, 1024])
            sch.step("sync", lambda e: e.dma_start(out=dbg_us[:], in_=ustars[:]),
                     reads=[b_ustars], dma=True, tag="dbg_us")
            sch.step("sync", lambda e: e.dma_start(out=dbg_q[:], in_=qsb[0][:]),
                     reads=[b_qsb[0]], dma=True, tag="dbg_q")
            sch.step("sync", lambda e: e.dma_start(out=dbg_ik[:], in_=ikf[:]),
                     reads=[b_ikf], dma=True, tag="dbg_ik")
            sch.step("sync", lambda e: e.dma_start(out=dbg_iq[:], in_=iqf[0][:]),
                     reads=[b_iqf[0]], dma=True, tag="dbg_iq")
            sch.step("sync", lambda e: e.dma_start(out=dbg_iw[:], in_=iwt[:]),
                     reads=[b_iwt], dma=True, tag="dbg_iw")
            dbg_y = pr.dout("dbg_y", [128, 512], BF16)
            dbg_pt = pr.dout("dbg_pt", [128, 512], BF16)
            dbg_rs = pr.dout("dbg_rs", [1, 512])
            dbg_ao = pr.dout("dbg_ao", [128, 512], BF16)
            dbg_bt = pr.dout("dbg_bt", [128, 1024], BF16)

        for s_ in range(NSLOT):
            Ek = EXT[s_] * 4
            W = EXT[s_] * 512
            q0c = s_ * 512
            for qs in range(4):
                jA = 0
                At, bAt = A_sb[0], b_A[0]
                trow = s_ * 4 + qs
                jM = qs % 2
                sch.step("sync", lambda e, s_=s_, qs=qs, jM=jM: e.dma_start(
                    out=mskt[jM][:], in_=mqk_d[s_, qs * 128:(qs + 1) * 128, :]),
                    writes=[b_mskt[jM]], dma=True, tag=f"cmld{s_}_{qs}")
                for ih in range(IDXH):
                    for kt5 in range(EXT[s_]):
                        pa, ba = qk[kt5 % 2], b_qk[kt5 % 2]
                        sch.step("tensor", lambda e, ih=ih, kt5=kt5, trow=trow, pa=pa:
                                 e.matmul(pa[:],
                                          iqf[ih // 2][(ih % 2) * 64:(ih % 2) * 64 + 64,
                                                       trow * 128:(trow + 1) * 128],
                                          ikf[(ih % 2) * 64:(ih % 2) * 64 + 64,
                                              kt5 * 512:(kt5 + 1) * 512],
                                          start=True, stop=True, skip_group_check=True),
                                 reads=[b_iqf[ih // 2], b_ikf], writes=[ba],
                                 tag=f"cam{s_}_{qs}_{ih}_{kt5}")
                        jR = kt5 % 2
                        sch.step("scalar", lambda e, pa=pa, jR=jR: e.activation(
                            out=relu_sb[jR][:], in_=pa[:], func=AF.Relu),
                            reads=[ba], writes=[b_relu[jR]], tag=f"car{s_}_{qs}_{ih}_{kt5}")
                        iwcol = iwt[:, trow * IDXH + ih:trow * IDXH + ih + 1]
                        if ih == 0:
                            sch.step("vector", lambda e, jR=jR, kt5=kt5, At=At,
                                     iwcol=iwcol: e.tensor_scalar(
                                out=At[:, kt5 * 512:(kt5 + 1) * 512], in0=relu_sb[jR][:],
                                scalar1=iwcol, scalar2=None, op0=ALU.mult),
                                reads=[b_relu[jR], b_iwt], writes=[bAt],
                                tag=f"caa{s_}_{qs}_{ih}_{kt5}")
                        else:
                            sch.step("vector", lambda e, jR=jR, kt5=kt5, At=At,
                                     iwcol=iwcol: e.scalar_tensor_tensor(
                                out=At[:, kt5 * 512:(kt5 + 1) * 512], in0=relu_sb[jR][:],
                                scalar=iwcol, in1=At[:, kt5 * 512:(kt5 + 1) * 512],
                                op0=ALU.mult, op1=ALU.add),
                                reads=[b_relu[jR], b_iwt, bAt], writes=[bAt],
                                tag=f"caa{s_}_{qs}_{ih}_{kt5}")
                for mh in range(2):
                    sch.step("vector", lambda e, mh=mh, At=At, jM=jM, W=W: e.tensor_tensor(
                        out=At[:, W - 1024 + mh * 512:W - 1024 + (mh + 1) * 512],
                        in0=At[:, W - 1024 + mh * 512:W - 1024 + (mh + 1) * 512],
                        in1=mskt[jM][:, mh * 512:(mh + 1) * 512], op=ALU.add),
                        reads=[bAt, b_mskt[jM]], writes=[bAt], tag=f"cama{s_}_{qs}_{mh}")
                th, bth = ths[qs], b_ths[qs]
                tu, btu = thu[qs], b_thu[qs]
                lo, hi, mid, cnt, th_f = (th[:, 0:1], th[:, 1:2], th[:, 2:3],
                                          th[:, 3:4], th[:, 4:5])
                sch.step("vector", lambda e, At=At, hi=hi, W=W: e.reduce_max(
                    hi, At[:, 0:W], axis=AX.X),
                    reads=[bAt], writes=[bth], tag=f"cbmx{s_}_{qs}")
                jM = qs % 2
                sch.step("vector", lambda e, At=At, jM=jM, W=W: e.scalar_tensor_tensor(
                    out=bbf[:, 0:1024],
                    in0=mskt[jM][:], scalar=-2.0, in1=At[:, W - 1024:W],
                    op0=ALU.mult, op1=ALU.add),
                    reads=[bAt, b_mskt[jM]], writes=[b_bbf], tag=f"cbms{s_}_{qs}")
                sch.step("vector", lambda e, lo=lo: e.tensor_reduce(
                    lo, bbf[:, 0:1024], axis=AX.X, op=ALU.min),
                    reads=[b_bbf], writes=[bth], tag=f"cbmn{s_}_{qs}")
                if W > 1024:
                    sch.step("vector", lambda e, At=At, mid=mid, W=W: e.tensor_reduce(
                        mid, At[:, 0:W - 1024], axis=AX.X, op=ALU.min),
                        reads=[bAt], writes=[bth], tag=f"cbmn2{s_}_{qs}")
                    sch.step("vector", lambda e, lo=lo, mid=mid: e.tensor_tensor(
                        out=lo, in0=lo, in1=mid, op=ALU.min),
                        reads=[bth], writes=[bth], tag=f"cbmn3{s_}_{qs}")
                sch.step("scalar", lambda e, lo=lo, mid=mid: e.activation(
                    out=mid, in_=lo, func=AF.Abs),
                    reads=[bth], writes=[bth], tag=f"cbw{s_}_{qs}a")
                sch.step("vector", lambda e, lo=lo, mid=mid: e.scalar_tensor_tensor(
                    out=lo, in0=mid, scalar=-0.01, in1=lo, op0=ALU.mult, op1=ALU.add),
                    reads=[bth], writes=[bth], tag=f"cbw{s_}_{qs}b")
                sch.step("vector", lambda e, lo=lo: e.tensor_scalar(
                    out=lo, in0=lo, scalar1=-1e-3, scalar2=None, op0=ALU.add),
                    reads=[bth], writes=[bth], tag=f"cbw{s_}_{qs}c")
                for it in range(NIT):
                    sch.step("vector", lambda e, lo=lo, hi=hi, mid=mid: e.tensor_tensor(
                        out=mid, in0=lo, in1=hi, op=ALU.add),
                        reads=[bth], writes=[bth], tag=f"cbi{s_}_{qs}_{it}a")
                    sch.step("vector", lambda e, mid=mid: e.tensor_scalar(
                        out=mid, in0=mid, scalar1=0.5, scalar2=None, op0=ALU.mult),
                        reads=[bth], writes=[bth], tag=f"cbi{s_}_{qs}_{it}b")
                    sch.step("vector", lambda e, At=At, mid=mid, cnt=cnt, W=W:
                             e.tensor_scalar(
                        out=msk_scr[:, 0:W], in0=At[:, 0:W], scalar1=mid,
                        scalar2=0.0, op0=ALU.is_ge, op1=ALU.add, accum_out=cnt),
                        reads=[bAt, bth], writes=[b_mskscr, bth],
                        tag=f"cbi{s_}_{qs}_{it}c")
                    sch.step("vector", lambda e, cnt=cnt, tu=tu: e.tensor_scalar(
                        out=tu[:, 0:1], in0=cnt, scalar1=float(TOPK), scalar2=None,
                        op0=ALU.is_ge),
                        reads=[bth], writes=[btu], tag=f"cbi{s_}_{qs}_{it}d")
                    sch.step("vector", lambda e, lo=lo, mid=mid, tu=tu: e.select(
                        lo, tu[:, 0:1], mid, lo),
                        reads=[bth, btu], writes=[bth], tag=f"cbi{s_}_{qs}_{it}e")
                    sch.step("vector", lambda e, hi=hi, mid=mid, tu=tu: e.select(
                        hi, tu[:, 0:1], hi, mid),
                        reads=[bth, btu], writes=[bth], tag=f"cbi{s_}_{qs}_{it}f")
                sch.step("vector", lambda e, tu=tu, trow=trow: e.tensor_scalar(
                    out=tu[:, 1:2], in0=ovf[:, trow:trow + 1], scalar1=0.5,
                    scalar2=None, op0=ALU.is_gt),
                    reads=[b_ovf], writes=[btu], tag=f"cbo{s_}_{qs}")
                sch.step("vector", lambda e, th_f=th_f, lo=lo, tu=tu: e.select(
                    th_f, tu[:, 1:2], negmil[:], lo),
                    reads=[bth, btu, b_negmil], writes=[bth], tag=f"cbth{s_}_{qs}")
                if dbgth:
                    sch.step("vector", lambda e, th_f=th_f, trow=trow: e.tensor_copy(
                        out=dbg_th_sb[:, trow:trow + 1], in_=th_f),
                        reads=[bth], writes=[b_dbgth], tag=f"dbgth{s_}_{qs}")
                    sch.step("vector", lambda e, th=th, trow=trow: e.tensor_copy(
                        out=thsnap[:, trow * 8:(trow + 1) * 8], in_=th[:, 0:8]),
                        reads=[bth], writes=[b_thsnap], tag=f"dbgts{s_}_{qs}")
                    if s_ == 1 and qs == 0:
                        sch.step("sync", lambda e, At=At: e.dma_start(
                            out=dbg_A1[:], in_=At[:, :]),
                            reads=[bAt], dma=True, tag="dbg_A1")
                    sch.step("sync", lambda e, At=At, trow=trow: e.dma_start(
                        out=dbg_Aall[trow], in_=At[:, :]),
                        reads=[bAt], dma=True, tag=f"dbg_Aall{trow}")
                if dbg:
                    if s_ == 0 and qs == 0:
                        sch.step("sync", lambda e, At=At: e.dma_start(
                            out=dbg_A[:], in_=At[:, 0:1024]),
                            reads=[bAt], dma=True, tag="dbg_A")
                sch.step("vector", lambda e, At=At, th_f=th_f, W=W: e.tensor_scalar(
                    out=msk_scr[:, 0:W], in0=At[:, 0:W], scalar1=th_f, scalar2=0.0,
                    op0=ALU.is_ge, op1=ALU.add),
                    reads=[bAt, bth], writes=[b_mskscr], tag=f"cbm{s_}_{qs}")
                sch.step("vector", lambda e, W=W: e.tensor_scalar(
                    out=bbf[:, 0:W], in0=msk_scr[:, 0:W], scalar1=1.0e30,
                    scalar2=-1.0e30, op0=ALU.mult, op1=ALU.add),
                    reads=[b_mskscr], writes=[b_bbf], tag=f"cbb{s_}_{qs}")
                for kt in range(Ek):
                    c0 = (kt % 4) * 128
                    sch.step("tensor", lambda e, kt=kt, c0=c0: e.transpose(
                        ptbf[:, c0:c0 + 128], bbf[:, kt * 128:(kt + 1) * 128],
                        cx.ident_bf[:]),
                        reads=[b_bbf, cx.b_ident_bf], writes=[b_ptbf], tag=f"cbt{s_}_{qs}_{kt}")
                    sch.step("vector", lambda e, kt=kt, qs=qs, c0=c0: e.tensor_copy(
                        out=btT[:, kt * 512 + qs * 128:kt * 512 + (qs + 1) * 128],
                        in_=ptbf[:, c0:c0 + 128]),
                        reads=[b_ptbf], writes=[b_btT], tag=f"cbe{s_}_{qs}_{kt}")

            for hg in range(4):
                for kt in range(Ek):
                    r = kt % 3
                    sch.step("sync", lambda e, r=r, kt=kt, hg=hg: e.dma_start(
                        out=ktl[r][:].rearrange("p (a c) -> p a c", c=128),
                        in_=k_sp[kt, hg * 2:(hg + 1) * 2].rearrange("a p c -> p a c")),
                        reads=[b_ksp], writes=[b_ktl[r]], dma=True,
                        tag=f"cktl{s_}_{hg}_{kt}")
                    sch.step("sync", lambda e, r=r, kt=kt, hg=hg: e.dma_start(
                        out=vtl[r][:], in_=v_sp[kt, :, hg * 4 * 65:(hg + 1) * 4 * 65]),
                        reads=[b_vspb], writes=[b_vtl[r]], dma=True,
                        tag=f"cvtl{s_}_{hg}_{kt}")
                    for hl in range(4):
                        h = hg * 4 + hl
                        jq = hl % 2
                        pqk, bqk = qk[jq], b_qk[jq]
                        sch.step("tensor", lambda e, pqk=pqk, kt=kt: e.matmul(
                            pqk[:], cx.ident_bf[:], btT[:, kt * 512:(kt + 1) * 512],
                            start=True, stop=False, skip_group_check=True),
                            reads=[b_btT, cx.b_ident_bf], writes=[bqk],
                            tag=f"cbad{s_}_{hg}_{kt}_{hl}")
                        sch.step("tensor", lambda e, pqk=pqk, h=h, hl=hl, q0c=q0c,
                                 kt=kt: e.matmul(
                            pqk[:], ktl[kt % 3][(h % 2) * 64:(h % 2) * 64 + 64,
                                                (hl // 2) * 128:(hl // 2) * 128 + 128],
                            qsb[h // 2][(h % 2) * 64:(h % 2) * 64 + 64, q0c:q0c + 512],
                            start=False, stop=True, skip_group_check=True),
                            reads=[b_ktl[kt % 3], b_qsb[h // 2]], writes=[bqk],
                            tag=f"cqk{s_}_{hg}_{kt}_{hl}")
                        if hl % 2 == 1:
                            hp = hl // 2
                            ptt, bptt = pt_t[hp][kt % 2], b_pt[hp][kt % 2]
                            pairidx = hg * 2 + hp
                            sch.step("scalar", lambda e, ptt=ptt, pairidx=pairidx:
                                     e.activation(
                                out=ptt[:], in_=qkp[:], func=AF.Exp,
                                bias=upair[:, pairidx:pairidx + 1]),
                                reads=[b_qk[0], b_qk[1], b_upair], writes=[bptt],
                                tag=f"cex{s_}_{hg}_{kt}_{hl}")
                            for hh in (hl - 1, hl):
                                ppv, bpv = pv[hh], b_pv[hh]
                                sch.step("tensor", lambda e, ppv=ppv, hh=hh, kt=kt,
                                         ptt=ptt, Ek=Ek: e.matmul(
                                    ppv[:], vtl[kt % 3][:, hh * 65:(hh + 1) * 65],
                                    ptt[:, (hh % 2) * 512:(hh % 2 + 1) * 512],
                                    start=(kt == 0), stop=(kt == Ek - 1),
                                    skip_group_check=True),
                                    reads=[b_vtl[kt % 3], bptt], writes=[bpv],
                                    tag=f"cpv{s_}_{hg}_{kt}_{hh}")
                if dbg and s_ == 0 and hg == 0:
                    sch.step("sync", lambda e: e.dma_start(
                        out=dbg_pt[:], in_=pt_t[0][(EXT[0] * 4 - 1) % 2][:, 0:512]),
                        reads=[b_pt[0][(EXT[0] * 4 - 1) % 2]], dma=True, tag="dbg_pt")
                for hl in range(4):
                    h = hg * 4 + hl
                    ppv, bpv = pv[hl], b_pv[hl]
                    sch.step("vector", lambda e, ppv=ppv: e.reciprocal(
                        rsum[0:1, :], ppv[64:65, :]),
                        reads=[bpv], writes=[b_rsum], tag=f"crc{s_}_{h}")
                    if dbg and s_ == 0 and h == 0:
                        sch.step("sync", lambda e: e.dma_start(
                            out=dbg_rs[:], in_=rsum[0:1, :]),
                            reads=[b_rsum], dma=True, tag="dbg_rs")
                    sch.step("tensor", lambda e: e.matmul(
                        spare[0:64, :], cx.ones_row[0:1, 0:64], rsum[0:1, :],
                        start=True, stop=True, skip_group_check=True),
                        reads=[b_rsum, cx.b_ones_row], writes=[b_spare], tag=f"cbc{s_}_{h}")
                    sch.step("vector", lambda e, ppv=ppv: e.tensor_copy(
                        out=ytmp[:], in_=ppv[0:64, :]),
                        reads=[bpv], writes=[b_ytmp], tag=f"cyc{s_}_{h}")
                    hc, hb = h // 2, (h % 2) * 64
                    sch.step("vector", lambda e, hc=hc, hb=hb: e.tensor_tensor(
                        out=yfm[hc][hb:hb + 64, :], in0=ytmp[:], in1=spare[0:64, :],
                        op=ALU.mult),
                        reads=[b_ytmp, b_spare], writes=[b_yfm[hc]], tag=f"cyn{s_}_{h}")
            if dbg and s_ == 0:
                sch.step("sync", lambda e: e.dma_start(out=dbg_y[:], in_=yfm[0][:]),
                         reads=[b_yfm[0]], dma=True, tag="dbg_y")
                sch.step("sync", lambda e: e.dma_start(out=dbg_bt[:], in_=btT[:, 0:1024]),
                         reads=[b_btT], dma=True, tag="dbg_bt")
            for dm in range(DC):
                j = dm % 2
                pw, bw = qk[j], b_qk[j]
                for hc in range(DC):
                    sch.step("tensor", lambda e, pw=pw, hc=hc, dm=dm: e.matmul(
                        pw[:], wo[:, hc * 1024 + dm * 128:hc * 1024 + (dm + 1) * 128],
                        yfm[hc][:], start=(hc == 0), stop=(hc == DC - 1),
                        skip_group_check=True),
                        reads=[b_wo, b_yfm[hc]], writes=[bw], tag=f"cwo{s_}_{dm}_{hc}")
                sch.step("scalar", lambda e, pw=pw, dm=dm: e.activation(
                    out=ao[dm][:], in_=pw[:], func=AF.Identity),
                    reads=[bw], writes=[b_ao[dm]], tag=f"cwe{s_}_{dm}")
                if dbg and s_ == 0 and dm == 0:
                    sch.step("sync", lambda e: e.dma_start(out=dbg_ao[:], in_=ao[0][:]),
                             reads=[b_ao[0]], dma=True, tag="dbg_ao")
            for qs in range(4):
                j = qs % 2
                for dm in range(DC):
                    sch.step("tensor", lambda e, dm=dm, qs=qs: e.transpose(
                        ptbf[:, dm * 128:(dm + 1) * 128],
                        ao[dm][:, qs * 128:(qs + 1) * 128], cx.ident_bf[:]),
                        reads=[b_ao[dm], cx.b_ident_bf], writes=[b_ptbf],
                        tag=f"caot{s_}_{qs}_{dm}")
                sch.step("vector", lambda e, j=j: e.tensor_copy(
                    out=aot[j][:], in_=ptbf[:]),
                    reads=[b_ptbf], writes=[b_aot[j]], tag=f"caoe{s_}_{qs}")
                lnk.emit(s_ * 4 + qs, aot[j], b_aot[j], lnres[j], b_lnres[j],
                         f"cel{s_}_{qs}")
                row0 = s_ * 512 + qs * 128
                sch.step("sync", lambda e, j=j, row0=row0: e.dma_start(
                    out=xrow[j][:], in_=x_own_d[row0:row0 + 128, :]),
                    reads=list(XR), writes=[b_xrow[j]], dma=True, tag=f"cxl{s_}_{qs}")
                sch.step("vector", lambda e, j=j: e.tensor_tensor(
                    out=xout[j][:], in0=lnres[j][:], in1=xrow[j][:], op=ALU.add),
                    reads=[b_lnres[j], b_xrow[j]], writes=[b_xout[j]], tag=f"cad{s_}_{qs}")
                sch.step("sync", lambda e, j=j, row0=row0: e.dma_start(
                    out=xo_d[row0:row0 + 128, :], in_=xout[j][:]),
                    reads=[b_xout[j]], writes=list(XW), dma=True, tag=f"cst{s_}_{qs}")
        if dbgth:
            sch.step("sync", lambda e: e.dma_start(out=dbg_th[:], in_=dbg_th_sb[:]),
                     reads=[b_dbgth], dma=True, tag="dbg_th")
            sch.step("sync", lambda e: e.dma_start(out=dbg_ths[:], in_=thsnap[:]),
                     reads=[b_thsnap], dma=True, tag="dbg_ths")
            for j in range(6):
                sch.step("sync", lambda e, j=j: e.dma_start(
                    out=dbg_can[:, j * 8:(j + 1) * 8], in_=cans[j][:]),
                    reads=[b_cans[j]], dma=True, tag=f"dbg_can{j}")
        phS.close()

        if fx is None:
            sch.emit(block, sems)
    return pr


def stage_c_core_inputs(inputs, core, x_stage):
    b, p = core // 2, core % 2
    qkv = inputs["L1_attn_qkv_w"].astype(np.float32)
    rows = _own_rows(p)
    d = {
        "x_own": np.ascontiguousarray(x_stage[b][rows]),
        "x_batch": np.ascontiguousarray(x_stage[b]),
        "WqT": np.ascontiguousarray(qkv[0:D].T),
        "WkT": np.ascontiguousarray(qkv[D:2 * D].T),
        "WvT": np.ascontiguousarray(qkv[2 * D:3 * D].T),
        "WoT": np.ascontiguousarray(inputs["L1_attn_wo_w"].astype(np.float32).T),
        "iqT": np.ascontiguousarray(inputs["L1_attn_iq_w"].astype(np.float32).T),
        "ikT": np.ascontiguousarray(np.concatenate([inputs["L1_attn_ik_w"].astype(np.float32).T] * 2, axis=1)),
        "iwT": np.ascontiguousarray(inputs["L1_attn_iw_w"].astype(np.float32).T),
        "mask_qk": _causal_masks_qk(p),
        "th_ovr": np.ascontiguousarray(_theta_override(p).reshape(16, 128).T),
    }
    chi, clo, shi, slo = _rope_tables64(rows)
    d.update(rope_chi_o=chi, rope_clo_o=clo, rope_shi_o=shi, rope_slo_o=slo)
    chi, clo, shi, slo = _rope_tables64(np.arange(S))
    d.update(rope_chi_b=chi, rope_clo_b=clo, rope_shi_b=shi, rope_slo_b=slo)
    d.update(const_inputs())
    return d



def build_stage_d(fx=None):
    if fx is None:
        pr = Prog()
        st = pr.stack
    else:
        pr = fx["pr"]
        st = fx["st"]
    sch = pr.sch
    nc = pr.nc
    XR = [fx["x_in_buf"]] if fx else []
    XW = [fx["x_out_buf"]] if fx else []

    x_own_d = fx["x_in"] if fx else pr.din("x_own", [TOK, D])
    gwT_d = pr.din("gwT", [D, 8])
    ewT_d = pr.din("ewT", [8, D, D])
    xo_d = fx["x_out"] if fx else pr.dout("x_out", [TOK, D])
    if fx is None:
        sems = {e: pr.stack.enter_context(nc.semaphore("sem_" + e)) for e in ENGS}
        blk_cm = nc.Block()
    else:
        sems = fx["sems"]
        blk_cm = _UseBlock(fx["block"])

    with blk_cm as block:
        cx = fx["cx"] if fx else emit_consts(pr, st)
        lnk = LNKit(pr, st, "dlk")
        gw = pr.sb(st, "gw", [128, DC * 8], BF16)
        b_gw = sch.buf("gw")
        ew = [pr.sb(st, f"ew{e}", [128, DC * 1024], BF16) for e in range(8)]
        b_ew = [sch.buf(f"ew{e}") for e in range(8)]
        for c in range(DC):
            sch.step("gpsimd", lambda e, c=c: e.dma_start(
                out=gw[:, c * 8:(c + 1) * 8], in_=gwT_d[c * 128:(c + 1) * 128, :]),
                writes=[b_gw], dma=True, tag=f"dgw{c}")
            for ex in range(8):
                sch.step("gpsimd", lambda e, c=c, ex=ex: e.dma_start(
                    out=ew[ex][:, c * 1024:(c + 1) * 1024],
                    in_=ewT_d[ex, c * 128:(c + 1) * 128, :]),
                    writes=[b_ew[ex]], dma=True, tag=f"dew{ex}_{c}")

        xb = [pr.sb(st, f"dx{j}", [128, D], F32) for j in range(2)]
        bxb = [sch.buf(f"dx{j}") for j in range(2)]
        lnf = [pr.sb(st, f"dln{j}", [128, D], F32) for j in range(2)]
        b_lnf = [sch.buf(f"dln{j}") for j in range(2)]
        lnb = [pr.sb(st, f"dlb{j}", [128, D], BF16) for j in range(2)]
        b_lnb = [sch.buf(f"dlb{j}") for j in range(2)]
        lfm = [pr.sb(st, f"dfm{j}", [128, D], BF16) for j in range(2)]
        b_lfm = [sch.buf(f"dfm{j}") for j in range(2)]
        sco = pr.sb(st, "dsco", [128, 12], F32)
        b_sco = sch.buf("dsco")
        ohu = pr.sb(st, "dohu", [128, 24], U8)
        b_ohu = sch.buf("dohu")
        oh = pr.sb(st, "doh", [128, 12], F32)
        b_oh = sch.buf("doh")
        acc = [pr.sb(st, f"dacc{j}", [128, D], F32) for j in range(2)]
        b_acc = [sch.buf(f"dacc{j}") for j in range(2)]
        ffo = [pr.sb(st, f"dffo{j}", [128, D], F32) for j in range(2)]
        b_ffo = [sch.buf(f"dffo{j}") for j in range(2)]
        lnr = [pr.sb(st, f"dlr{j}", [128, D], F32) for j in range(2)]
        b_lnr = [sch.buf(f"dlr{j}") for j in range(2)]
        xou = [pr.sb(st, f"dxo{j}", [128, D], F32) for j in range(2)]
        b_xou = [sch.buf(f"dxo{j}") for j in range(2)]
        ptbf = pr.ps(st, "dptbf", [128, D], BF16)
        b_ptbf = sch.buf("dptbf")
        psc = pr.ps(st, "dpsc", [128, 8], F32)
        b_psc = sch.buf("dpsc")
        py = [pr.ps(st, f"dpy{j}", [128, 1024], F32) for j in range(2)]
        b_py = [sch.buf(f"dpy{j}") for j in range(2)]

        for i in range(16):
            j = i % 2
            sch.step("sync", lambda e, i=i, j=j: e.dma_start(
                out=xb[j][:], in_=x_own_d[i * 128:(i + 1) * 128, :]),
                reads=list(XR), writes=[bxb[j]], dma=True, tag=f"dld{i}")
            lnk.emit(i, xb[j], bxb[j], lnf[j], b_lnf[j], f"d1_{i}")
            sch.step("vector", lambda e, j=j: e.tensor_copy(out=lnb[j][:], in_=lnf[j][:]),
                     reads=[b_lnf[j]], writes=[b_lnb[j]], tag=f"dbf{i}")
            for c in range(DC):
                sch.step("tensor", lambda e, c=c, j=j: e.transpose(
                    ptbf[:, c * 128:(c + 1) * 128], lnb[j][:, c * 128:(c + 1) * 128],
                    cx.ident_bf[:]),
                    reads=[b_lnb[j], cx.b_ident_bf], writes=[b_ptbf], tag=f"dtr{i}_{c}")
            sch.step("vector", lambda e, j=j: e.tensor_copy(out=lfm[j][:], in_=ptbf[:]),
                     reads=[b_ptbf], writes=[b_lfm[j]], tag=f"dte{i}")
            for c in range(DC):
                sch.step("tensor", lambda e, c=c, j=j: e.matmul(
                    psc[:], lfm[j][:, c * 128:(c + 1) * 128], gw[:, c * 8:(c + 1) * 8],
                    start=(c == 0), stop=(c == DC - 1), skip_group_check=True),
                    reads=[b_lfm[j], b_gw], writes=[b_psc], tag=f"dsm{i}_{c}")
            sch.step("vector", lambda e, j=j: e.tensor_copy(
                out=sco[:, 0:8], in_=psc[:]),
                reads=[b_psc], writes=[b_sco], tag=f"dse{i}")
            sch.step("vector", lambda e: e.tensor_reduce(
                sco[:, 8:9], sco[:, 0:8], axis=AX.X, op=ALU.max),
                reads=[b_sco], writes=[b_sco], tag=f"dmx{i}")
            sch.step("vector", lambda e: e.tensor_scalar(
                out=ohu[:, 0:8], in0=sco[:, 0:8], scalar1=sco[:, 8:9], scalar2=None,
                op0=ALU.is_ge),
                reads=[b_sco], writes=[b_ohu], tag=f"deq{i}")
            sch.step("vector", lambda e: e.tensor_copy(out=oh[:, 0:8], in_=ohu[:, 0:8]),
                     reads=[b_ohu], writes=[b_oh], tag=f"deqf{i}")
            sch.step("vector", lambda e: e.memset(sco[:, 9:10], 0.0),
                     writes=[b_sco], tag=f"dr0{i}")
            for ex in range(8):
                if ex > 0:
                    sch.step("vector", lambda e, ex=ex: e.tensor_tensor(
                        out=sco[:, 9:10], in0=sco[:, 9:10], in1=oh[:, ex - 1:ex],
                        op=ALU.add),
                        reads=[b_oh, b_sco], writes=[b_sco], tag=f"dra{i}_{ex}")
                    sch.step("vector", lambda e, ex=ex: e.tensor_scalar(
                        out=ohu[:, 8 + ex:9 + ex], in0=sco[:, 9:10], scalar1=0.5,
                        scalar2=None, op0=ALU.is_lt),
                        reads=[b_sco], writes=[b_ohu], tag=f"drl{i}_{ex}")
                    sch.step("vector", lambda e, ex=ex: e.tensor_tensor(
                        out=ohu[:, 16 + ex:17 + ex], in0=ohu[:, ex:ex + 1],
                        in1=ohu[:, 8 + ex:9 + ex], op=ALU.logical_and),
                        reads=[b_ohu], writes=[b_ohu], tag=f"dan{i}_{ex}")
                else:
                    sch.step("vector", lambda e: e.tensor_copy(
                        out=ohu[:, 16:17], in_=ohu[:, 0:1]),
                        reads=[b_ohu], writes=[b_ohu], tag=f"dan{i}_0")
            sch.step("vector", lambda e: e.tensor_copy(
                out=oh[:, 0:8], in_=ohu[:, 16:24]),
                reads=[b_ohu], writes=[b_oh], tag=f"dohf{i}")
            for ex in range(8):
                jp = ex % 2
                pa, ba = py[jp], b_py[jp]
                for half in range(2):
                    for c in range(DC):
                        sch.step("tensor", lambda e, c=c, ex=ex, half=half, j=j, pa=pa:
                                 e.matmul(pa[:, half * 512:(half + 1) * 512],
                                          lfm[j][:, c * 128:(c + 1) * 128],
                                          ew[ex][:, c * 1024 + half * 512:c * 1024 + (half + 1) * 512],
                                          start=(c == 0), stop=(c == DC - 1),
                                          skip_group_check=True),
                                 reads=[b_lfm[j], b_ew[ex]], writes=[ba],
                                 tag=f"dym{i}_{ex}_{half}_{c}")
                ohc = oh[:, ex:ex + 1]
                if ex == 0:
                    sch.step("vector", lambda e, pa=pa, ohc=ohc, j=j: e.tensor_scalar(
                        out=acc[j][:], in0=pa[:], scalar1=ohc, scalar2=None,
                        op0=ALU.mult),
                        reads=[ba, b_oh], writes=[b_acc[j]], tag=f"dac{i}_{ex}")
                else:
                    sch.step("vector", lambda e, pa=pa, ohc=ohc, j=j:
                             e.scalar_tensor_tensor(
                        out=acc[j][:], in0=pa[:], scalar=ohc, in1=acc[j][:],
                        op0=ALU.mult, op1=ALU.add),
                        reads=[ba, b_oh, b_acc[j]], writes=[b_acc[j]], tag=f"dac{i}_{ex}")
            sch.step("vector", lambda e, j=j: e.tensor_tensor(
                out=ffo[j][:], in0=lnf[j][:], in1=acc[j][:], op=ALU.add),
                reads=[b_lnf[j], b_acc[j]], writes=[b_ffo[j]], tag=f"dff{i}")
            lnk.emit(i + 16, ffo[j], b_ffo[j], lnr[j], b_lnr[j], f"d2_{i}")
            sch.step("vector", lambda e, j=j: e.tensor_tensor(
                out=xou[j][:], in0=lnr[j][:], in1=xb[j][:], op=ALU.add),
                reads=[b_lnr[j], bxb[j]], writes=[b_xou[j]], tag=f"dad{i}")
            sch.step("sync", lambda e, i=i, j=j: e.dma_start(
                out=xo_d[i * 128:(i + 1) * 128, :], in_=xou[j][:]),
                reads=[b_xou[j]], writes=list(XW), dma=True, tag=f"dst{i}")

        if fx is None:
            sch.emit(block, sems)
    return pr


def stage_d_core_inputs(inputs, core, x_stage):
    b, p = core // 2, core % 2
    ew = inputs["L1_ffn_expert_w"].astype(np.float32)
    d = {
        "x_own": np.ascontiguousarray(x_stage[b][_own_rows(p)]),
        "gwT": np.ascontiguousarray(inputs["L1_ffn_gate_w"].astype(np.float32).T),
        "ewT": np.ascontiguousarray(ew.transpose(0, 2, 1)),
    }
    d.update(const_inputs())
    return d



_PROGS = {}


def _get_prog(name, builder):
    if name not in _PROGS:
        _PROGS[name] = builder()
    return _PROGS[name]


def kernel(**inputs):
    return kernel_fused(**inputs)


def kernel_unfused(**inputs):
    inputs = dict(inputs)
    x = np.ascontiguousarray(np.asarray(inputs["L0_attn_x_bld"], np.float32))
    inputs["x_full"] = x

    prA = _get_prog("A", build_stage_a)
    res = run_bass_kernel_spmd(prA.nc, [stage_a_core_inputs(inputs, c) for c in range(8)],
                               core_ids=list(range(8)))
    x1 = _unshard([res.results[c]["x_out"] for c in range(8)])

    prB = _get_prog("B", build_stage_b)
    res = run_bass_kernel_spmd(prB.nc, [stage_b_core_inputs(inputs, c, x1) for c in range(8)],
                               core_ids=list(range(8)))
    x2 = _unshard([res.results[c]["x_out"] for c in range(8)])

    prC = _get_prog("C", build_stage_c)
    res = run_bass_kernel_spmd(prC.nc, [stage_c_core_inputs(inputs, c, x2) for c in range(8)],
                               core_ids=list(range(8)))
    x3 = _unshard([res.results[c]["x_out"] for c in range(8)])

    prD = _get_prog("D", build_stage_d)
    res = run_bass_kernel_spmd(prD.nc, [stage_d_core_inputs(inputs, c, x3) for c in range(8)],
                               core_ids=list(range(8)))
    return _unshard([res.results[c]["x_out"] for c in range(8)])



def build_fused(dbg=False):
    pr = Prog()
    nc = pr.nc
    sch = pr.sch
    sems = {e: pr.stack.enter_context(nc.semaphore("sem_" + e)) for e in ENGS}

    with nc.Block() as block:
        cx = emit_consts(pr, pr.stack)
        x1_dr = nc.dram_tensor("x1_dr", [TOK, D], F32)
        x2_dr = nc.dram_tensor("x2_dr", [TOK, D], F32)
        x3_dr = nc.dram_tensor("x3_dr", [TOK, D], F32)
        ag_out = nc.dram_tensor("ag_out", [8 * TOK, D], F32)
        b_x1 = sch.buf("x1_dr", kind="d")
        b_x2 = sch.buf("x2_dr", kind="d")
        b_x3 = sch.buf("x3_dr", kind="d")
        b_ag = sch.buf("ag_out", kind="d")
        b_x0 = sch.buf("x0_dummy", kind="d")

        pr.ns = "a_"
        x0_d = pr.din("x_own", [TOK, D])
        st_a = ExitStack()
        build_stage_a(fx=dict(pr=pr, st=st_a, block=block, sems=sems, cx=cx,
                              x_in=x0_d, x_in_buf=b_x0, x_out=x1_dr, x_out_buf=b_x1,
                              dbga=dbg))
        st_a.close()
        sch.barrier()

        pr.ns = "b_"
        st_b = ExitStack()
        build_stage_b(fx=dict(pr=pr, st=st_b, block=block, sems=sems, cx=cx,
                              x_in=x1_dr, x_in_buf=b_x1, x_out=x2_dr, x_out_buf=b_x2))
        st_b.close()
        sch.barrier()

        sch.step("gpsimd", lambda e: e.collective_compute(
            "AllGather", ALU.bypass, replica_groups=[list(range(8))],
            ins=[x2_dr.ap().opt()], outs=[ag_out.ap().opt()]),
            reads=[b_x2], writes=[b_ag], collective=True, tag="ag_x2", gsem="cc")

        pr.ns = "c_"
        st_c = ExitStack()
        build_stage_c(fx=dict(pr=pr, st=st_c, block=block, sems=sems, cx=cx,
                              dbgth=dbg,
                              x_in=x2_dr, x_in_buf=b_x2, x_out=x3_dr, x_out_buf=b_x3,
                              ag_out=ag_out, ag_out_buf=b_ag))
        st_c.close()
        sch.barrier()

        pr.ns = "d_"
        xo_d = pr.dout("x_out", [TOK, D])
        st_d = ExitStack()
        build_stage_d(fx=dict(pr=pr, st=st_d, block=block, sems=sems, cx=cx,
                              x_in=x3_dr, x_in_buf=b_x3, x_out=xo_d,
                              x_out_buf=sch.buf("xo_dummy", kind="d")))
        st_d.close()

        pr.ns = ""
        if dbg:
            for nm, t, bb in (("dbg_x1", x1_dr, b_x1), ("dbg_x2", x2_dr, b_x2),
                              ("dbg_x3", x3_dr, b_x3), ("dbg_ag", ag_out, b_ag)):
                od = pr.dout(nm, list(t.shape))
                sch.step("gpsimd", lambda e, od=od, t=t: e.dma_start(
                    out=od[:, :], in_=t[:, :]),
                    reads=[bb], dma=True, tag=nm)
        sch.emit(block, sems)
    return pr


def fused_core_inputs(inputs, core):
    x = inputs["x_full"]
    b = core // 2
    d = dict(const_inputs())
    for pfx, fn, xs in (("a_", stage_a_core_inputs, None),
                        ("b_", stage_b_core_inputs, x),
                        ("c_", stage_c_core_inputs, x),
                        ("d_", stage_d_core_inputs, x)):
        sub = fn(inputs, core) if xs is None else fn(inputs, core, xs)
        for k, v in sub.items():
            if k in ("ident_bf", "ident_f", "ones_bf", "ones_f", "ones_row"):
                continue
            if pfx != "a_" and k in ("x_own", "x_batch"):
                continue
            d[pfx + k] = v
    bfl = np.zeros((128, 4), np.float32)
    bfl[:, b] = 1.0
    d["c_bflags"] = bfl
    return d


def kernel_fused(**inputs):
    inputs = dict(inputs)
    inputs["x_full"] = np.ascontiguousarray(np.asarray(inputs["L0_attn_x_bld"], np.float32))
    prF = _get_prog("F", build_fused)
    res = run_bass_kernel_spmd(prF.nc, [fused_core_inputs(inputs, c) for c in range(8)],
                               core_ids=list(range(8)))
    return _unshard([res.results[c]["d_x_out"] for c in range(8)])


# version 07
# speedup vs baseline: 1.0573x; 1.0355x over previous
import numpy as np
import ml_dtypes
from contextlib import ExitStack

import concourse.bass as bass
import concourse.mybir as mybir
from concourse.bass_utils import run_bass_kernel_spmd

F32 = mybir.dt.float32
BF16 = mybir.dt.bfloat16
I32 = mybir.dt.int32
U8 = mybir.dt.uint8
AF = mybir.ActivationFunctionType
ALU = mybir.AluOpType
AX = mybir.AxisListType
BF = ml_dtypes.bfloat16

B, S, D = 4, 4096, 1024
H, HD = 16, 64
LD = 256
TOK = 2048
NSLOT, SLOT = 4, 512
EXT = [2, 4, 6, 8]
Q0 = {0: [0, 1536, 2048, 3584], 1: [512, 1024, 2560, 3072]}
NEG = -1.0e30
TOPK = 512
IDXH, IDXD = 4, 64
FFI = 3072
EPS = 1e-5
DC = D // 128

ENGS = ("sync", "tensor", "scalar", "vector", "gpsimd")



class Buf:
    __slots__ = ("name", "w", "r", "kind", "wg")

    def __init__(self, name, kind="s"):
        self.name = name
        self.kind = kind
        self.w = None
        self.r = []
        self.wg = {}


class Step:
    __slots__ = ("eng", "fn", "deps", "idx", "inc_ord", "amt", "tag", "semo", "gsem",
                 "gidx")

    def __init__(self, eng, fn, deps, amt, tag, semo=False, gsem=None):
        self.eng, self.fn, self.deps, self.amt, self.tag = eng, fn, deps, amt, tag
        self.inc_ord = None
        self.semo = semo
        self.gsem = gsem

    @property
    def key(self):
        return self.gsem or self.eng


class Sched:

    def __init__(self, pr):
        self.pr = pr
        self.steps = {e: [] for e in ENGS}
        self.all = []
        self.barriers = []

    def buf(self, name, kind="s"):
        return Buf(name, kind)

    def barrier(self):
        self.barriers.append({e: len(self.steps[e]) for e in ENGS})

    def step(self, eng, fn, reads=(), writes=(), tag="", dma=False, collective=False,
             gsem=None):
        deps = set()
        for b_ in reads:
            if b_.w is not None:
                deps.add(b_.w)
            deps.update(b_.wg.values())
        for b_ in writes:
            if b_.w is not None:
                deps.add(b_.w)
            deps.update(b_.wg.values())
            for rd in b_.r:
                deps.add(rd)
        if dma and not collective and gsem is None:
            sb = ([b_ for b_ in writes if b_.kind == "s"]
                  or [b_ for b_ in reads if b_.kind == "s"]
                  or list(writes) or list(reads))
            gsem = "g_" + sb[0].name if sb else "g_misc"
        st = Step(eng, fn, deps, 16 if (dma and not collective) else 1, tag,
                  semo=collective, gsem=gsem)
        st.idx = (eng, len(self.steps[eng]))
        st.gidx = len(self.all)
        self.steps[eng].append(st)
        self.all.append(st)
        for b_ in reads:
            b_.r.append(st)
        for b_ in writes:
            b_.w = st
            b_.r = []
            if dma:
                b_.wg[gsem] = st
            else:
                b_.wg.clear()
        return st

    def emit(self, block, sems, final_outputs=()):
        pr = self.pr
        import bisect as _bisect
        bars_by_eng = {e: sorted(b[e] for b in self.barriers) for e in ENGS}
        gsegs = {}
        for e in ENGS:
            for i, st in enumerate(self.steps[e]):
                if st.gsem and st.gsem != "cc":
                    seg = _bisect.bisect_right(bars_by_eng[e], i)
                    gsegs.setdefault(st.gsem, set()).add(seg)
        pool = []
        assign = {}
        for g in sorted(gsegs):
            for bi, used in enumerate(pool):
                if used.isdisjoint(gsegs[g]):
                    assign[g] = bi
                    used |= gsegs[g]
                    break
            else:
                assign[g] = len(pool)
                pool.append(set(gsegs[g]))
        for st in self.all:
            if st.gsem and st.gsem != "cc":
                st.gsem = f"P{assign[st.gsem]}"
        gnames = sorted({st.gsem for st in self.all if st.gsem})
        for gn in gnames:
            sems[gn] = pr.stack.enter_context(pr.nc.semaphore("gsem_" + gn))
        keys = list(ENGS) + gnames

        needed = set()
        for st in self.all:
            if st.amt == 16 or st.semo or st.gsem:
                needed.add(st)
            for dp in st.deps:
                if dp.eng != st.eng or dp.amt == 16 or dp.semo:
                    needed.add(dp)
        for bar in self.barriers:
            for e in ENGS:
                if bar[e] > 0:
                    needed.add(self.steps[e][bar[e] - 1])
        for e in ENGS:
            if self.steps[e]:
                needed.add(self.steps[e][-1])
        cum = {k: 0 for k in keys}
        for st in self.all:
            if st in needed:
                cum[st.key] += st.amt
                st.inc_ord = cum[st.key]
        totals = dict(cum)

        bar_cnt = []
        for bar in self.barriers:
            cnt = {k: 0 for k in keys}
            for e in ENGS:
                for st in self.steps[e][:bar[e]]:
                    if st in needed:
                        cnt[st.key] += st.amt
            bar_cnt.append(cnt)

        barrier_at = {e: {} for e in ENGS}
        for bi, bar in enumerate(self.barriers):
            for e in ENGS:
                barrier_at[e].setdefault(bar[e], []).append(bi)

        def run_engine(ename):
            def body(eng):
                watermark = {k: 0 for k in keys}
                last_drain = -1
                for i, st in enumerate(self.steps[ename]):
                    for bi in barrier_at[ename].get(i, ()):
                        for k in keys:
                            v = bar_cnt[bi][k]
                            if v > watermark[k]:
                                watermark[k] = v
                                eng.wait_ge(sems[k], v)
                    need_drain = False
                    for dp in st.deps:
                        if (dp.eng == ename and dp.amt != 16 and not dp.semo
                                and dp.gsem is None):
                            if dp.idx[1] > last_drain and ename != "tensor":
                                need_drain = True
                        else:
                            v = dp.inc_ord
                            assert v is not None, (st.tag, dp.tag)
                            if v > watermark[dp.key]:
                                watermark[dp.key] = v
                                eng.wait_ge(sems[dp.key], v)
                    if need_drain:
                        eng.drain()
                        last_drain = i - 1
                    inst = st.fn(eng)
                    if st.inc_ord is not None:
                        assert inst is not None, st.tag
                        inst.then_inc(sems[st.key], st.amt)
            return body

        for e in ENGS:
            if self.steps[e]:
                getattr(block, e)(run_engine(e))

        def fin(eng):
            for k in keys:
                if k != "sync" and totals[k]:
                    eng.wait_ge(sems[k], totals[k])
            if totals["sync"]:
                eng.wait_ge(sems["sync"], totals["sync"])
        block.sync(fin)


class Prog:
    def __init__(self):
        self.nc = bass.Bass("TRN2", target_bir_lowering=False, debug=False)
        self.stack = ExitStack()
        self.sch = Sched(self)
        self._sb_names = set()
        self.ns = ""

    def din(self, name, shape, dt=F32):
        return self.nc.dram_tensor(self.ns + name, list(shape), dt,
                                   kind="ExternalInput").ap()

    def dout(self, name, shape, dt=F32):
        return self.nc.dram_tensor(self.ns + name, list(shape), dt,
                                   kind="ExternalOutput").ap()

    def dram(self, name, shape, dt=F32):
        return self.nc.dram_tensor(self.ns + name, list(shape), dt)

    def sb(self, st, name, shape, dt=F32):
        name = self.ns + name
        assert name not in self._sb_names, name
        self._sb_names.add(name)
        return st.enter_context(self.nc.sbuf_tensor(name, list(shape), dt))

    def ps(self, st, name, shape, dt=F32):
        return st.enter_context(self.nc.psum_tensor(self.ns + name, list(shape), dt))



def _own_rows(parity):
    return np.concatenate([np.arange(q0, q0 + SLOT) for q0 in Q0[parity]])


def _causal_masks(parity, layer):
    out = np.zeros((NSLOT, 2, SLOT, SLOT), np.float32)
    for s_ in range(NSLOT):
        q0 = Q0[parity][s_]
        k_base = (EXT[s_] - 2) * SLOT
        q = q0 + np.arange(SLOT)[None, :]
        k = k_base + np.arange(2 * SLOT)[:, None]
        m = np.where(k <= q, 0.0, NEG).astype(np.float32)
        if layer == 'l1' and q0 == 0:
            m = np.where(k < SLOT, 0.0, NEG).astype(np.float32)
        out[s_, 0] = m[:SLOT]
        out[s_, 1] = m[SLOT:]
    return np.ascontiguousarray(out.astype(BF))


_IDENT_BF = np.eye(128, dtype=BF)
_IDENT_F32 = np.eye(128, dtype=np.float32)
_ONES_BF = np.ones((128, 1), dtype=BF)
_ONES_F32 = np.ones((128, 1), dtype=np.float32)


def _unshard(per_core_arrays):
    x = np.zeros((B, S, D), np.float32)
    for c in range(8):
        x[c // 2][_own_rows(c % 2)] = per_core_arrays[c]
    return x



class Ctx:
    pass


class _UseBlock:

    def __init__(self, block):
        self.block = block

    def __enter__(self):
        return self.block

    def __exit__(self, *a):
        return False


def emit_consts(pr, st):
    cx = Ctx()
    cx.ident_bf_d = pr.din("ident_bf", [128, 128], BF16)
    cx.ident_f_d = pr.din("ident_f", [128, 128], F32)
    cx.ones_bf_d = pr.din("ones_bf", [128, 1], BF16)
    cx.ones_row_d = pr.din("ones_row", [1, 128], F32)
    cx.ones_f_d = pr.din("ones_f", [128, 1], F32)
    cx.ident_bf = pr.sb(st, "c_ident_bf", [128, 128], BF16)
    cx.ident_f = pr.sb(st, "c_ident_f", [128, 128], F32)
    cx.ones_bf = pr.sb(st, "c_ones_bf", [128, 1], BF16)
    cx.ones_row = pr.sb(st, "c_ones_row", [1, 128], F32)
    cx.ones_f = pr.sb(st, "c_ones_f", [128, 1], F32)
    sch = pr.sch
    cx.b_ident_bf, cx.b_ident_f = sch.buf("ident_bf"), sch.buf("ident_f")
    cx.b_ones_bf, cx.b_ones_f = sch.buf("ones_bf"), sch.buf("ones_f")
    cx.b_ones_row = sch.buf("ones_row")
    for t, d_, b_ in ((cx.ident_bf, cx.ident_bf_d, cx.b_ident_bf),
                     (cx.ident_f, cx.ident_f_d, cx.b_ident_f),
                     (cx.ones_bf, cx.ones_bf_d, cx.b_ones_bf),
                     (cx.ones_row, cx.ones_row_d, cx.b_ones_row),
                     (cx.ones_f, cx.ones_f_d, cx.b_ones_f)):
        sch.step("sync", lambda e, t=t, d=d_: e.dma_start(out=t[:], in_=d[:]),
                 writes=[b_], dma=True, tag="const")
    return cx


def const_inputs():
    return {"ident_bf": _IDENT_BF, "ident_f": _IDENT_F32,
            "ones_bf": _ONES_BF, "ones_f": _ONES_F32,
            "ones_row": np.ones((1, 128), np.float32)}


def emit_ln_to_fm(pr, st, cx, name, src_dram, n_tiles, fm_chunks, fm_col0,
                  keep_tm=None, src_buf=None):
    sch = pr.sch
    xb = [pr.sb(st, f"{name}_x{j}", [128, D], F32) for j in range(2)]
    bxb = [sch.buf(f"{name}_x{j}") for j in range(2)]
    ssq = [pr.sb(st, f"{name}_q{j}", [128, 1], F32) for j in range(2)]
    bssq = [sch.buf(f"{name}_q{j}") for j in range(2)]
    sscr = pr.sb(st, f"{name}_scr", [128, D], F32)
    bscr = sch.buf(f"{name}_scr")
    stat = [pr.sb(st, f"{name}_st{j}", [128, 4], F32) for j in range(2)]
    bstat = [sch.buf(f"{name}_st{j}") for j in range(2)]
    lnt = [pr.sb(st, f"{name}_l{j}", [128, D], BF16) for j in range(2)]
    blnt = [sch.buf(f"{name}_l{j}") for j in range(2)]
    ptr = [pr.ps(st, f"{name}_p{j}", [128, D], BF16) for j in range(2)]
    bptr = [sch.buf(f"{name}_p{j}") for j in range(2)]

    for i in range(n_tiles):
        j = i % 2
        x, bx = xb[j], bxb[j]
        sq, bsq = ssq[j], bssq[j]
        stt, bst = stat[j], bstat[j]
        ln, bln = lnt[j], blnt[j]
        pt, bpt = ptr[j], bptr[j]
        mean, var, rstd, nmr = (stt[:, 0:1], stt[:, 1:2], stt[:, 2:3], stt[:, 3:4])
        sch.step("sync", lambda e, x=x, i=i: e.dma_start(out=x[:], in_=src_dram(i)),
                 writes=[bx], dma=True, tag=f"{name}_ld{i}")
        sch.step("scalar", lambda e, x=x, sq=sq: e.activation(
            out=sscr[:], in_=x[:], func=AF.Square, accum_out=sq[:]),
            reads=[bx], writes=[bscr, bsq], tag=f"{name}_sq{i}")
        sch.step("vector", lambda e, x=x, mean=mean: e.tensor_reduce(
            mean, x[:], axis=AX.X, op=ALU.add), reads=[bx], writes=[bst],
            tag=f"{name}_sum{i}")
        sch.step("vector", lambda e, mean=mean: e.tensor_scalar(
            out=mean, in0=mean, scalar1=1.0 / D, scalar2=None, op0=ALU.mult),
            reads=[bst], writes=[bst], tag=f"{name}_mn{i}")
        sch.step("vector", lambda e, var=var, sq=sq: e.tensor_scalar(
            out=var, in0=sq[:], scalar1=1.0 / D, scalar2=None, op0=ALU.mult),
            reads=[bsq, bst], writes=[bst], tag=f"{name}_v1{i}")
        sch.step("vector", lambda e, nmr=nmr, mean=mean: e.tensor_tensor(
            out=nmr, in0=mean, in1=mean, op=ALU.mult),
            reads=[bst], writes=[bst], tag=f"{name}_m2{i}")
        sch.step("vector", lambda e, var=var, nmr=nmr: e.tensor_tensor(
            out=var, in0=var, in1=nmr, op=ALU.subtract),
            reads=[bst], writes=[bst], tag=f"{name}_v2{i}")
        sch.step("vector", lambda e, var=var: e.tensor_scalar(
            out=var, in0=var, scalar1=EPS, scalar2=None, op0=ALU.add),
            reads=[bst], writes=[bst], tag=f"{name}_v3{i}")
        sch.step("scalar", lambda e, rstd=rstd, var=var: e.activation(
            out=rstd, in_=var, func=AF.Ln), reads=[bst], writes=[bst],
            tag=f"{name}_ln{i}")
        sch.step("scalar", lambda e, rstd=rstd: e.activation(
            out=rstd, in_=rstd, func=AF.Exp, scale=-0.5),
            reads=[bst], writes=[bst], tag=f"{name}_ex{i}")
        sch.step("vector", lambda e, nmr=nmr, mean=mean, rstd=rstd:
                 e.scalar_tensor_tensor(out=nmr, in0=mean, scalar=-1.0,
                                        in1=rstd, op0=ALU.mult, op1=ALU.mult),
                 reads=[bst], writes=[bst], tag=f"{name}_nm{i}")
        sch.step("scalar", lambda e, ln=ln, x=x, rstd=rstd, nmr=nmr:
                 e.activation(out=ln[:], in_=x[:], func=AF.Identity,
                              bias=nmr, scale=rstd),
                 reads=[bx, bst], writes=[bln], tag=f"{name}_ap{i}")
        if keep_tm is not None:
            dst_t, dst_b = keep_tm
            sch.step("vector", lambda e, ln=ln, i=i: e.tensor_copy(
                out=dst_t[:, i * D:(i + 1) * D], in_=ln[:]),
                reads=[bln], writes=[dst_b], tag=f"{name}_tm{i}")
        for jj in range(DC):
            sch.step("tensor", lambda e, ln=ln, pt=pt, jj=jj: e.transpose(
                pt[:, jj * 128:(jj + 1) * 128], ln[:, jj * 128:(jj + 1) * 128],
                cx.ident_bf[:]),
                reads=[bln, cx.b_ident_bf], writes=[bpt], tag=f"{name}_tr{i}_{jj}")
        for jj in range(DC):
            fmt, fmb = fm_chunks[jj]
            c0 = fm_col0 + i * 128
            sch.step("vector", lambda e, pt=pt, jj=jj, fmt=fmt, c0=c0:
                     e.tensor_copy(out=fmt[:, c0:c0 + 128],
                                   in_=pt[:, jj * 128:(jj + 1) * 128]),
                     reads=[bpt], writes=[fmb], tag=f"{name}_ev{i}_{jj}")


def emit_mm_chain(pr, sch, name, out_ps, bout, lhs_list, rhs_list, reads,
                  extra_first=None):
    n = len(lhs_list)
    for i in range(n):
        first = (i == 0 and extra_first is None)
        sch.step("tensor", lambda e, i=i, first=first:
                 e.matmul(out_ps, lhs_list[i], rhs_list[i],
                          start=first, stop=(i == n - 1)),
                 reads=list(reads), writes=[bout], tag=f"{name}_mm{i}")



def build_stage_a(dbg=False, spill_out=False, fx=None):
    if fx is None:
        pr = Prog()
        st = pr.stack
    else:
        pr = fx["pr"]
        st = fx["st"]
    sch = pr.sch
    nc = pr.nc
    XR = [fx["x_in_buf"]] if fx else []
    XW = [fx["x_out_buf"]] if fx else []

    x_own_d = fx["x_in"] if fx else pr.din("x_own", [TOK, D])
    x_bat_d = pr.din("x_batch", [S, D])
    wq_d = pr.din("WqT", [D, H * HD])
    wkvd_d = pr.din("Wkvd", [D, LD])
    wku_d = pr.din("WkuT", [LD, H * HD])
    wvu_d = pr.din("WvuT", [LD, H * HD])
    wout_d = pr.din("WoutT", [H * HD, D])
    mask_d = pr.din("mask", [NSLOT, 2, SLOT, SLOT], BF16)
    mk = pr.dout if (dbg or spill_out) else pr.dram
    k_sp = mk("k_sp", [32, 8, 128, 128], BF16)
    v_sp = mk("v_sp", [32, 128, 16 * 65], BF16)
    xo_d = fx["x_out"] if fx else pr.dout("x_out", [TOK, D])

    if fx is None:
        sems = {e: pr.stack.enter_context(nc.semaphore("sem_" + e)) for e in ENGS}
        blk_cm = nc.Block()
    else:
        sems = fx["sems"]
        blk_cm = _UseBlock(fx["block"])

    with blk_cm as block:
        cx = fx["cx"] if fx else emit_consts(pr, st)

        wq = pr.sb(st, "wq", [128, DC * 1024], BF16); b_wq = sch.buf("wq")
        wkvd = pr.sb(st, "wkvd", [128, DC * LD], BF16); b_wkvd = sch.buf("wkvd")
        wku = pr.sb(st, "wku", [128, 2 * 1024], BF16); b_wku = sch.buf("wku")
        wvu = pr.sb(st, "wvu", [128, 2 * 1024], BF16); b_wvu = sch.buf("wvu")
        wout = pr.sb(st, "wout", [128, DC * 1024], BF16); b_wout = sch.buf("wout")
        for c in range(DC):
            sch.step("gpsimd", lambda e, c=c: e.dma_start(
                out=wq[:, c * 1024:(c + 1) * 1024], in_=wq_d[c * 128:(c + 1) * 128, :]),
                writes=[b_wq], dma=True, tag=f"wq{c}")
            sch.step("gpsimd", lambda e, c=c: e.dma_start(
                out=wkvd[:, c * LD:(c + 1) * LD], in_=wkvd_d[c * 128:(c + 1) * 128, :]),
                writes=[b_wkvd], dma=True, tag=f"wkvd{c}")
            sch.step("gpsimd", lambda e, c=c: e.dma_start(
                out=wout[:, c * 1024:(c + 1) * 1024], in_=wout_d[c * 128:(c + 1) * 128, :]),
                writes=[b_wout], dma=True, tag=f"wout{c}")
        for l in range(2):
            sch.step("gpsimd", lambda e, l=l: e.dma_start(
                out=wku[:, l * 1024:(l + 1) * 1024], in_=wku_d[l * 128:(l + 1) * 128, :]),
                writes=[b_wku], dma=True, tag=f"wku{l}")
            sch.step("gpsimd", lambda e, l=l: e.dma_start(
                out=wvu[:, l * 1024:(l + 1) * 1024], in_=wvu_d[l * 128:(l + 1) * 128, :]),
                writes=[b_wvu], dma=True, tag=f"wvu{l}")

        lat = [pr.sb(st, f"lat{l}", [128, S], BF16) for l in range(2)]
        b_lat = [sch.buf(f"lat{l}") for l in range(2)]
        qsb = [pr.sb(st, f"qsb{c}", [128, TOK], BF16) for c in range(DC)]
        b_qsb = [sch.buf(f"qsb{c}") for c in range(DC)]
        nrm = pr.sb(st, "nrm", [1, 16], F32)
        b_nrm = sch.buf("nrm")
        knm = pr.sb(st, "knm", [1, 16], F32)
        b_knm = sch.buf("knm")
        qnm = pr.sb(st, "qnm", [1, 16], F32)
        b_qnm = sch.buf("qnm")
        ustars = pr.sb(st, "ustars", [128, 16], F32)
        b_ustars = sch.buf("ustars")

        ph13 = ExitStack()
        gfm = [[pr.sb(ph13, f"gfm{r}_{c}", [128, 512], BF16) for c in range(DC)]
               for r in range(2)]
        b_gfm = [sch.buf(f"gfm{r}") for r in range(2)]
        xb = [pr.sb(ph13, f"lnx{j}", [128, D], F32) for j in range(2)]
        bxb = [sch.buf(f"lnx{j}") for j in range(2)]
        ssq = [pr.sb(ph13, f"lnq{j}", [128, 1], F32) for j in range(2)]
        bssq = [sch.buf(f"lnq{j}") for j in range(2)]
        sscr = pr.sb(ph13, "lnscr", [128, D], F32)
        bscr = sch.buf("lnscr")
        stat = [pr.sb(ph13, f"lnst{j}", [128, 4], F32) for j in range(2)]
        bstat = [sch.buf(f"lnst{j}") for j in range(2)]
        lnt = [pr.sb(ph13, f"lnl{j}", [128, D], BF16) for j in range(2)]
        blnt = [sch.buf(f"lnl{j}") for j in range(2)]
        ptr = [pr.ps(ph13, f"lnp{j}", [128, D], BF16) for j in range(2)]
        bptr = [sch.buf(f"lnp{j}") for j in range(2)]
        acc = [pr.ps(ph13, f"acc{j}", [128, 512], F32) for j in range(2)]
        b_acc = [sch.buf(f"acc{j}") for j in range(2)]
        psn = [pr.ps(ph13, f"psn{j}", [1, 512], F32) for j in range(2)]
        b_psn = [sch.buf(f"psn{j}") for j in range(2)]

        def ln_tile(i, src_ap, act_scale=None, xrd=()):
            j = i % 2
            x, bx, sq, bsq = xb[j], bxb[j], ssq[j], bssq[j]
            stt, bst, ln, bln = stat[j], bstat[j], lnt[j], blnt[j]
            mean, var, rstd, nmr = stt[:, 0:1], stt[:, 1:2], stt[:, 2:3], stt[:, 3:4]
            sch.step("sync", lambda e: e.dma_start(out=x[:], in_=src_ap),
                     reads=list(xrd), writes=[bx], dma=True, tag=f"lnld{i}")
            sch.step("scalar", lambda e: e.activation(
                out=sscr[:], in_=x[:], func=AF.Square, accum_out=sq[:]),
                reads=[bx], writes=[bscr, bsq], tag=f"lnsq{i}")
            sch.step("vector", lambda e: e.tensor_reduce(mean, x[:], axis=AX.X, op=ALU.add),
                     reads=[bx], writes=[bst], tag=f"lnsum{i}")
            sch.step("vector", lambda e: e.tensor_scalar(
                out=mean, in0=mean, scalar1=1.0 / D, scalar2=None, op0=ALU.mult),
                reads=[bst], writes=[bst], tag=f"lnmn{i}")
            sch.step("vector", lambda e: e.tensor_scalar(
                out=var, in0=sq[:], scalar1=1.0 / D, scalar2=None, op0=ALU.mult),
                reads=[bsq, bst], writes=[bst], tag=f"lnv1{i}")
            sch.step("vector", lambda e: e.tensor_tensor(
                out=nmr, in0=mean, in1=mean, op=ALU.mult),
                reads=[bst], writes=[bst], tag=f"lnm2{i}")
            sch.step("vector", lambda e: e.tensor_tensor(
                out=var, in0=var, in1=nmr, op=ALU.subtract),
                reads=[bst], writes=[bst], tag=f"lnv2{i}")
            sch.step("vector", lambda e: e.tensor_scalar(
                out=var, in0=var, scalar1=EPS, scalar2=None, op0=ALU.add),
                reads=[bst], writes=[bst], tag=f"lnv3{i}")
            sch.step("scalar", lambda e: e.activation(out=rstd, in_=var, func=AF.Ln),
                     reads=[bst], writes=[bst], tag=f"lnln{i}")
            sch.step("scalar", lambda e: e.activation(
                out=rstd, in_=rstd, func=AF.Exp, scale=-0.5),
                reads=[bst], writes=[bst], tag=f"lnex{i}")
            sch.step("vector", lambda e: e.scalar_tensor_tensor(
                out=nmr, in0=mean, scalar=-1.0, in1=rstd, op0=ALU.mult, op1=ALU.mult),
                reads=[bst], writes=[bst], tag=f"lnnm{i}")
            sch.step("scalar", lambda e: e.activation(
                out=ln[:], in_=x[:], func=AF.Identity, bias=nmr, scale=rstd),
                reads=[bx, bst], writes=[bln], tag=f"lnap{i}")
            return ln, bln

        def transpose_to(i, ln, bln, dst_list, dst_buf, col0):
            j = i % 2
            pt, bpt = ptr[j], bptr[j]
            for c in range(DC):
                sch.step("tensor", lambda e, c=c: e.transpose(
                    pt[:, c * 128:(c + 1) * 128], ln[:, c * 128:(c + 1) * 128],
                    cx.ident_bf[:]),
                    reads=[bln, cx.b_ident_bf], writes=[bpt], tag=f"tr{i}_{c}")
            for c in range(DC):
                sch.step("vector", lambda e, c=c: e.tensor_copy(
                    out=dst_list[c][:, col0:col0 + 128],
                    in_=pt[:, c * 128:(c + 1) * 128]),
                    reads=[bpt], writes=[dst_buf], tag=f"ev{i}_{c}")

        ti = 0
        for g in range(8):
            r = g % 2
            for u in range(4):
                ln, bln = ln_tile(ti, x_bat_d[(g * 4 + u) * 128:(g * 4 + u + 1) * 128, :])
                transpose_to(ti, ln, bln, gfm[r], b_gfm[r], u * 128)
                ti += 1
            for l in range(2):
                pa, ba = acc[(g * 2 + l) % 2], b_acc[(g * 2 + l) % 2]
                for c in range(DC):
                    sch.step("tensor", lambda e, c=c, l=l, r=r, pa=pa: e.matmul(
                        pa[:], wkvd[:, c * LD + l * 128:c * LD + (l + 1) * 128],
                        gfm[r][c][:], start=(c == 0), stop=(c == DC - 1)),
                        reads=[b_wkvd, b_gfm[r]], writes=[ba], tag=f"latm{g}_{l}_{c}")
                sch.step("scalar", lambda e, l=l, g=g, pa=pa: e.activation(
                    out=lat[l][:, g * 512:(g + 1) * 512], in_=pa[:], func=AF.Identity),
                    reads=[ba], writes=[b_lat[l]], tag=f"late{g}_{l}")
        for qt in range(4):
            r = ti % 2
            for u in range(4):
                ln, bln = ln_tile(ti, x_own_d[(qt * 4 + u) * 128:(qt * 4 + u + 1) * 128, :],
                                  xrd=XR)
                transpose_to(ti, ln, bln, gfm[r], b_gfm[r], u * 128)
                ti += 1
            for hc in range(DC):
                pa, ba = acc[(qt * DC + hc) % 2], b_acc[(qt * DC + hc) % 2]
                for c in range(DC):
                    sch.step("tensor", lambda e, c=c, hc=hc, r=r, pa=pa: e.matmul(
                        pa[:], wq[:, c * 1024 + hc * 128:c * 1024 + (hc + 1) * 128],
                        gfm[r][c][:], start=(c == 0), stop=(c == DC - 1)),
                        reads=[b_wq, b_gfm[r]], writes=[ba], tag=f"qm{qt}_{hc}_{c}")
                sch.step("scalar", lambda e, hc=hc, qt=qt, pa=pa: e.activation(
                    out=qsb[hc][:, qt * 512:(qt + 1) * 512], in_=pa[:],
                    func=AF.Identity, scale=0.125),
                    reads=[ba], writes=[b_qsb[hc]], tag=f"qe{qt}_{hc}")

        kst = [pr.sb(ph13, f"kst{j}", [128, 512], BF16) for j in range(2)]
        b_kst = [sch.buf(f"kst{j}") for j in range(2)]
        ksq = [pr.sb(ph13, f"ksq{j}", [128, 512], BF16) for j in range(2)]
        b_ksq = [sch.buf(f"ksq{j}") for j in range(2)]
        b_ksp = sch.buf("k_sp", kind="d")
        b_vspb = sch.buf("v_sp", kind="d")
        sch.step("vector", lambda e: e.memset(nrm[:], 0.0), writes=[b_nrm], tag="nrm0")
        sch.step("vector", lambda e: e.memset(knm[:], 0.0), writes=[b_knm], tag="knm0")
        sch.step("vector", lambda e: e.memset(qnm[:], 0.0), writes=[b_qnm], tag="qnm0")
        for kt in range(8):
            for hc in range(DC):
                j = (kt * DC + hc) % 2
                pa, ba = acc[j], b_acc[j]
                for l in range(2):
                    sch.step("tensor", lambda e, l=l, hc=hc, kt=kt, pa=pa: e.matmul(
                        pa[:], wku[:, l * 1024 + hc * 128:l * 1024 + (hc + 1) * 128],
                        lat[l][:, kt * 512:(kt + 1) * 512],
                        start=(l == 0), stop=(l == 1)),
                        reads=[b_wku, b_lat[l]], writes=[ba], tag=f"km{kt}_{hc}_{l}")
                sch.step("scalar", lambda e, j=j, pa=pa: e.activation(
                    out=kst[j][:], in_=pa[:], func=AF.Identity),
                    reads=[ba], writes=[b_kst[j]], tag=f"ke{kt}_{hc}")
                sch.step("scalar", lambda e, j=j: e.activation(
                    out=ksq[j][:], in_=kst[j][:], func=AF.Square),
                    reads=[b_kst[j]], writes=[b_ksq[j]], tag=f"kq{kt}_{hc}")
                for par in range(2):
                    pn, bn = psn[par], b_psn[par]
                    sch.step("tensor", lambda e, j=j, par=par, pn=pn: e.matmul(
                        pn[:], cx.ones_bf[par * 64:(par + 1) * 64, :],
                        ksq[j][par * 64:(par + 1) * 64, :],
                        start=True, stop=True, skip_group_check=True),
                        reads=[b_ksq[j], cx.b_ones_bf], writes=[bn], tag=f"kn{kt}_{hc}_{par}")
                for par in range(2):
                    hh = hc * 2 + par
                    pn, bn = psn[par], b_psn[par]
                    sch.step("vector", lambda e, pn=pn: e.reduce_max(
                        nrm[0:1, 1:2], pn[0:1, :], axis=AX.X),
                        reads=[bn], writes=[b_nrm], tag=f"knr{kt}_{hc}_{par}")
                    sch.step("vector", lambda e, hh=hh: e.tensor_tensor(
                        out=knm[0:1, hh:hh + 1], in0=knm[0:1, hh:hh + 1],
                        in1=nrm[0:1, 1:2], op=ALU.max),
                        reads=[b_nrm, b_knm], writes=[b_knm], tag=f"knx{kt}_{hc}_{par}")
                sch.step("sync", lambda e, j=j, kt=kt, hc=hc: e.dma_start(
                    out=k_sp[kt * 4:(kt + 1) * 4, hc].rearrange("a p c -> p a c"),
                    in_=kst[j][:].rearrange("p (a c) -> p a c", c=128)),
                    reads=[b_kst[j]], writes=[b_ksp], dma=True, tag=f"ks{kt}_{hc}")


        vst = [pr.sb(ph13, f"vst{j}", [128, 16 * 65], BF16) for j in range(2)]
        b_vst = [sch.buf(f"vst{j}") for j in range(2)]
        for kt in range(32):
            j = kt % 2
            sch.step("vector", lambda e, j=j: e.memset(
                vst[j][:].rearrange("p (h c) -> p h c", c=65)[:, :, 64:65], 1.0),
                writes=[b_vst[j]], tag=f"vo{kt}")
            for half in range(2):
                pa, ba = acc[half], b_acc[half]
                for l in range(2):
                    sch.step("tensor", lambda e, l=l, kt=kt, half=half, pa=pa: e.matmul(
                        pa[:], lat[l][:, kt * 128:(kt + 1) * 128],
                        wvu[:, l * 1024 + half * 512:l * 1024 + (half + 1) * 512],
                        start=(l == 0), stop=(l == 1)),
                        reads=[b_wvu, b_lat[l]], writes=[ba], tag=f"vm{kt}_{half}_{l}")
                sch.step("scalar", lambda e, j=j, half=half, pa=pa: e.activation(
                    out=vst[j][:].rearrange("p (h c) -> p h c", c=65)
                    [:, half * 8:(half + 1) * 8, 0:64],
                    in_=pa[:].rearrange("p (h c) -> p h c", c=64),
                    func=AF.Identity),
                    reads=[ba], writes=[b_vst[j]], tag=f"ve{kt}_{half}")
            sch.step("sync", lambda e, j=j, kt=kt: e.dma_start(
                out=v_sp[kt], in_=vst[j][:]),
                reads=[b_vst[j]], writes=[b_vspb], dma=True, tag=f"vs{kt}")

        qq = [pr.sb(ph13, f"qq{j}", [128, 512], BF16) for j in range(2)]
        b_qq = [sch.buf(f"qq{j}") for j in range(2)]
        for qt in range(4):
            for hc in range(DC):
                j = (qt * DC + hc) % 2
                sch.step("vector", lambda e, j=j, hc=hc, qt=qt: e.tensor_tensor(
                    out=qq[j][:], in0=qsb[hc][:, qt * 512:(qt + 1) * 512],
                    in1=qsb[hc][:, qt * 512:(qt + 1) * 512], op=ALU.mult),
                    reads=[b_qsb[hc]], writes=[b_qq[j]], tag=f"qsq{qt}_{hc}")
                for par in range(2):
                    pn, bn = psn[par], b_psn[par]
                    sch.step("tensor", lambda e, j=j, par=par, pn=pn: e.matmul(
                        pn[:], cx.ones_bf[par * 64:(par + 1) * 64, :],
                        qq[j][par * 64:(par + 1) * 64, :],
                        start=True, stop=True, skip_group_check=True),
                        reads=[b_qq[j], cx.b_ones_bf], writes=[bn], tag=f"qn{qt}_{hc}_{par}")
                for par in range(2):
                    hh = hc * 2 + par
                    pn, bn = psn[par], b_psn[par]
                    sch.step("vector", lambda e, pn=pn: e.reduce_max(
                        nrm[0:1, 2:3], pn[0:1, :], axis=AX.X),
                        reads=[bn], writes=[b_nrm], tag=f"qnr{qt}_{hc}_{par}")
                    sch.step("vector", lambda e, hh=hh: e.tensor_tensor(
                        out=qnm[0:1, hh:hh + 1], in0=qnm[0:1, hh:hh + 1],
                        in1=nrm[0:1, 2:3], op=ALU.max),
                        reads=[b_nrm, b_qnm], writes=[b_qnm], tag=f"qnx{qt}_{hc}_{par}")
        sch.step("vector", lambda e: e.tensor_tensor(
            out=nrm[0:1, 0:16], in0=qnm[0:1, :], in1=knm[0:1, :], op=ALU.mult),
            reads=[b_qnm, b_knm], writes=[b_nrm], tag="u2")
        sch.step("scalar", lambda e: e.activation(
            out=nrm[0:1, 0:16], in_=nrm[0:1, 0:16], func=AF.Sqrt),
            reads=[b_nrm], writes=[b_nrm], tag="usqrt")
        psb = pr.ps(ph13, "psb", [128, 512], F32)
        b_psb = sch.buf("psb")
        sch.step("tensor", lambda e: e.matmul(
            psb[:, 0:16], cx.ones_row[0:1, :],
            nrm[0:1, 0:16], start=True, stop=True, skip_group_check=True),
            reads=[b_nrm, cx.b_ones_row], writes=[b_psb], tag="ubc")
        sch.step("vector", lambda e: e.tensor_scalar(
            out=ustars[:], in0=psb[:, 0:16], scalar1=-1.0, scalar2=None, op0=ALU.mult),
            reads=[b_psb], writes=[b_ustars], tag="uneg")

        if dbg:
            dbg_lat = pr.dout("dbg_lat", [128, S], BF16)
            dbg_q = pr.dout("dbg_q", [128, TOK], BF16)
            dbg_nrm = pr.dout("dbg_nrm", [1, 16])
            dbg_us = pr.dout("dbg_us", [128, 1])
            sch.step("sync", lambda e: e.dma_start(out=dbg_lat[:], in_=lat[0][:]),
                     reads=[b_lat[0]], dma=True, tag="dbg1")
            sch.step("sync", lambda e: e.dma_start(out=dbg_q[:], in_=qsb[0][:]),
                     reads=[b_qsb[0]], dma=True, tag="dbg2")
            sch.step("sync", lambda e: e.dma_start(out=dbg_nrm[:], in_=nrm[:]),
                     reads=[b_nrm], dma=True, tag="dbg3")
            sch.step("sync", lambda e: e.dma_start(out=dbg_us[:], in_=ustars[:, 0:1]),
                     reads=[b_ustars], dma=True, tag="dbg4")
        if fx is not None and fx.get("dbga"):
            dbgu = pr.dout("dbgu", [128, 16])
            dbgk = pr.dout("dbgk", [1, 16])
            dbgq2 = pr.dout("dbgq2", [1, 16])
            dbglt = pr.dout("dbglt", [128, S], BF16)
            sch.step("sync", lambda e: e.dma_start(out=dbgu[:, :], in_=ustars[:]),
                     reads=[b_ustars], dma=True, tag="dbgu")
            sch.step("sync", lambda e: e.dma_start(out=dbgk[:, :], in_=knm[0:1, :]),
                     reads=[b_knm], dma=True, tag="dbgk")
            sch.step("sync", lambda e: e.dma_start(out=dbgq2[:, :], in_=qnm[0:1, :]),
                     reads=[b_qnm], dma=True, tag="dbgq2")
            sch.step("sync", lambda e: e.dma_start(out=dbglt[:, :], in_=lat[0][:]),
                     reads=[b_lat[0]], dma=True, tag="dbglt")

        ph13.close()
        sch.barrier()

        ph4 = ExitStack()
        qkp = pr.ps(ph4, "qkp", [128, 1024], F32)
        qk = [qkp[:, 0:512], qkp[:, 512:1024]]
        b_qk = [sch.buf(f"qk{j}") for j in range(2)]
        pv = [pr.ps(ph4, f"pv{j}", [65, 512], F32) for j in range(4)]
        b_pv = [sch.buf(f"pv{j}") for j in range(4)]
        ptbf = pr.ps(ph4, "ptbf", [128, D], BF16)
        b_ptbf = sch.buf("ptbf")
        spare = pr.ps(ph4, "spare", [128, 512], F32)
        b_spare = sch.buf("spare")

        msk = pr.sb(st, "msk", [128, 8 * 512], BF16)
        b_msk = sch.buf("msk")
        ktl = [pr.sb(st, f"ktl{r}", [128, 256], BF16) for r in range(3)]
        b_ktl = [sch.buf(f"ktl{r}") for r in range(3)]
        vtl = [pr.sb(st, f"vtl{r}", [128, 4 * 65], BF16) for r in range(3)]
        b_vtl = [sch.buf(f"vtl{r}") for r in range(3)]
        pt_t = [[pr.sb(st, f"pt{hp}_{j}", [128, 1024], BF16) for j in range(2)]
                for hp in range(2)]
        b_pt = [[sch.buf(f"pt{hp}_{j}") for j in range(2)] for hp in range(2)]
        upair = pr.sb(st, "upair", [128, 8], F32)
        b_upair = sch.buf("upair")
        sch.step("vector", lambda e: e.tensor_reduce(
            upair[:], ustars[:].rearrange("p (a b) -> p a b", b=2), axis=AX.X,
            op=ALU.min),
            reads=[b_ustars], writes=[b_upair], tag="upair")
        yfm = [pr.sb(st, f"yfm{c}", [128, 512], BF16) for c in range(DC)]
        b_yfm = [sch.buf(f"yfm{c}") for c in range(DC)]
        rsum = pr.sb(st, "rsum", [1, 512], F32)
        b_rsum = sch.buf("rsum")
        ytmp = pr.sb(st, "ytmp", [64, 512], F32)
        b_ytmp = sch.buf("ytmp")
        ao = [pr.sb(st, f"ao{c}", [128, 512], BF16) for c in range(DC)]
        b_ao = [sch.buf(f"ao{c}") for c in range(DC)]
        aot = [pr.sb(st, f"aot{j}", [128, D], BF16) for j in range(2)]
        b_aot = [sch.buf(f"aot{j}") for j in range(2)]
        estat = [pr.sb(st, f"est{j}", [128, 4], F32) for j in range(2)]
        b_estat = [sch.buf(f"est{j}") for j in range(2)]
        essq = [pr.sb(st, f"esq{j}", [128, 1], F32) for j in range(2)]
        b_essq = [sch.buf(f"esq{j}") for j in range(2)]
        escr = pr.sb(st, "escr", [128, D], F32)
        b_escr = sch.buf("escr")
        lnres = [pr.sb(st, f"lnr{j}", [128, D], F32) for j in range(2)]
        b_lnres = [sch.buf(f"lnr{j}") for j in range(2)]
        xrow = [pr.sb(st, f"xr{j}", [128, D], F32) for j in range(2)]
        b_xrow = [sch.buf(f"xr{j}") for j in range(2)]
        xout = [pr.sb(st, f"xo{j}", [128, D], F32) for j in range(2)]
        b_xout = [sch.buf(f"xo{j}") for j in range(2)]

        for s_ in range(NSLOT):
            Ek = EXT[s_] * 4
            q0c = s_ * 512
            sch.step("sync", lambda e, s_=s_: e.dma_start(
                out=msk[:].rearrange("p (m q) -> p m q", q=512),
                in_=mask_d[s_].rearrange("u (j p) q -> p (u j) q", p=128)),
                writes=[b_msk], dma=True, tag=f"mld{s_}")
            for hg in range(4):
                for kt in range(Ek):
                    r = kt % 3
                    sch.step("sync", lambda e, r=r, kt=kt, hg=hg: e.dma_start(
                        out=ktl[r][:].rearrange("p (a c) -> p a c", c=128),
                        in_=k_sp[kt, hg * 2:(hg + 1) * 2].rearrange("a p c -> p a c")),
                        reads=[b_ksp], writes=[b_ktl[r]], dma=True, tag=f"ktl{s_}_{hg}_{kt}")
                    sch.step("sync", lambda e, r=r, kt=kt, hg=hg: e.dma_start(
                        out=vtl[r][:], in_=v_sp[kt, :, hg * 4 * 65:(hg + 1) * 4 * 65]),
                        reads=[b_vspb], writes=[b_vtl[r]], dma=True, tag=f"vtl{s_}_{hg}_{kt}")
                    masked = kt >= Ek - 8
                    mcol = (kt - (Ek - 8)) * 512
                    for hl in range(4):
                        h = hg * 4 + hl
                        jq = hl % 2
                        pqk, bqk = qk[jq], b_qk[jq]
                        if masked:
                            sch.step("tensor", lambda e, pqk=pqk, mcol=mcol: e.matmul(
                                pqk[:], cx.ident_bf[:], msk[:, mcol:mcol + 512],
                                start=True, stop=False, skip_group_check=True),
                                reads=[b_msk, cx.b_ident_bf], writes=[bqk],
                                tag=f"bad{s_}_{hg}_{kt}_{hl}")
                        sch.step("tensor", lambda e, pqk=pqk, h=h, hl=hl, q0c=q0c,
                                 kt=kt, masked=masked: e.matmul(
                            pqk[:], ktl[kt % 3][(h % 2) * 64:(h % 2) * 64 + 64,
                                                (hl // 2) * 128:(hl // 2) * 128 + 128],
                            qsb[h // 2][(h % 2) * 64:(h % 2) * 64 + 64, q0c:q0c + 512],
                            start=(not masked), stop=True, skip_group_check=True),
                            reads=[b_ktl[kt % 3], b_qsb[h // 2]], writes=[bqk],
                            tag=f"qk{s_}_{hg}_{kt}_{hl}")
                        if hl % 2 == 1:
                            hp = hl // 2
                            ptt, bptt = pt_t[hp][kt % 2], b_pt[hp][kt % 2]
                            pairidx = hg * 2 + hp
                            sch.step("scalar", lambda e, ptt=ptt, pairidx=pairidx:
                                     e.activation(
                                out=ptt[:], in_=qkp[:], func=AF.Exp,
                                bias=upair[:, pairidx:pairidx + 1]),
                                reads=[b_qk[0], b_qk[1], b_upair], writes=[bptt],
                                tag=f"ex{s_}_{hg}_{kt}_{hl}")
                            for hh in (hl - 1, hl):
                                ppv, bpv = pv[hh], b_pv[hh]
                                sch.step("tensor", lambda e, ppv=ppv, hh=hh, kt=kt,
                                         ptt=ptt, Ek=Ek: e.matmul(
                                    ppv[:], vtl[kt % 3][:, hh * 65:(hh + 1) * 65],
                                    ptt[:, (hh % 2) * 512:(hh % 2 + 1) * 512],
                                    start=(kt == 0), stop=(kt == Ek - 1),
                                    skip_group_check=True),
                                    reads=[b_vtl[kt % 3], bptt], writes=[bpv],
                                    tag=f"pv{s_}_{hg}_{kt}_{hh}")
                for hl in range(4):
                    h = hg * 4 + hl
                    ppv, bpv = pv[hl], b_pv[hl]
                    sch.step("vector", lambda e, ppv=ppv: e.reciprocal(
                        rsum[0:1, :], ppv[64:65, :]),
                        reads=[bpv], writes=[b_rsum], tag=f"rc{s_}_{h}")
                    sch.step("tensor", lambda e: e.matmul(
                        spare[0:64, :], cx.ones_row[0:1, 0:64],
                        rsum[0:1, :], start=True, stop=True, skip_group_check=True),
                        reads=[b_rsum, cx.b_ones_row], writes=[b_spare], tag=f"bc{s_}_{h}")
                    sch.step("vector", lambda e, ppv=ppv: e.tensor_copy(
                        out=ytmp[:], in_=ppv[0:64, :]),
                        reads=[bpv], writes=[b_ytmp], tag=f"yc{s_}_{h}")
                    hc, hb = h // 2, (h % 2) * 64
                    sch.step("vector", lambda e, hc=hc, hb=hb: e.tensor_tensor(
                        out=yfm[hc][hb:hb + 64, :], in0=ytmp[:], in1=spare[0:64, :],
                        op=ALU.mult),
                        reads=[b_ytmp, b_spare], writes=[b_yfm[hc]], tag=f"yn{s_}_{h}")
            for dm in range(DC):
                j = dm % 2
                pw, bw = qk[j], b_qk[j]
                for hc in range(DC):
                    sch.step("tensor", lambda e, pw=pw, hc=hc, dm=dm: e.matmul(
                        pw[:], wout[:, hc * 1024 + dm * 128:hc * 1024 + (dm + 1) * 128],
                        yfm[hc][:], start=(hc == 0), stop=(hc == DC - 1)),
                        reads=[b_wout, b_yfm[hc]], writes=[bw], tag=f"wo{s_}_{dm}_{hc}")
                sch.step("scalar", lambda e, pw=pw, dm=dm: e.activation(
                    out=ao[dm][:], in_=pw[:], func=AF.Identity),
                    reads=[bw], writes=[b_ao[dm]], tag=f"we{s_}_{dm}")
            for qs in range(4):
                j = qs % 2
                for dm in range(DC):
                    sch.step("tensor", lambda e, dm=dm, qs=qs: e.transpose(
                        ptbf[:, dm * 128:(dm + 1) * 128],
                        ao[dm][:, qs * 128:(qs + 1) * 128], cx.ident_bf[:]),
                        reads=[b_ao[dm], cx.b_ident_bf], writes=[b_ptbf],
                        tag=f"aot{s_}_{qs}_{dm}")
                sch.step("vector", lambda e, j=j: e.tensor_copy(
                    out=aot[j][:], in_=ptbf[:]),
                    reads=[b_ptbf], writes=[b_aot[j]], tag=f"aoe{s_}_{qs}")
                stt, bst = estat[j], b_estat[j]
                sq, bsq = essq[j], b_essq[j]
                mean, var, rstd, nmr = stt[:, 0:1], stt[:, 1:2], stt[:, 2:3], stt[:, 3:4]
                sch.step("scalar", lambda e, j=j, sq=sq: e.activation(
                    out=escr[:], in_=aot[j][:], func=AF.Square, accum_out=sq[:]),
                    reads=[b_aot[j]], writes=[b_escr, bsq], tag=f"esq{s_}_{qs}")
                sch.step("vector", lambda e, j=j, mean=mean: e.tensor_reduce(
                    mean, aot[j][:], axis=AX.X, op=ALU.add),
                    reads=[b_aot[j]], writes=[bst], tag=f"esm{s_}_{qs}")
                sch.step("vector", lambda e, mean=mean: e.tensor_scalar(
                    out=mean, in0=mean, scalar1=1.0 / D, scalar2=None, op0=ALU.mult),
                    reads=[bst], writes=[bst], tag=f"emn{s_}_{qs}")
                sch.step("vector", lambda e, var=var, sq=sq: e.tensor_scalar(
                    out=var, in0=sq[:], scalar1=1.0 / D, scalar2=None, op0=ALU.mult),
                    reads=[bsq, bst], writes=[bst], tag=f"ev1{s_}_{qs}")
                sch.step("vector", lambda e, nmr=nmr, mean=mean: e.tensor_tensor(
                    out=nmr, in0=mean, in1=mean, op=ALU.mult),
                    reads=[bst], writes=[bst], tag=f"em2{s_}_{qs}")
                sch.step("vector", lambda e, var=var, nmr=nmr: e.tensor_tensor(
                    out=var, in0=var, in1=nmr, op=ALU.subtract),
                    reads=[bst], writes=[bst], tag=f"ev2{s_}_{qs}")
                sch.step("vector", lambda e, var=var: e.tensor_scalar(
                    out=var, in0=var, scalar1=EPS, scalar2=None, op0=ALU.add),
                    reads=[bst], writes=[bst], tag=f"ev3{s_}_{qs}")
                sch.step("scalar", lambda e, rstd=rstd, var=var: e.activation(
                    out=rstd, in_=var, func=AF.Ln),
                    reads=[bst], writes=[bst], tag=f"eln{s_}_{qs}")
                sch.step("scalar", lambda e, rstd=rstd: e.activation(
                    out=rstd, in_=rstd, func=AF.Exp, scale=-0.5),
                    reads=[bst], writes=[bst], tag=f"eex{s_}_{qs}")
                sch.step("vector", lambda e, nmr=nmr, mean=mean, rstd=rstd:
                         e.scalar_tensor_tensor(out=nmr, in0=mean, scalar=-1.0,
                                                in1=rstd, op0=ALU.mult, op1=ALU.mult),
                         reads=[bst], writes=[bst], tag=f"enm{s_}_{qs}")
                sch.step("scalar", lambda e, j=j, rstd=rstd, nmr=nmr: e.activation(
                    out=lnres[j][:], in_=aot[j][:], func=AF.Identity,
                    bias=nmr, scale=rstd),
                    reads=[b_aot[j], bst], writes=[b_lnres[j]], tag=f"eap{s_}_{qs}")
                row0 = s_ * 512 + qs * 128
                sch.step("sync", lambda e, j=j, row0=row0: e.dma_start(
                    out=xrow[j][:], in_=x_own_d[row0:row0 + 128, :]),
                    reads=list(XR), writes=[b_xrow[j]], dma=True, tag=f"exl{s_}_{qs}")
                sch.step("vector", lambda e, j=j: e.tensor_tensor(
                    out=xout[j][:], in0=lnres[j][:], in1=xrow[j][:], op=ALU.add),
                    reads=[b_lnres[j], b_xrow[j]], writes=[b_xout[j]], tag=f"eadd{s_}_{qs}")
                sch.step("sync", lambda e, j=j, row0=row0: e.dma_start(
                    out=xo_d[row0:row0 + 128, :], in_=xout[j][:]),
                    reads=[b_xout[j]], writes=list(XW), dma=True, tag=f"exs{s_}_{qs}")
        ph4.close()

        if fx is None:
            sch.emit(block, sems)
    return pr


def stage_a_core_inputs(inputs, core):
    b, p = core // 2, core % 2
    x = np.asarray(inputs["x_full"], np.float32)
    d = {
        "x_own": np.ascontiguousarray(x[b][_own_rows(p)]),
        "x_batch": np.ascontiguousarray(x[b]),
        "WqT": np.ascontiguousarray(inputs["L0_attn_Wq"].reshape(D, H * HD).astype(np.float32)),
        "Wkvd": np.ascontiguousarray(inputs["L0_attn_Wkv_down"].astype(np.float32)),
        "WkuT": np.ascontiguousarray(inputs["L0_attn_Wk_up"].reshape(LD, H * HD).astype(np.float32)),
        "WvuT": np.ascontiguousarray(inputs["L0_attn_Wv_up"].reshape(LD, H * HD).astype(np.float32)),
        "WoutT": np.ascontiguousarray(inputs["L0_attn_Wout"].reshape(H * HD, D).astype(np.float32)),
        "mask": _causal_masks(p, 'l0'),
    }
    d.update(const_inputs())
    return d



class LNKit:

    def __init__(self, pr, st, name, nring=2):
        self.pr, self.name = pr, name
        sch = pr.sch
        self.ssq = [pr.sb(st, f"{name}_q{j}", [128, 1], F32) for j in range(nring)]
        self.bssq = [sch.buf(f"{name}_q{j}") for j in range(nring)]
        self.scr = pr.sb(st, f"{name}_scr", [128, D], F32)
        self.bscr = sch.buf(f"{name}_scr")
        self.stat = [pr.sb(st, f"{name}_st{j}", [128, 4], F32) for j in range(nring)]
        self.bstat = [sch.buf(f"{name}_st{j}") for j in range(nring)]
        self.n = nring

    def emit(self, i, x_tile, bx, out_tile, bout, tagp):
        pr, sch = self.pr, self.pr.sch
        j = i % self.n
        sq, bsq = self.ssq[j], self.bssq[j]
        stt, bst = self.stat[j], self.bstat[j]
        mean, var, rstd, nmr = stt[:, 0:1], stt[:, 1:2], stt[:, 2:3], stt[:, 3:4]
        sch.step("scalar", lambda e: e.activation(
            out=self.scr[:], in_=x_tile[:], func=AF.Square, accum_out=sq[:]),
            reads=[bx], writes=[self.bscr, bsq], tag=f"{tagp}sq")
        sch.step("vector", lambda e: e.tensor_reduce(mean, x_tile[:], axis=AX.X, op=ALU.add),
                 reads=[bx], writes=[bst], tag=f"{tagp}sum")
        sch.step("vector", lambda e: e.tensor_scalar(
            out=mean, in0=mean, scalar1=1.0 / D, scalar2=None, op0=ALU.mult),
            reads=[bst], writes=[bst], tag=f"{tagp}mn")
        sch.step("vector", lambda e: e.tensor_scalar(
            out=var, in0=sq[:], scalar1=1.0 / D, scalar2=None, op0=ALU.mult),
            reads=[bsq, bst], writes=[bst], tag=f"{tagp}v1")
        sch.step("vector", lambda e: e.tensor_tensor(
            out=nmr, in0=mean, in1=mean, op=ALU.mult),
            reads=[bst], writes=[bst], tag=f"{tagp}m2")
        sch.step("vector", lambda e: e.tensor_tensor(
            out=var, in0=var, in1=nmr, op=ALU.subtract),
            reads=[bst], writes=[bst], tag=f"{tagp}v2")
        sch.step("vector", lambda e: e.tensor_scalar(
            out=var, in0=var, scalar1=EPS, scalar2=None, op0=ALU.add),
            reads=[bst], writes=[bst], tag=f"{tagp}v3")
        sch.step("scalar", lambda e: e.activation(out=rstd, in_=var, func=AF.Ln),
                 reads=[bst], writes=[bst], tag=f"{tagp}ln")
        sch.step("scalar", lambda e: e.activation(
            out=rstd, in_=rstd, func=AF.Exp, scale=-0.5),
            reads=[bst], writes=[bst], tag=f"{tagp}ex")
        sch.step("vector", lambda e: e.scalar_tensor_tensor(
            out=nmr, in0=mean, scalar=-1.0, in1=rstd, op0=ALU.mult, op1=ALU.mult),
            reads=[bst], writes=[bst], tag=f"{tagp}nm")
        sch.step("scalar", lambda e: e.activation(
            out=out_tile[:], in_=x_tile[:], func=AF.Identity, bias=nmr, scale=rstd),
            reads=[bx, bst], writes=[bout], tag=f"{tagp}ap")


def emit_absmax_to_scalar(pr, sch, cx, colmax, b_colmax, dst_ap, b_dst, psb, b_psb, tagp):
    sch.step("tensor", lambda e: e.transpose(
        psb[0:1, 0:128], colmax[:], cx.ident_f[:]),
        reads=[b_colmax, cx.b_ident_f], writes=[b_psb], tag=f"{tagp}tr")
    sch.step("vector", lambda e: e.reduce_max(dst_ap, psb[0:1, 0:128], axis=AX.X),
             reads=[b_psb], writes=[b_dst], tag=f"{tagp}rm")


def build_stage_b(upto=99, b3parts='all', fx=None):
    if fx is None:
        pr = Prog()
        st = pr.stack
    else:
        pr = fx["pr"]
        st = fx["st"]
    sch = pr.sch
    nc = pr.nc
    XR = [fx["x_in_buf"]] if fx else []
    XW = [fx["x_out_buf"]] if fx else []

    x_own_d = fx["x_in"] if fx else pr.din("x_own", [TOK, D])
    wg_d = pr.din("w_gate", [FFI, D])
    wu_d = pr.din("w_up", [FFI, D])
    wd_d = pr.din("w_down", [D, FFI])
    xo_d = fx["x_out"] if fx else pr.dout("x_out", [TOK, D])
    ln_dr = pr.dram("ln_dr", [TOK, D])
    up_dr = pr.dram("up_dr", [FFI, TOK], BF16)
    gt_dr = pr.dram("gt_dr", [FFI, TOK], BF16)
    o_dr = pr.dram("o_dr", [FFI, TOK], BF16)
    cc_in = pr.dram("cc_in", [1, 16])
    cc_out = pr.dram("cc_out", [1, 16])
    b_lndr = sch.buf("ln_dr", kind="d"); b_updr = sch.buf("up_dr", kind="d")
    b_gtdr = sch.buf("gt_dr", kind="d"); b_odr = sch.buf("o_dr", kind="d")
    b_ccin = sch.buf("cc_in", kind="d"); b_ccout = sch.buf("cc_out", kind="d")

    IC = FFI // 128
    if fx is None:
        sems = {e: pr.stack.enter_context(nc.semaphore("sem_" + e)) for e in ENGS}
        blk_cm = nc.Block()
    else:
        sems = fx["sems"]
        blk_cm = _UseBlock(fx["block"])

    with blk_cm as block:
        cx = fx["cx"] if fx else emit_consts(pr, st)
        mx = pr.sb(st, "mx", [128, 10], F32)
        b_mx = sch.buf("mx")
        sc_r = pr.sb(st, "sc_r", [1, 16], F32)
        b_scr_ = sch.buf("sc_r")
        scb = pr.sb(st, "scb", [128, 8], F32)
        b_scb = sch.buf("scb")
        ccs = pr.sb(st, "ccs", [1, 16], F32)
        b_ccs = sch.buf("ccs")
        wdT = [pr.sb(st, f"wdT{r}", [128, D], BF16) for r in range(IC)]
        b_wdT = [sch.buf(f"wdT{r}") for r in range(IC)]
        lnk = LNKit(pr, st, "blk")
        sch.step("vector", lambda e: e.memset(mx[:], 0.0), writes=[b_mx], tag="mx0")
        sch.step("vector", lambda e: e.memset(ccs[:], 0.0), writes=[b_ccs], tag="ccs0")
        xb = [pr.sb(st, f"bx{j}", [128, D], F32) for j in range(2)]
        bxb = [sch.buf(f"bx{j}") for j in range(2)]
        lnf = [pr.sb(st, f"blnf{j}", [128, D], F32) for j in range(2)]
        b_lnf = [sch.buf(f"blnf{j}") for j in range(2)]
        xi32 = [pr.sb(st, f"xi{j}", [128, 512], I32) for j in range(2)]
        b_xi32 = [sch.buf(f"xi{j}") for j in range(2)]
        xhat = [pr.sb(st, f"xh{j}", [128, D], BF16) for j in range(2)]
        b_xhat = [sch.buf(f"xh{j}") for j in range(2)]
        xfm = [pr.sb(st, f"xfm{c}", [128, 512], BF16) for c in range(DC)]
        b_xfm = sch.buf("xfm")
        ugst = [pr.sb(st, f"ugst{j}", [128, 512], BF16) for j in range(2)]
        b_ugst = [sch.buf(f"ugst{j}") for j in range(2)]
        tbf = [pr.sb(st, f"tbf{j}", [128, 512], BF16) for j in range(2)]
        b_tbf = [sch.buf(f"tbf{j}") for j in range(2)]
        ti32 = [pr.sb(st, f"ti32{j}", [128, 512], I32) for j in range(2)]
        b_ti32 = [sch.buf(f"ti32{j}") for j in range(2)]
        tf32 = [pr.sb(st, f"tf32{j}", [128, 512], F32) for j in range(2)]
        b_tf32 = [sch.buf(f"tf32{j}") for j in range(2)]

        pst = pr.ps(st, "pst", [128, D], BF16)
        b_pst = sch.buf("pst")
        ps2 = [pr.ps(st, f"ps2_{j}", [128, 512], F32) for j in range(2)]
        b_ps2 = [sch.buf(f"ps2_{j}") for j in range(2)]

        def make_dq(stk, width, pfx, nring=2):
            wnat = [pr.sb(stk, f"{pfx}nat{j}", [128, width], F32) for j in range(nring)]
            wi32 = [pr.sb(stk, f"{pfx}i32{j}", [128, width], I32) for j in range(nring)]
            wbf = [pr.sb(stk, f"{pfx}bf{j}", [128, width], BF16) for j in range(nring)]
            wsc = [pr.sb(stk, f"{pfx}sc{j}", [128, width // 32], F32) for j in range(nring)]
            wri = [pr.sb(stk, f"{pfx}ri{j}", [128, width // 32], F32) for j in range(nring)]
            bufs = {k: [sch.buf(f"{pfx}{k}{j}") for j in range(nring)]
                    for k in ("nat", "i32", "bf", "sc", "ri")}
            def dequant_tile(idx, src_ap, tagp):
                j = idx % nring
                nb = width // 32
                wn, bwn = wnat[j], bufs["nat"][j]
                wi, bwi = wi32[j], bufs["i32"][j]
                wb, bwb = wbf[j], bufs["bf"][j]
                ws, bws = wsc[j], bufs["sc"][j]
                wr, bwr = wri[j], bufs["ri"][j]
                sch.step("sync", lambda e: e.dma_start(out=wn[:], in_=src_ap),
                         writes=[bwn], dma=True, tag=f"{tagp}ld")
                w3 = wn[:].rearrange("p (b k) -> p b k", k=32)
                sch.step("vector", lambda e: e.reduce_max(
                    ws[:], w3, axis=AX.X, apply_absolute_value=True),
                    reads=[bwn], writes=[bws], tag=f"{tagp}am")
                sch.step("vector", lambda e: e.tensor_scalar(
                    out=ws[:], in0=ws[:], scalar1=1.0 / 7.0, scalar2=1e-12,
                    op0=ALU.mult, op1=ALU.max),
                    reads=[bws], writes=[bws], tag=f"{tagp}sc")
                sch.step("vector", lambda e: e.reciprocal(wr[:], ws[:]),
                         reads=[bws], writes=[bwr], tag=f"{tagp}rc")
                rb = wr[:].to_broadcast([128, nb, 32])
                sch.step("vector", lambda e: e.tensor_tensor(
                    out=wi[:].rearrange("p (b k) -> p b k", k=32),
                    in0=w3, in1=rb, op=ALU.mult),
                    reads=[bwn, bwr], writes=[bwi], tag=f"{tagp}dv")
                sb_ = ws[:].to_broadcast([128, nb, 32])
                sch.step("vector", lambda e: e.tensor_tensor(
                    out=wb[:].rearrange("p (b k) -> p b k", k=32),
                    in0=wi[:].rearrange("p (b k) -> p b k", k=32),
                    in1=sb_, op=ALU.mult),
                    reads=[bwi, bws], writes=[bwb], tag=f"{tagp}ml")
                return wb, bwb
            return dequant_tile

        ph0d = ExitStack()
        _B = upto
        dqd = make_dq(ph0d, FFI, "d")
        for r in range(DC):
            wb, bwb = dqd(r, wd_d[r * 128:(r + 1) * 128, :], f"dqd{r}_")
            for c in range(IC):
                sch.step("tensor", lambda e, c=c, wb=wb, bwb=bwb: e.transpose(
                    pst[:, (c % 8) * 128:((c % 8) + 1) * 128],
                    wb[:, c * 128:(c + 1) * 128], cx.ident_bf[:]),
                    reads=[bwb, cx.b_ident_bf], writes=[b_pst], tag=f"tqd{r}_{c}")
                sch.step("vector", lambda e, c=c, r=r: e.tensor_copy(
                    out=wdT[c][:, r * 128:(r + 1) * 128],
                    in_=pst[:, (c % 8) * 128:((c % 8) + 1) * 128]),
                    reads=[b_pst], writes=[b_wdT[c]], tag=f"ted{r}_{c}")
        ph0d.close()

        if _B < 1:
            sch.step("sync", lambda e: e.dma_start(out=xo_d[0:128, :], in_=wdT[0][:]),
                     reads=[b_wdT[0]], dma=True, tag="early_out") if False else None
            sch.step("scalar", lambda e: e.activation(out=lnf[0][:], in_=lnf[0][:], func=AF.Identity) , writes=[b_lnf[0]], tag="noop")
            sch.step("sync", lambda e: e.dma_start(out=xo_d[0:128, :], in_=lnf[0][:]),
                     reads=[b_lnf[0]], dma=True, tag="early_o2")
            sch.emit(block, sems)
            return pr
        ph0 = ExitStack()
        wuT = [pr.sb(ph0, f"wuT{c}", [128, FFI], BF16) for c in range(DC)]
        b_wuT = [sch.buf(f"wuT{c}") for c in range(DC)]
        wgT = [pr.sb(ph0, f"wgT{c}", [128, FFI], BF16) for c in range(DC)]
        b_wgT = [sch.buf(f"wgT{c}") for c in range(DC)]
        dqu = make_dq(ph0, D, "u", nring=1)
        for wsel, (wdr, wT, bwT) in enumerate(
                ((wu_d, wuT, b_wuT), (wg_d, wgT, b_wgT))):
            for r in range(IC):
                wb, bwb = dqu(wsel * IC + r, wdr[r * 128:(r + 1) * 128, :],
                              f"dq{wsel}_{r}_")
                for c in range(DC):
                    sch.step("tensor", lambda e, c=c, wb=wb, bwb=bwb: e.transpose(
                        pst[:, c * 128:(c + 1) * 128], wb[:, c * 128:(c + 1) * 128],
                        cx.ident_bf[:]),
                        reads=[bwb, cx.b_ident_bf], writes=[b_pst], tag=f"tq{wsel}_{r}_{c}")
                for c in range(DC):
                    sch.step("vector", lambda e, c=c, r=r, wT=wT, bwT=bwT: e.tensor_copy(
                        out=wT[c][:, r * 128:(r + 1) * 128],
                        in_=pst[:, c * 128:(c + 1) * 128]),
                        reads=[b_pst], writes=[bwT[c]], tag=f"te{wsel}_{r}_{c}")

        if _B < 2:
            sch.step("sync", lambda e: e.dma_start(out=xo_d[0:128, 0:FFI // 2], in_=wuT[0][:, 0:FFI // 2]),
                     reads=[b_wuT[0]], dma=True, tag="early_o2")
            sch.emit(block, sems)
            ph0.close()
            return pr
        for i in range(16):
            j = i % 2
            sch.step("sync", lambda e, i=i, j=j: e.dma_start(
                out=xb[j][:], in_=x_own_d[i * 128:(i + 1) * 128, :]),
                reads=list(XR), writes=[bxb[j]], dma=True, tag=f"b1ld{i}")
            lnk.emit(i, xb[j], bxb[j], lnf[j], b_lnf[j], f"b1_{i}")
            sch.step("vector", lambda e, j=j: e.reduce_max(
                mx[:, 1:2], lnf[j][:], axis=AX.X, apply_absolute_value=True),
                reads=[b_lnf[j]], writes=[b_mx], tag=f"b1rm{i}")
            sch.step("vector", lambda e: e.tensor_tensor(
                out=mx[:, 0:1], in0=mx[:, 0:1], in1=mx[:, 1:2], op=ALU.max),
                reads=[b_mx], writes=[b_mx], tag=f"b1mx{i}")
            sch.step("sync", lambda e, i=i, j=j: e.dma_start(
                out=ln_dr[i * 128:(i + 1) * 128, :], in_=lnf[j][:]),
                reads=[b_lnf[j]], writes=[b_lndr], dma=True, tag=f"b1sp{i}")

        if _B < 3:
            sch.step("sync", lambda e: e.dma_start(out=xo_d[0:128, :], in_=lnf[1][:]),
                     reads=[b_lnf[1]], dma=True, tag="early_o3")
            sch.emit(block, sems)
            ph0.close()
            return pr
        psb = ps2[0]
        b_psb = b_ps2[0]
        emit_absmax_to_scalar(pr, sch, cx, mx[:, 0:1], b_mx, sc_r[0:1, 0:1], b_scr_,
                              psb, b_psb, "b1s")
        sch.step("vector", lambda e: e.tensor_copy(out=ccs[0:1, 0:1], in_=sc_r[0:1, 0:1]),
                 reads=[b_scr_], writes=[b_ccs], tag="cc1c")
        sch.step("gpsimd", lambda e: e.dma_start(out=cc_in[0:1, :], in_=ccs[0:1, :]),
                 reads=[b_ccs], writes=[b_ccin], dma=True, tag="cc1i")
        sch.step("gpsimd", lambda e: e.collective_compute(
            "AllReduce", ALU.max, replica_groups=[list(range(8))],
            ins=[cc_in.ap().opt()], outs=[cc_out.ap().opt()]),
            reads=[b_ccin], writes=[b_ccout], collective=True, tag="cc1",
            gsem="cc")
        sch.step("gpsimd", lambda e: e.dma_start(out=ccs[0:1, 1:2], in_=cc_out[0:1, 0:1]),
            reads=[b_ccout], writes=[b_ccs], dma=True, tag="cc1o")
        sch.step("vector", lambda e: e.tensor_scalar(
            out=sc_r[0:1, 2:3], in0=ccs[0:1, 1:2], scalar1=1.0 / 32767.0,
            scalar2=1e-12, op0=ALU.mult, op1=ALU.max),
            reads=[b_ccs], writes=[b_scr_], tag="sx")
        sch.step("vector", lambda e: e.reciprocal(sc_r[0:1, 3:4], sc_r[0:1, 2:3]),
                 reads=[b_scr_], writes=[b_scr_], tag="rsx")
        sch.step("tensor", lambda e: e.matmul(
            psb[:, 0:2], cx.ones_row[0:1, :], sc_r[0:1, 2:4],
            start=True, stop=True, skip_group_check=True),
            reads=[b_scr_, cx.b_ones_row], writes=[b_psb], tag="sxb")
        sch.step("vector", lambda e: e.tensor_copy(out=scb[:, 0:2], in_=psb[:, 0:2]),
                 reads=[b_psb], writes=[b_scb], tag="sxe")

        if _B < 4:
            sch.step("sync", lambda e: e.dma_start(out=xo_d[0:128, 0:8], in_=scb[:, 0:8]),
                     reads=[b_scb], dma=True, tag="early_o4")
            sch.emit(block, sems)
            ph0.close()
            return pr
        for tt in range(4):
            for u in range(4):
                i = tt * 4 + u
                j = i % 2
                sch.step("sync", lambda e, i=i, j=j: e.dma_start(
                    out=xb[j][:], in_=ln_dr[i * 128:(i + 1) * 128, :]),
                    reads=[b_lndr], writes=[bxb[j]], dma=True, tag=f"b3ld{i}")
                for hf in range(2):
                    sch.step("vector", lambda e, j=j, hf=hf: e.tensor_scalar(
                        out=xi32[j][:], in0=xb[j][:, hf * 512:(hf + 1) * 512],
                        scalar1=scb[:, 1:2], scalar2=None, op0=ALU.mult),
                        reads=[bxb[j], b_scb], writes=[b_xi32[j]], tag=f"b3q{i}_{hf}")
                    sch.step("vector", lambda e, j=j, hf=hf: e.tensor_scalar(
                        out=xhat[j][:, hf * 512:(hf + 1) * 512], in0=xi32[j][:],
                        scalar1=scb[:, 0:1], scalar2=None, op0=ALU.mult),
                        reads=[b_xi32[j], b_scb], writes=[b_xhat[j]], tag=f"b3s{i}_{hf}")
                for c in range(DC):
                    sch.step("tensor", lambda e, c=c, j=j: e.transpose(
                        pst[:, c * 128:(c + 1) * 128], xhat[j][:, c * 128:(c + 1) * 128],
                        cx.ident_bf[:]),
                        reads=[b_xhat[j], cx.b_ident_bf], writes=[b_pst], tag=f"b3t{i}_{c}")
                for c in range(DC):
                    sch.step("vector", lambda e, c=c, u=u: e.tensor_copy(
                        out=xfm[c][:, u * 128:(u + 1) * 128],
                        in_=pst[:, c * 128:(c + 1) * 128]),
                        reads=[b_pst], writes=[b_xfm], tag=f"b3e{i}_{c}")
            for ic in range(IC if b3parts != 'x' else 0):
                for wsel, (wT, bwT, dr, bdr, mcol) in enumerate(
                        ((wuT, b_wuT, up_dr, b_updr, 2), (wgT, b_wgT, gt_dr, b_gtdr, 4))):
                    j2 = (ic * 2 + wsel) % 2
                    pa, ba = ps2[j2], b_ps2[j2]
                    for c in range(DC):
                        sch.step("tensor", lambda e, c=c, ic=ic, wT=wT, pa=pa: e.matmul(
                            pa[:], wT[c][:, ic * 128:(ic + 1) * 128], xfm[c][:],
                            start=(c == 0), stop=(c == DC - 1), skip_group_check=True),
                            reads=[bwT[c], b_xfm], writes=[ba], tag=f"b3m{tt}_{ic}_{wsel}_{c}")
                    stg, bstg = ugst[wsel], b_ugst[wsel]
                    sch.step("scalar", lambda e, pa=pa, stg=stg: e.activation(
                        out=stg[:], in_=pa[:], func=AF.Identity),
                        reads=[ba], writes=[bstg], tag=f"b3v{tt}_{ic}_{wsel}")
                    if b3parts in ('xmr', 'all'):
                        sch.step("vector", lambda e, stg=stg, mcol=mcol: e.reduce_max(
                            mx[:, mcol + 1:mcol + 2], stg[:], axis=AX.X,
                            apply_absolute_value=True),
                            reads=[bstg], writes=[b_mx], tag=f"b3r{tt}_{ic}_{wsel}")
                        sch.step("vector", lambda e, mcol=mcol: e.tensor_tensor(
                            out=mx[:, mcol:mcol + 1], in0=mx[:, mcol:mcol + 1],
                            in1=mx[:, mcol + 1:mcol + 2], op=ALU.max),
                            reads=[b_mx], writes=[b_mx], tag=f"b3x{tt}_{ic}_{wsel}")
                    if b3parts in ('xms', 'all'):
                        sch.step("sync", lambda e, stg=stg, ic=ic, tt=tt, dr=dr: e.dma_start(
                            out=dr[ic * 128:(ic + 1) * 128, tt * 512:(tt + 1) * 512],
                            in_=stg[:]),
                            reads=[bstg], writes=[bdr], dma=True, tag=f"b3s{tt}_{ic}_{wsel}")
        ph0.close()

        if _B < 5:
            sch.step("sync", lambda e: e.dma_start(out=xo_d[0:128, 0:8], in_=mx[:, 0:8]),
                     reads=[b_mx], dma=True, tag="early_o5")
            sch.emit(block, sems)
            return pr
        emit_absmax_to_scalar(pr, sch, cx, mx[:, 2:3], b_mx, sc_r[0:1, 4:5], b_scr_,
                              psb, b_psb, "b3su")
        emit_absmax_to_scalar(pr, sch, cx, mx[:, 4:5], b_mx, sc_r[0:1, 5:6], b_scr_,
                              psb, b_psb, "b3sg")
        sch.step("vector", lambda e: e.tensor_copy(out=ccs[0:1, 0:2], in_=sc_r[0:1, 4:6]),
                 reads=[b_scr_], writes=[b_ccs], tag="cc2c")
        sch.step("gpsimd", lambda e: e.dma_start(out=cc_in[0:1, :], in_=ccs[0:1, :]),
                 reads=[b_ccs], writes=[b_ccin], dma=True, tag="cc2i")
        sch.step("gpsimd", lambda e: e.collective_compute(
            "AllReduce", ALU.max, replica_groups=[list(range(8))],
            ins=[cc_in.ap().opt()], outs=[cc_out.ap().opt()]),
            reads=[b_ccin], writes=[b_ccout], collective=True, tag="cc2",
            gsem="cc")
        sch.step("gpsimd", lambda e: e.dma_start(out=ccs[0:1, 2:4], in_=cc_out[0:1, 0:2]),
                 reads=[b_ccout], writes=[b_ccs], dma=True, tag="cc2o")
        sch.step("vector", lambda e: e.tensor_scalar(
            out=sc_r[0:1, 6:8], in0=ccs[0:1, 2:4], scalar1=1.0 / 32767.0,
            scalar2=1e-12, op0=ALU.mult, op1=ALU.max),
            reads=[b_ccs], writes=[b_scr_], tag="sug")
        sch.step("vector", lambda e: e.reciprocal(sc_r[0:1, 8:10], sc_r[0:1, 6:8]),
                 reads=[b_scr_], writes=[b_scr_], tag="rsug")
        sch.step("tensor", lambda e: e.matmul(
            psb[:, 0:4], cx.ones_row[0:1, :], sc_r[0:1, 6:10],
            start=True, stop=True, skip_group_check=True),
            reads=[b_scr_, cx.b_ones_row], writes=[b_psb], tag="sugb")
        sch.step("vector", lambda e: e.tensor_copy(out=scb[:, 2:6], in_=psb[:, 0:4]),
                 reads=[b_psb], writes=[b_scb], tag="suge")

        if _B < 6:
            sch.step("sync", lambda e: e.dma_start(out=xo_d[0:128, 0:8], in_=scb[:, 0:8]),
                     reads=[b_scb], dma=True, tag="early_o6")
            sch.emit(block, sems)
            return pr
        ph4 = ExitStack()
        g2sb = [pr.sb(ph4, f"g2_{ic}", [128, TOK], BF16) for ic in range(IC)]
        b_g2sb = [sch.buf(f"g2_{ic}") for ic in range(IC)]
        for ic in range(IC):
            for tt in range(4):
                i = ic * 4 + tt
                j = i % 2
                sch.step("sync", lambda e, ic=ic, tt=tt, j=j: e.dma_start(
                    out=tbf[j][:], in_=gt_dr[ic * 128:(ic + 1) * 128,
                                            tt * 512:(tt + 1) * 512]),
                    reads=[b_gtdr], writes=[b_tbf[j]], dma=True, tag=f"b4ld{i}")
                sch.step("vector", lambda e, j=j: e.tensor_scalar(
                    out=ti32[j][:], in0=tbf[j][:], scalar1=scb[:, 5:6], scalar2=None,
                    op0=ALU.mult),
                    reads=[b_tbf[j], b_scb], writes=[b_ti32[j]], tag=f"b4q{i}")
                sch.step("vector", lambda e, j=j: e.tensor_scalar(
                    out=tf32[0][:], in0=ti32[j][:], scalar1=scb[:, 3:4], scalar2=None,
                    op0=ALU.mult),
                    reads=[b_ti32[j], b_scb], writes=[b_tf32[0]], tag=f"b4g{i}")
                sch.step("scalar", lambda e, j=j: e.activation(
                    out=tf32[1][:], in_=tf32[0][:], func=AF.Sigmoid),
                    reads=[b_tf32[0]], writes=[b_tf32[1]], tag=f"b4sg{i}")
                sch.step("vector", lambda e, j=j: e.tensor_scalar(
                    out=ti32[j][:], in0=tf32[1][:], scalar1=65536.0, scalar2=None,
                    op0=ALU.mult),
                    reads=[b_tf32[1]], writes=[b_ti32[j]], tag=f"b4si{i}")
                sch.step("vector", lambda e, j=j: e.tensor_scalar(
                    out=tf32[1][:], in0=ti32[j][:], scalar1=1.0 / 65536.0,
                    scalar2=65535.0 / 65536.0, op0=ALU.mult, op1=ALU.min),
                    reads=[b_ti32[j]], writes=[b_tf32[1]], tag=f"b4sq{i}")
                sch.step("vector", lambda e, j=j, ic=ic, tt=tt: e.tensor_tensor(
                    out=g2sb[ic][:, tt * 512:(tt + 1) * 512], in0=tf32[0][:],
                    in1=tf32[1][:], op=ALU.mult),
                    reads=[b_tf32[0], b_tf32[1]], writes=[b_g2sb[ic]], tag=f"b4g2{i}")
                sch.step("vector", lambda e, ic=ic, tt=tt: e.reduce_max(
                    mx[:, 7:8], g2sb[ic][:, tt * 512:(tt + 1) * 512], axis=AX.X,
                    apply_absolute_value=True),
                    reads=[b_g2sb[ic]], writes=[b_mx], tag=f"b4rm{i}")
                sch.step("vector", lambda e: e.tensor_tensor(
                    out=mx[:, 6:7], in0=mx[:, 6:7], in1=mx[:, 7:8], op=ALU.max),
                    reads=[b_mx], writes=[b_mx], tag=f"b4mx{i}")

        if _B < 7:
            sch.step("sync", lambda e: e.dma_start(out=xo_d[0:128, 0:8], in_=mx[:, 0:8]),
                     reads=[b_mx], dma=True, tag="early_o7")
            sch.emit(block, sems)
            ph4.close()
            return pr
        emit_absmax_to_scalar(pr, sch, cx, mx[:, 6:7], b_mx, sc_r[0:1, 10:11], b_scr_,
                              psb, b_psb, "b4s")
        sch.step("vector", lambda e: e.tensor_copy(out=ccs[0:1, 0:1], in_=sc_r[0:1, 10:11]),
                 reads=[b_scr_], writes=[b_ccs], tag="cc3c")
        sch.step("gpsimd", lambda e: e.dma_start(out=cc_in[0:1, :], in_=ccs[0:1, :]),
                 reads=[b_ccs], writes=[b_ccin], dma=True, tag="cc3i")
        sch.step("gpsimd", lambda e: e.collective_compute(
            "AllReduce", ALU.max, replica_groups=[list(range(8))],
            ins=[cc_in.ap().opt()], outs=[cc_out.ap().opt()]),
            reads=[b_ccin], writes=[b_ccout], collective=True, tag="cc3",
            gsem="cc")
        sch.step("gpsimd", lambda e: e.dma_start(out=ccs[0:1, 2:3], in_=cc_out[0:1, 0:1]),
                 reads=[b_ccout], writes=[b_ccs], dma=True, tag="cc3o")
        sch.step("vector", lambda e: e.tensor_scalar(
            out=sc_r[0:1, 11:12], in0=ccs[0:1, 2:3], scalar1=1.0 / 32767.0,
            scalar2=1e-12, op0=ALU.mult, op1=ALU.max),
            reads=[b_ccs], writes=[b_scr_], tag="sg2")
        sch.step("vector", lambda e: e.reciprocal(sc_r[0:1, 12:13], sc_r[0:1, 11:12]),
                 reads=[b_scr_], writes=[b_scr_], tag="rsg2")
        sch.step("tensor", lambda e: e.matmul(
            psb[:, 0:2], cx.ones_row[0:1, :], sc_r[0:1, 11:13],
            start=True, stop=True, skip_group_check=True),
            reads=[b_scr_, cx.b_ones_row], writes=[b_psb], tag="sg2b")
        sch.step("vector", lambda e: e.tensor_copy(out=scb[:, 6:8], in_=psb[:, 0:2]),
                 reads=[b_psb], writes=[b_scb], tag="sg2e")

        if _B < 8:
            sch.step("sync", lambda e: e.dma_start(out=xo_d[0:128, 0:8], in_=scb[:, 0:8]),
                     reads=[b_scb], dma=True, tag="early_o8")
            sch.emit(block, sems)
            ph4.close()
            return pr
        for ic in range(IC):
            for tt in range(4):
                i = ic * 4 + tt
                j = i % 2
                sch.step("vector", lambda e, j=j, ic=ic, tt=tt: e.tensor_scalar(
                    out=ti32[j][:], in0=g2sb[ic][:, tt * 512:(tt + 1) * 512],
                    scalar1=scb[:, 7:8], scalar2=None, op0=ALU.mult),
                    reads=[b_g2sb[ic], b_scb], writes=[b_ti32[j]], tag=f"b5q{i}")
                sch.step("vector", lambda e, j=j: e.tensor_scalar(
                    out=tf32[0][:], in0=ti32[j][:], scalar1=scb[:, 6:7], scalar2=None,
                    op0=ALU.mult),
                    reads=[b_ti32[j], b_scb], writes=[b_tf32[0]], tag=f"b5g{i}")
                sch.step("sync", lambda e, ic=ic, tt=tt, j=j: e.dma_start(
                    out=tbf[j][:], in_=up_dr[ic * 128:(ic + 1) * 128,
                                            tt * 512:(tt + 1) * 512]),
                    reads=[b_updr], writes=[b_tbf[j]], dma=True, tag=f"b5ld{i}")
                sch.step("vector", lambda e, j=j: e.tensor_scalar(
                    out=ti32[j][:], in0=tbf[j][:], scalar1=scb[:, 4:5], scalar2=None,
                    op0=ALU.mult),
                    reads=[b_tbf[j], b_scb, b_tf32[0]], writes=[b_ti32[j]], tag=f"b5uq{i}")
                sch.step("vector", lambda e, j=j: e.tensor_scalar(
                    out=tf32[1][:], in0=ti32[j][:], scalar1=scb[:, 2:3], scalar2=None,
                    op0=ALU.mult),
                    reads=[b_ti32[j], b_scb], writes=[b_tf32[1]], tag=f"b5us{i}")
                sch.step("vector", lambda e, j=j: e.tensor_tensor(
                    out=tbf[j][:], in0=tf32[0][:], in1=tf32[1][:], op=ALU.mult),
                    reads=[b_tf32[0], b_tf32[1]], writes=[b_tbf[j]], tag=f"b5o{i}")
                sch.step("vector", lambda e, j=j: e.reduce_max(
                    mx[:, 9:10], tbf[j][:], axis=AX.X, apply_absolute_value=True),
                    reads=[b_tbf[j]], writes=[b_mx], tag=f"b5rm{i}")
                sch.step("vector", lambda e: e.tensor_tensor(
                    out=mx[:, 8:9], in0=mx[:, 8:9], in1=mx[:, 9:10], op=ALU.max),
                    reads=[b_mx], writes=[b_mx], tag=f"b5mx{i}")
                sch.step("sync", lambda e, j=j, ic=ic, tt=tt: e.dma_start(
                    out=o_dr[ic * 128:(ic + 1) * 128, tt * 512:(tt + 1) * 512],
                    in_=tbf[j][:]),
                    reads=[b_tbf[j]], writes=[b_odr], dma=True, tag=f"b5sp{i}")
        ph4.close()

        if _B < 9:
            sch.step("sync", lambda e: e.dma_start(out=xo_d[0:128, 0:8], in_=mx[:, 0:8]),
                     reads=[b_mx], dma=True, tag="early_o9")
            sch.emit(block, sems)
            return pr
        emit_absmax_to_scalar(pr, sch, cx, mx[:, 8:9], b_mx, sc_r[0:1, 13:14], b_scr_,
                              psb, b_psb, "b5s")
        sch.step("vector", lambda e: e.tensor_copy(out=ccs[0:1, 0:1], in_=sc_r[0:1, 13:14]),
                 reads=[b_scr_], writes=[b_ccs], tag="cc4c")
        sch.step("gpsimd", lambda e: e.dma_start(out=cc_in[0:1, :], in_=ccs[0:1, :]),
                 reads=[b_ccs], writes=[b_ccin], dma=True, tag="cc4i")
        sch.step("gpsimd", lambda e: e.collective_compute(
            "AllReduce", ALU.max, replica_groups=[list(range(8))],
            ins=[cc_in.ap().opt()], outs=[cc_out.ap().opt()]),
            reads=[b_ccin], writes=[b_ccout], collective=True, tag="cc4",
            gsem="cc")
        sch.step("gpsimd", lambda e: e.dma_start(out=ccs[0:1, 2:3], in_=cc_out[0:1, 0:1]),
                 reads=[b_ccout], writes=[b_ccs], dma=True, tag="cc4o")
        sch.step("vector", lambda e: e.tensor_scalar(
            out=sc_r[0:1, 14:15], in0=ccs[0:1, 2:3], scalar1=1.0 / 32767.0,
            scalar2=1e-12, op0=ALU.mult, op1=ALU.max),
            reads=[b_ccs], writes=[b_scr_], tag="so")
        sch.step("vector", lambda e: e.reciprocal(sc_r[0:1, 15:16], sc_r[0:1, 14:15]),
                 reads=[b_scr_], writes=[b_scr_], tag="rso")
        sch.step("tensor", lambda e: e.matmul(
            psb[:, 0:2], cx.ones_row[0:1, :], sc_r[0:1, 14:16],
            start=True, stop=True, skip_group_check=True),
            reads=[b_scr_, cx.b_ones_row], writes=[b_psb], tag="sob")
        sch.step("vector", lambda e: e.tensor_copy(out=scb[:, 8:10] if False else scb[:, 6:8],
                                                   in_=psb[:, 0:2]),
                 reads=[b_psb], writes=[b_scb], tag="soe")

        ph6 = ExitStack()
        oq = [pr.sb(ph6, f"oq{ic}", [128, 512], BF16) for ic in range(IC)]
        b_oq = [sch.buf(f"oq{ic}") for ic in range(IC)]
        aod = [pr.sb(ph6, f"aod{c}", [128, 512], BF16) for c in range(DC)]
        b_aod = [sch.buf(f"aod{c}") for c in range(DC)]
        aotb = [pr.sb(ph6, f"aotb{j}", [128, D], BF16) for j in range(2)]
        b_aotb = [sch.buf(f"aotb{j}") for j in range(2)]
        lnob = [pr.sb(ph6, f"lnob{j}", [128, D], F32) for j in range(2)]
        b_lnob = [sch.buf(f"lnob{j}") for j in range(2)]
        xrw = [pr.sb(ph6, f"xrw{j}", [128, D], F32) for j in range(2)]
        b_xrw = [sch.buf(f"xrw{j}") for j in range(2)]
        xou = [pr.sb(ph6, f"xou{j}", [128, D], F32) for j in range(2)]
        b_xou = [sch.buf(f"xou{j}") for j in range(2)]
        for tt in range(4):
            for ic in range(IC):
                i = tt * IC + ic
                j = i % 2
                sch.step("sync", lambda e, ic=ic, tt=tt, j=j: e.dma_start(
                    out=tbf[j][:], in_=o_dr[ic * 128:(ic + 1) * 128,
                                            tt * 512:(tt + 1) * 512]),
                    reads=[b_odr], writes=[b_tbf[j]], dma=True, tag=f"b6ld{i}")
                sch.step("vector", lambda e, j=j: e.tensor_scalar(
                    out=ti32[j][:], in0=tbf[j][:], scalar1=scb[:, 7:8], scalar2=None,
                    op0=ALU.mult),
                    reads=[b_tbf[j], b_scb], writes=[b_ti32[j]], tag=f"b6q{i}")
                sch.step("vector", lambda e, j=j, ic=ic: e.tensor_scalar(
                    out=oq[ic][:], in0=ti32[j][:], scalar1=scb[:, 6:7], scalar2=None,
                    op0=ALU.mult),
                    reads=[b_ti32[j], b_scb], writes=[b_oq[ic]], tag=f"b6s{i}")
            for dm in range(DC):
                j2 = dm % 2
                pa, ba = ps2[j2], b_ps2[j2]
                for ic in range(IC):
                    sch.step("tensor", lambda e, ic=ic, dm=dm, pa=pa: e.matmul(
                        pa[:], wdT[ic][:, dm * 128:(dm + 1) * 128], oq[ic][:],
                        start=(ic == 0), stop=(ic == IC - 1), skip_group_check=True),
                        reads=[b_wdT[ic], b_oq[ic]], writes=[ba], tag=f"b6m{tt}_{dm}_{ic}")
                sch.step("scalar", lambda e, pa=pa, dm=dm: e.activation(
                    out=aod[dm][:], in_=pa[:], func=AF.Identity),
                    reads=[ba], writes=[b_aod[dm]], tag=f"b6e{tt}_{dm}")
            for qs in range(4):
                j = qs % 2
                for dm in range(DC):
                    sch.step("tensor", lambda e, dm=dm, qs=qs: e.transpose(
                        pst[:, dm * 128:(dm + 1) * 128],
                        aod[dm][:, qs * 128:(qs + 1) * 128], cx.ident_bf[:]),
                        reads=[b_aod[dm], cx.b_ident_bf], writes=[b_pst],
                        tag=f"b6t{tt}_{qs}_{dm}")
                sch.step("vector", lambda e, j=j: e.tensor_copy(
                    out=aotb[j][:], in_=pst[:]),
                    reads=[b_pst], writes=[b_aotb[j]], tag=f"b6a{tt}_{qs}")
                lnk.emit(tt * 4 + qs, aotb[j], b_aotb[j], lnob[j], b_lnob[j],
                         f"b6l{tt}_{qs}")
                row0 = tt * 512 + qs * 128
                sch.step("sync", lambda e, j=j, row0=row0: e.dma_start(
                    out=xrw[j][:], in_=x_own_d[row0:row0 + 128, :]),
                    reads=list(XR), writes=[b_xrw[j]], dma=True, tag=f"b6x{tt}_{qs}")
                sch.step("vector", lambda e, j=j: e.tensor_tensor(
                    out=xou[j][:], in0=lnob[j][:], in1=xrw[j][:], op=ALU.add),
                    reads=[b_lnob[j], b_xrw[j]], writes=[b_xou[j]], tag=f"b6ad{tt}_{qs}")
                sch.step("sync", lambda e, j=j, row0=row0: e.dma_start(
                    out=xo_d[row0:row0 + 128, :], in_=xou[j][:]),
                    reads=[b_xou[j]], writes=list(XW), dma=True, tag=f"b6st{tt}_{qs}")
        ph6.close()

        if fx is None:
            sch.emit(block, sems)
    return pr


def stage_b_core_inputs(inputs, core, x_stage):
    b, p = core // 2, core % 2
    d = {
        "x_own": np.ascontiguousarray(x_stage[b][_own_rows(p)]),
        "w_gate": np.ascontiguousarray(inputs["L0_ffn_w_gate"].astype(np.float32)),
        "w_up": np.ascontiguousarray(inputs["L0_ffn_w_up"].astype(np.float32)),
        "w_down": np.ascontiguousarray(inputs["L0_ffn_w_down"].astype(np.float32)),
    }
    d.update(const_inputs())
    return d



def _causal_masks_qk(parity):
    out = np.zeros((NSLOT, SLOT, 2 * SLOT), np.float32)
    for s_ in range(NSLOT):
        q0 = Q0[parity][s_]
        kb = (EXT[s_] - 2) * SLOT
        q = q0 + np.arange(SLOT)[:, None]
        k = kb + np.arange(2 * SLOT)[None, :]
        if q0 == 0:
            m = np.where(k < SLOT, 0.0, NEG)
        else:
            m = np.where(k <= q, 0.0, NEG)
        out[s_] = m
    return np.ascontiguousarray(out.astype(BF))


def _theta_override(parity):
    o = np.zeros((TOK, 1), np.float32)
    if Q0[parity][0] == 0:
        o[:SLOT] = 1.0
    return o


def _rope_tables64(rows):
    inv = 1.0 / (10000.0 ** (np.arange(0, HD, 2, dtype=np.float64) / HD))
    fr = np.outer(np.asarray(rows, np.float64), inv)

    def build(t64):
        t = np.repeat(t64, 2, axis=1)
        hi = t.astype(np.float32)
        lo = (t - hi.astype(np.float64)).astype(np.float32)
        return np.ascontiguousarray(hi), np.ascontiguousarray(lo)
    chi, clo = build(np.cos(fr))
    shi, slo = build(np.sin(fr))
    return chi, clo, shi, slo


def build_stage_c(dbg=False, spill_out=False, fx=None):
    if fx is None:
        pr = Prog()
        st = pr.stack
    else:
        pr = fx["pr"]
        st = fx["st"]
    sch = pr.sch
    nc = pr.nc
    NIT = 26
    XR = [fx["x_in_buf"]] if fx else []
    XW = [fx["x_out_buf"]] if fx else []

    x_own_d = fx["x_in"] if fx else pr.din("x_own", [TOK, D])
    x_bat_d = None if fx else pr.din("x_batch", [S, D])
    wqT_d = pr.din("WqT", [D, D])
    wkT_d = pr.din("WkT", [D, D])
    wvT_d = pr.din("WvT", [D, D])
    woT_d = pr.din("WoT", [D, D])
    iqT_d = pr.din("iqT", [D, IDXH * IDXD])
    ikT_d = pr.din("ikT", [D, 2 * IDXD])
    iwT_d = pr.din("iwT", [D, IDXH])
    mqk_d = pr.din("mask_qk", [NSLOT, SLOT, 2 * SLOT], BF16)
    ovr_d = pr.din("th_ovr", [128, 16])
    rco_d = pr.din("rope_chi_o", [TOK, 64])
    rcl_d = pr.din("rope_clo_o", [TOK, 64])
    rso_d = pr.din("rope_shi_o", [TOK, 64])
    rsl_d = pr.din("rope_slo_o", [TOK, 64])
    rcbh_d = pr.din("rope_chi_b", [S, 64])
    rcbl_d = pr.din("rope_clo_b", [S, 64])
    rsbh_d = pr.din("rope_shi_b", [S, 64])
    rsbl_d = pr.din("rope_slo_b", [S, 64])
    mk = pr.dout if (dbg or spill_out) else pr.dram
    k_sp = mk("k_sp", [32, 8, 128, 128], BF16)
    v_sp = mk("v_sp", [32, 128, 16 * 65], BF16)
    b_ksp = sch.buf("k_sp", kind="d")
    b_vspb = sch.buf("v_sp", kind="d")
    xo_d = fx["x_out"] if fx else pr.dout("x_out", [TOK, D])

    if fx is None:
        sems = {e: pr.stack.enter_context(nc.semaphore("sem_" + e)) for e in ENGS}
        blk_cm = nc.Block()
    else:
        sems = fx["sems"]
        blk_cm = _UseBlock(fx["block"])

    with blk_cm as block:
        cx = fx["cx"] if fx else emit_consts(pr, st)
        lnk = LNKit(pr, st, "clk")

        qsb = [pr.sb(st, f"qsb{c}", [128, TOK], BF16) for c in range(DC)]
        b_qsb = [sch.buf(f"qsb{c}") for c in range(DC)]
        iqf = [pr.sb(st, f"iqf{c}", [128, TOK], BF16) for c in range(2)]
        b_iqf = [sch.buf(f"iqf{c}") for c in range(2)]
        ikf = pr.sb(st, "ikf", [128, S], BF16)
        b_ikf = sch.buf("ikf")
        iwt = pr.sb(st, "iwt", [128, 16 * IDXH], F32)
        b_iwt = sch.buf("iwt")
        wo = pr.sb(st, "wo", [128, DC * 1024], BF16)
        b_wo = sch.buf("wo")
        nrm = pr.sb(st, "nrm", [1, 16], F32)
        b_nrm = sch.buf("nrm")
        knm = pr.sb(st, "knm", [1, 16], F32)
        b_knm = sch.buf("knm")
        qnm = pr.sb(st, "qnm", [1, 16], F32)
        b_qnm = sch.buf("qnm")
        ustars = pr.sb(st, "ustars", [128, 16], F32)
        b_ustars = sch.buf("ustars")

        sch.step("vector", lambda e: e.memset(knm[:], 0.0), writes=[b_knm], tag="knm0")
        sch.step("vector", lambda e: e.memset(qnm[:], 0.0), writes=[b_qnm], tag="qnm0")
        for c in range(DC):
            sch.step("gpsimd", lambda e, c=c: e.dma_start(
                out=wo[:, c * 1024:(c + 1) * 1024], in_=woT_d[c * 128:(c + 1) * 128, :]),
                writes=[b_wo], dma=True, tag=f"wo{c}")
        ovf = pr.sb(st, "ovf", [128, 16], F32)
        b_ovf = sch.buf("ovf")
        sch.step("sync", lambda e: e.dma_start(out=ovf[:], in_=ovr_d[:]),
            writes=[b_ovf], dma=True, tag="ovrld")
        if fx:
            bfl_d = pr.din("bflags", [128, 4])
            bfl = pr.sb(st, "bfl", [128, 4], F32)
            b_bfl = sch.buf("bfl")
            sch.step("sync", lambda e: e.dma_start(out=bfl[:], in_=bfl_d[:]),
                     writes=[b_bfl], dma=True, tag="bflld")
            cand = pr.sb(st, "cand", [128, D], F32)
            b_cand = sch.buf("cand")

        phW = ExitStack()
        wq = pr.sb(phW, "wq", [128, DC * 1024], BF16); b_wq = sch.buf("wq")
        wk = pr.sb(phW, "wk", [128, DC * 1024], BF16); b_wk = sch.buf("wk")
        wv = pr.sb(phW, "wv", [128, DC * 1024], BF16); b_wv = sch.buf("wv")
        wiq = pr.sb(phW, "wiq", [128, DC * 256], BF16); b_wiq = sch.buf("wiq")
        wik = pr.sb(phW, "wik", [128, DC * 128], BF16); b_wik = sch.buf("wik")
        wiw = pr.sb(phW, "wiw", [128, DC * IDXH], BF16); b_wiw = sch.buf("wiw")
        for c in range(DC):
            for wt_, wd_, bw_, w_ in ((wq, wqT_d, b_wq, 1024), (wk, wkT_d, b_wk, 1024),
                                      (wv, wvT_d, b_wv, 1024), (wiq, iqT_d, b_wiq, 256),
                                      (wik, ikT_d, b_wik, 128), (wiw, iwT_d, b_wiw, IDXH)):
                sch.step("gpsimd", lambda e, c=c, wt_=wt_, wd_=wd_, w_=w_: e.dma_start(
                    out=wt_[:, c * w_:(c + 1) * w_], in_=wd_[c * 128:(c + 1) * 128, :]),
                    writes=[bw_], dma=True, tag=f"wld{c}_{w_}")
        rt_o = [pr.sb(phW, f"rto{i}", [128, 16 * 64], F32) for i in range(2)]
        b_rto = [sch.buf(f"rto{i}") for i in range(2)]
        rt_b = [pr.sb(phW, f"rtb{i}", [128, 32 * 64], F32) for i in range(2)]
        b_rtb = [sch.buf(f"rtb{i}") for i in range(2)]
        rtmp = [pr.sb(phW, "rtmp0", [128, 32 * 64], F32)]
        b_rtmp = [sch.buf("rtmp0")]
        for dst, bdst, hi_d, lo_d, ntile in (
                (rt_o[0], b_rto[0], rco_d, rcl_d, 16), (rt_o[1], b_rto[1], rso_d, rsl_d, 16),
                (rt_b[0], b_rtb[0], rcbh_d, rcbl_d, 32), (rt_b[1], b_rtb[1], rsbh_d, rsbl_d, 32)):
            sch.step("sync", lambda e, dst=dst, hi_d=hi_d, ntile=ntile: e.dma_start(
                out=dst[:].rearrange("p (a b) -> p a b", b=64)[:, 0:ntile],
                in_=hi_d[:].rearrange("(a p) b -> p a b", p=128)),
                writes=[bdst], dma=True, tag="rth")
            sch.step("sync", lambda e, lo_d=lo_d, ntile=ntile: e.dma_start(
                out=rtmp[0][:].rearrange("p (a b) -> p a b", b=64)[:, 0:ntile],
                in_=lo_d[:].rearrange("(a p) b -> p a b", p=128)),
                writes=[b_rtmp[0]], dma=True, tag="rtl")
            sch.step("vector", lambda e, dst=dst, ntile=ntile: e.tensor_tensor(
                out=dst[:, 0:ntile * 64], in0=dst[:, 0:ntile * 64],
                in1=rtmp[0][:, 0:ntile * 64], op=ALU.add),
                reads=[bdst, b_rtmp[0]], writes=[bdst], tag="rta")

        phC1 = ExitStack()
        gfm = [[pr.sb(phC1, f"gfm{r}_{c}", [128, 512], BF16) for c in range(DC)]
               for r in range(2)]
        b_gfm = [sch.buf(f"gfm{r}") for r in range(2)]
        xb = [pr.sb(phC1, f"cx{j}", [128, D], F32) for j in range(2)]
        bxb = [sch.buf(f"cx{j}") for j in range(2)]
        lnt = [pr.sb(phC1, f"cl{j}", [128, D], BF16) for j in range(2)]
        blnt = [sch.buf(f"cl{j}") for j in range(2)]
        ktm = [pr.sb(phC1, f"ktm{j}", [128, D], BF16) for j in range(2)]
        b_ktm = [sch.buf(f"ktm{j}") for j in range(2)]
        krot = pr.sb(phC1, "krot", [128, D], BF16)
        b_krot = sch.buf("krot")
        ksq = pr.sb(phC1, "ksq", [128, D], BF16)
        b_ksq = sch.buf("ksq")
        vst = [pr.sb(phC1, f"vst{j}", [128, 16 * 65], BF16) for j in range(2)]
        b_vst = [sch.buf(f"vst{j}") for j in range(2)]
        kst = pr.sb(phC1, "kst", [128, 512], BF16)
        b_kst = sch.buf("kst")
        ptr = [pr.ps(phC1, f"cp{j}", [128, D], BF16) for j in range(2)]
        bptr = [sch.buf(f"cp{j}") for j in range(2)]
        acc = [pr.ps(phC1, f"cacc{j}", [128, 512], F32) for j in range(2)]
        b_acc = [sch.buf(f"cacc{j}") for j in range(2)]
        psn = [pr.ps(phC1, f"cpsn{j}", [16, 512], F32) for j in range(2)]
        b_psn = [sch.buf(f"cpsn{j}") for j in range(2)]

        AG_P = [0, 1, 1, 0, 0, 1, 1, 0]
        AG_SLOT = [0, 0, 1, 1, 2, 2, 3, 3]

        def ln_tile_c(i, src_ap, batch_tile=None, xrd=()):
            j = i % 2
            if fx and batch_tile is not None:
                g, sub = batch_tile // 4, batch_tile % 4
                off = AG_SLOT[g] * 512 + sub * 128
                ag, bag = fx["ag_out"], fx["ag_out_buf"]
                for b4 in range(4):
                    row = (2 * b4 + AG_P[g]) * TOK + off
                    sch.step("sync", lambda e, b4=b4, row=row: e.dma_start(
                        out=cand[:] if b4 else xb[j][:], in_=ag[row:row + 128, :]),
                        reads=[bag], writes=[b_cand if b4 else bxb[j]], dma=True,
                        tag=f"cldc{i}_{b4}")
                    if b4 == 0:
                        sch.step("vector", lambda e: e.tensor_scalar(
                            out=xb[j][:], in0=xb[j][:], scalar1=bfl[:, 0:1],
                            scalar2=None, op0=ALU.mult),
                            reads=[bxb[j], b_bfl], writes=[bxb[j]], tag=f"cldm{i}_0")
                    else:
                        sch.step("vector", lambda e, b4=b4: e.scalar_tensor_tensor(
                            out=xb[j][:], in0=cand[:], scalar=bfl[:, b4:b4 + 1],
                            in1=xb[j][:], op0=ALU.mult, op1=ALU.add),
                            reads=[b_cand, b_bfl, bxb[j]], writes=[bxb[j]],
                            tag=f"cldm{i}_{b4}")
            else:
                sch.step("sync", lambda e: e.dma_start(out=xb[j][:], in_=src_ap),
                         reads=list(xrd), writes=[bxb[j]], dma=True, tag=f"cld{i}")
            lnk.emit(i, xb[j], bxb[j], lnt[j], blnt[j], f"c1_{i}")
            return lnt[j], blnt[j]

        def transpose_to_c(i, ln, bln, dst_list, dst_buf, col0):
            j = i % 2
            pt, bpt = ptr[j], bptr[j]
            for c in range(DC):
                sch.step("tensor", lambda e, c=c: e.transpose(
                    pt[:, c * 128:(c + 1) * 128], ln[:, c * 128:(c + 1) * 128],
                    cx.ident_bf[:]),
                    reads=[bln, cx.b_ident_bf], writes=[bpt], tag=f"ctr{i}_{c}")
            for c in range(DC):
                sch.step("vector", lambda e, c=c: e.tensor_copy(
                    out=dst_list[c][:, col0:col0 + 128],
                    in_=pt[:, c * 128:(c + 1) * 128]),
                    reads=[bpt], writes=[dst_buf], tag=f"cev{i}_{c}")

        def rope_tile(tm, btm, rcos, brcos, rsin, brsin, rc0, dst, bdst, scale_done,
                      tagp):
            v3 = tm[:].rearrange("p (h d) -> p h d", d=64)
            cosb = rcos[:, rc0:rc0 + 64].rearrange("p (a d) -> p a d", a=1).to_broadcast([128, H, 64])
            sinb = rsin[:, rc0:rc0 + 64].rearrange("p (a d) -> p a d", a=1).to_broadcast([128, H, 64])
            sch.step("vector", lambda e: e.tensor_scalar(
                out=krot[:].rearrange("p (h d) -> p h d", d=64)[:, :, 0:64:2],
                in0=v3[:, :, 1:64:2], scalar1=-1.0, scalar2=None, op0=ALU.mult),
                reads=[btm], writes=[b_krot], tag=f"{tagp}r1")
            sch.step("vector", lambda e: e.tensor_copy(
                out=krot[:].rearrange("p (h d) -> p h d", d=64)[:, :, 1:64:2],
                in_=v3[:, :, 0:64:2]),
                reads=[btm], writes=[b_krot], tag=f"{tagp}r2")
            sch.step("vector", lambda e: e.tensor_tensor(
                out=dst[:].rearrange("p (h d) -> p h d", d=64), in0=v3, in1=cosb,
                op=ALU.mult),
                reads=[btm, brcos], writes=[bdst], tag=f"{tagp}r3")
            sch.step("vector", lambda e: e.tensor_tensor(
                out=krot[:].rearrange("p (h d) -> p h d", d=64),
                in0=krot[:].rearrange("p (h d) -> p h d", d=64), in1=sinb,
                op=ALU.mult),
                reads=[b_krot, brsin], writes=[b_krot], tag=f"{tagp}r4")
            sch.step("vector", lambda e: e.tensor_tensor(
                out=dst[:], in0=dst[:], in1=krot[:], op=ALU.add),
                reads=[bdst, b_krot], writes=[bdst], tag=f"{tagp}r5")

        ti = 0
        for g in range(8):
            r = g % 2
            for u in range(4):
                ln, bln = ln_tile_c(
                    ti, None if fx else x_bat_d[(g * 4 + u) * 128:(g * 4 + u + 1) * 128, :],
                    batch_tile=g * 4 + u)
                transpose_to_c(ti, ln, bln, gfm[r], b_gfm[r], u * 128)
                ti += 1
            for u in range(4):
                kt128 = g * 4 + u
                jK = kt128 % 2
                for half in range(2):
                    pa, ba = acc[half], b_acc[half]
                    for c in range(DC):
                        sch.step("tensor", lambda e, c=c, u=u, half=half, r=r, pa=pa:
                                 e.matmul(pa[:], gfm[r][c][:, u * 128:(u + 1) * 128],
                                          wk[:, c * 1024 + half * 512:c * 1024 + (half + 1) * 512],
                                          start=(c == 0), stop=(c == DC - 1),
                                          skip_group_check=True),
                                 reads=[b_gfm[r], b_wk], writes=[ba], tag=f"ckm{kt128}_{half}_{c}")
                    sch.step("scalar", lambda e, half=half, jK=jK, pa=pa: e.activation(
                        out=ktm[jK][:, half * 512:(half + 1) * 512], in_=pa[:],
                        func=AF.Identity),
                        reads=[ba], writes=[b_ktm[jK]], tag=f"cke{kt128}_{half}")
                rope_tile(ktm[jK], b_ktm[jK], rt_b[0], b_rtb[0], rt_b[1], b_rtb[1],
                          kt128 * 64, ktm[jK], b_ktm[jK], False, f"ckr{kt128}_")
                sch.step("scalar", lambda e, jK=jK: e.activation(
                    out=ksq[:], in_=ktm[jK][:], func=AF.Square),
                    reads=[b_ktm[jK]], writes=[b_ksq], tag=f"cksq{kt128}")
                sch.step("vector", lambda e: e.reduce_max(
                    nrm[0:1, 0:1], ksq[0:1, 0:1], axis=AX.X),
                    reads=[b_ksq], writes=[b_nrm], tag=f"cknop{kt128}") if False else None
                sch.step("vector", lambda e, jK=jK: e.tensor_reduce(
                    lnk.scr[:, 0:16], ksq[:].rearrange("p (h d) -> p h d", d=64),
                    axis=AX.X, op=ALU.add),
                    reads=[b_ksq], writes=[lnk.bscr], tag=f"ckn{kt128}")
                pnn, bnn = psn[kt128 % 2], b_psn[kt128 % 2]
                sch.step("tensor", lambda e, pnn=pnn: e.transpose(
                    pnn[0:16, 0:128], krot[:, 0:16], cx.ident_f[:]),
                    reads=[b_krot, cx.b_ident_f], writes=[bnn], tag=f"cknt{kt128}") if False else None
                sch.step("tensor", lambda e, pnn=pnn: e.transpose(
                    pnn[0:16, 0:128], lnk.scr[:, 0:16], cx.ident_f[:]),
                    reads=[lnk.bscr, cx.b_ident_f], writes=[bnn], tag=f"cknt{kt128}")
                sch.step("vector", lambda e, pnn=pnn: e.reduce_max(
                    nrm[0:1, 1:2], pnn[0:16, 0:128].rearrange("a b -> () (a b)") if False
                    else pnn[0:1, 0:128], axis=AX.X),
                    reads=[bnn], writes=[b_nrm], tag=f"cknr2{kt128}") if False else None
                for hh in range(H):
                    sch.step("vector", lambda e, pnn=pnn, hh=hh: e.reduce_max(
                        nrm[0:1, 2:3], pnn[hh:hh + 1, 0:128], axis=AX.X),
                        reads=[bnn], writes=[b_nrm], tag=f"ckr{kt128}_{hh}") if False else None
                sch.step("vector", lambda e, pnn=pnn: e.tensor_copy(
                    out=lnk.scr[0:16, 0:128], in_=pnn[0:16, 0:128]),
                    reads=[bnn], writes=[lnk.bscr], tag=f"ckne{kt128}")
                sch.step("vector", lambda e: e.tensor_reduce(
                    lnk.scr[0:16, 128:129], lnk.scr[0:16, 0:128], axis=AX.X,
                    op=ALU.max),
                    reads=[lnk.bscr], writes=[lnk.bscr], tag=f"cknm{kt128}")
                sch.step("tensor", lambda e, pnn=pnn: e.transpose(
                    pnn[0:1, 0:16], lnk.scr[0:16, 128:129], cx.ident_f[0:16, 0:16]),
                    reads=[lnk.bscr, cx.b_ident_f], writes=[bnn], tag=f"cknt2{kt128}")
                sch.step("vector", lambda e, pnn=pnn: e.tensor_tensor(
                    out=knm[0:1, :], in0=knm[0:1, :], in1=pnn[0:1, 0:16], op=ALU.max),
                    reads=[bnn, b_knm], writes=[b_knm], tag=f"cknx{kt128}")
                jP = kt128 % 2
                pt, bpt = ptr[jP], bptr[jP]
                for c in range(DC):
                    sch.step("tensor", lambda e, c=c, jK=jK, pt=pt: e.transpose(
                        pt[:, c * 128:(c + 1) * 128],
                        ktm[jK][:, c * 128:(c + 1) * 128], cx.ident_bf[:]),
                        reads=[b_ktm[jK], cx.b_ident_bf], writes=[bpt], tag=f"cktr{kt128}_{c}")
                sch.step("vector", lambda e, pt=pt: e.tensor_copy(
                    out=kst[:, 0:512] if False else kst[:], in_=pt[:, 0:512]),
                    reads=[bpt], writes=[b_kst], tag=f"ckev{kt128}") if False else None
                sch.step("vector", lambda e, pt=pt: e.tensor_copy(
                    out=krot[:], in_=pt[:]),
                    reads=[bpt], writes=[b_krot], tag=f"ckev{kt128}")
                sch.step("sync", lambda e, kt128=kt128: e.dma_start(
                    out=k_sp[kt128].rearrange("a p c -> p a c"),
                    in_=krot[:].rearrange("p (a c) -> p a c", c=128)),
                    reads=[b_krot], writes=[b_ksp], dma=True, tag=f"cks{kt128}")
                jV = kt128 % 2
                sch.step("vector", lambda e, jV=jV: e.memset(
                    vst[jV][:].rearrange("p (h c) -> p h c", c=65)[:, :, 64:65], 1.0),
                    writes=[b_vst[jV]], tag=f"cvo{kt128}")
                for half in range(2):
                    pa, ba = acc[half], b_acc[half]
                    for c in range(DC):
                        sch.step("tensor", lambda e, c=c, u=u, half=half, r=r, pa=pa:
                                 e.matmul(pa[:], gfm[r][c][:, u * 128:(u + 1) * 128],
                                          wv[:, c * 1024 + half * 512:c * 1024 + (half + 1) * 512],
                                          start=(c == 0), stop=(c == DC - 1),
                                          skip_group_check=True),
                                 reads=[b_gfm[r], b_wv], writes=[ba], tag=f"cvm{kt128}_{half}_{c}")
                    sch.step("scalar", lambda e, half=half, jV=jV, pa=pa: e.activation(
                        out=vst[jV][:].rearrange("p (h c) -> p h c", c=65)
                        [:, half * 8:(half + 1) * 8, 0:64],
                        in_=pa[:].rearrange("p (h c) -> p h c", c=64),
                        func=AF.Identity),
                        reads=[ba], writes=[b_vst[jV]], tag=f"cve{kt128}_{half}")
                sch.step("sync", lambda e, jV=jV, kt128=kt128: e.dma_start(
                    out=v_sp[kt128], in_=vst[jV][:]),
                    reads=[b_vst[jV]], writes=[b_vspb], dma=True, tag=f"cvs{kt128}")
                pa, ba = acc[0], b_acc[0]
                for c in range(DC):
                    sch.step("tensor", lambda e, c=c, u=u, r=r, pa=pa: e.matmul(
                        pa[:, 0:128], gfm[r][c][:, u * 128:(u + 1) * 128],
                        wik[:, c * 128:(c + 1) * 128],
                        start=(c == 0), stop=(c == DC - 1), skip_group_check=True),
                        reads=[b_gfm[r], b_wik], writes=[ba], tag=f"cikm{kt128}_{c}")
                sch.step("scalar", lambda e, pa=pa: e.activation(
                    out=kst[:, 0:128], in_=pa[:, 0:128], func=AF.Identity),
                    reads=[ba], writes=[b_kst], tag=f"cike{kt128}")
                jP2 = (kt128 + 1) % 2
                pt2, bpt2 = ptr[jP2], bptr[jP2]
                sch.step("tensor", lambda e, pt2=pt2: e.transpose(
                    pt2[:, 0:128], kst[:, 0:128], cx.ident_bf[:]),
                    reads=[b_kst, cx.b_ident_bf], writes=[bpt2], tag=f"cikt{kt128}")
                sch.step("vector", lambda e, pt2=pt2, kt128=kt128: e.tensor_copy(
                    out=ikf[:, kt128 * 128:(kt128 + 1) * 128], in_=pt2[:, 0:128]),
                    reads=[bpt2], writes=[b_ikf], tag=f"cike2{kt128}")

        qtm = [pr.sb(phC1, f"qtm{j}", [128, D], BF16) for j in range(2)]
        b_qtm = [sch.buf(f"qtm{j}") for j in range(2)]
        for g in range(4):
            r = g % 2
            for u in range(4):
                ln, bln = ln_tile_c(ti, x_own_d[(g * 4 + u) * 128:(g * 4 + u + 1) * 128, :],
                                    xrd=XR)
                transpose_to_c(ti, ln, bln, gfm[r], b_gfm[r], u * 128)
                ti += 1
            for u in range(4):
                tt128 = g * 4 + u
                jQ = tt128 % 2
                for half in range(2):
                    pa, ba = acc[half], b_acc[half]
                    for c in range(DC):
                        sch.step("tensor", lambda e, c=c, u=u, half=half, r=r, pa=pa:
                                 e.matmul(pa[:], gfm[r][c][:, u * 128:(u + 1) * 128],
                                          wq[:, c * 1024 + half * 512:c * 1024 + (half + 1) * 512],
                                          start=(c == 0), stop=(c == DC - 1),
                                          skip_group_check=True),
                                 reads=[b_gfm[r], b_wq], writes=[ba], tag=f"cqm{tt128}_{half}_{c}")
                    sch.step("scalar", lambda e, half=half, jQ=jQ, pa=pa: e.activation(
                        out=qtm[jQ][:, half * 512:(half + 1) * 512], in_=pa[:],
                        func=AF.Identity, scale=0.125),
                        reads=[ba], writes=[b_qtm[jQ]], tag=f"cqe{tt128}_{half}")
                rope_tile(qtm[jQ], b_qtm[jQ], rt_o[0], b_rto[0], rt_o[1], b_rto[1],
                          tt128 * 64, qtm[jQ], b_qtm[jQ], True, f"cqr{tt128}_")
                sch.step("scalar", lambda e, jQ=jQ: e.activation(
                    out=ksq[:], in_=qtm[jQ][:], func=AF.Square),
                    reads=[b_qtm[jQ]], writes=[b_ksq], tag=f"cqsq{tt128}")
                sch.step("vector", lambda e: e.tensor_reduce(
                    lnk.scr[:, 0:16], ksq[:].rearrange("p (h d) -> p h d", d=64),
                    axis=AX.X, op=ALU.add),
                    reads=[b_ksq], writes=[lnk.bscr], tag=f"cqn{tt128}")
                pnn, bnn = psn[tt128 % 2], b_psn[tt128 % 2]
                sch.step("tensor", lambda e, pnn=pnn: e.transpose(
                    pnn[0:16, 0:128], lnk.scr[:, 0:16], cx.ident_f[:]),
                    reads=[lnk.bscr, cx.b_ident_f], writes=[bnn], tag=f"cqnt{tt128}")
                sch.step("vector", lambda e, pnn=pnn: e.tensor_copy(
                    out=lnk.scr[0:16, 0:128], in_=pnn[0:16, 0:128]),
                    reads=[bnn], writes=[lnk.bscr], tag=f"cqne{tt128}")
                sch.step("vector", lambda e: e.tensor_reduce(
                    lnk.scr[0:16, 128:129], lnk.scr[0:16, 0:128], axis=AX.X,
                    op=ALU.max),
                    reads=[lnk.bscr], writes=[lnk.bscr], tag=f"cqnm{tt128}")
                sch.step("tensor", lambda e, pnn=pnn: e.transpose(
                    pnn[0:1, 0:16], lnk.scr[0:16, 128:129], cx.ident_f[0:16, 0:16]),
                    reads=[lnk.bscr, cx.b_ident_f], writes=[bnn], tag=f"cqnt2{tt128}")
                sch.step("vector", lambda e, pnn=pnn: e.tensor_tensor(
                    out=qnm[0:1, :], in0=qnm[0:1, :], in1=pnn[0:1, 0:16], op=ALU.max),
                    reads=[bnn, b_qnm], writes=[b_qnm], tag=f"cqnx{tt128}")
                jP = tt128 % 2
                pt, bpt = ptr[jP], bptr[jP]
                for c in range(DC):
                    sch.step("tensor", lambda e, c=c, jQ=jQ, pt=pt: e.transpose(
                        pt[:, c * 128:(c + 1) * 128],
                        qtm[jQ][:, c * 128:(c + 1) * 128], cx.ident_bf[:]),
                        reads=[b_qtm[jQ], cx.b_ident_bf], writes=[bpt], tag=f"cqtr{tt128}_{c}")
                for c in range(DC):
                    sch.step("vector", lambda e, c=c, pt=pt, tt128=tt128: e.tensor_copy(
                        out=qsb[c][:, tt128 * 128:(tt128 + 1) * 128],
                        in_=pt[:, c * 128:(c + 1) * 128]),
                        reads=[bpt], writes=[b_qsb[c]], tag=f"cqev{tt128}_{c}")
                pa, ba = acc[0], b_acc[0]
                for c in range(DC):
                    sch.step("tensor", lambda e, c=c, u=u, r=r, pa=pa: e.matmul(
                        pa[:, 0:256], gfm[r][c][:, u * 128:(u + 1) * 128],
                        wiq[:, c * 256:(c + 1) * 256],
                        start=(c == 0), stop=(c == DC - 1), skip_group_check=True),
                        reads=[b_gfm[r], b_wiq], writes=[ba], tag=f"ciqm{tt128}_{c}")
                sch.step("scalar", lambda e, pa=pa: e.activation(
                    out=kst[:, 0:256], in_=pa[:, 0:256], func=AF.Identity),
                    reads=[ba], writes=[b_kst], tag=f"ciqe{tt128}")
                for c2 in range(2):
                    jP2 = (tt128 + 1) % 2
                    pt2, bpt2 = ptr[jP2], bptr[jP2]
                    sch.step("tensor", lambda e, c2=c2, pt2=pt2: e.transpose(
                        pt2[:, c2 * 128:(c2 + 1) * 128],
                        kst[:, c2 * 128:(c2 + 1) * 128], cx.ident_bf[:]),
                        reads=[b_kst, cx.b_ident_bf], writes=[bpt2], tag=f"ciqt{tt128}_{c2}")
                    sch.step("vector", lambda e, c2=c2, pt2=pt2, tt128=tt128: e.tensor_copy(
                        out=iqf[c2][:, tt128 * 128:(tt128 + 1) * 128],
                        in_=pt2[:, c2 * 128:(c2 + 1) * 128]),
                        reads=[bpt2], writes=[b_iqf[c2]], tag=f"ciqv{tt128}_{c2}")
                pa2, ba2 = acc[1], b_acc[1]
                for c in range(DC):
                    sch.step("tensor", lambda e, c=c, u=u, r=r, pa2=pa2: e.matmul(
                        pa2[:, 0:IDXH], gfm[r][c][:, u * 128:(u + 1) * 128],
                        wiw[:, c * IDXH:(c + 1) * IDXH],
                        start=(c == 0), stop=(c == DC - 1), skip_group_check=True),
                        reads=[b_gfm[r], b_wiw], writes=[ba2], tag=f"ciwm{tt128}_{c}")
                sch.step("vector", lambda e, pa2=pa2, tt128=tt128: e.tensor_copy(
                    out=iwt[:, tt128 * IDXH:(tt128 + 1) * IDXH], in_=pa2[:, 0:IDXH]),
                    reads=[ba2], writes=[b_iwt], tag=f"ciwe{tt128}")

        sch.step("vector", lambda e: e.tensor_tensor(
            out=nrm[0:1, 0:16], in0=qnm[0:1, :], in1=knm[0:1, :], op=ALU.mult),
            reads=[b_qnm, b_knm], writes=[b_nrm], tag="cu2")
        sch.step("scalar", lambda e: e.activation(
            out=nrm[0:1, 0:16], in_=nrm[0:1, 0:16], func=AF.Sqrt),
            reads=[b_nrm], writes=[b_nrm], tag="cusq")
        psb = acc[0]
        b_psb = b_acc[0]
        sch.step("tensor", lambda e: e.matmul(
            psb[:, 0:16], cx.ones_row[0:1, :], nrm[0:1, 0:16],
            start=True, stop=True, skip_group_check=True),
            reads=[b_nrm, cx.b_ones_row], writes=[b_psb], tag="cubc")
        sch.step("vector", lambda e: e.tensor_scalar(
            out=ustars[:], in0=psb[:, 0:16], scalar1=-1.0, scalar2=None, op0=ALU.mult),
            reads=[b_psb], writes=[b_ustars], tag="cuneg")
        phC1.close()
        phW.close()
        sch.barrier()

        phS = ExitStack()
        A_sb = [pr.sb(phS, "A0", [128, 4096], F32)]
        b_A = [sch.buf("A0")]
        btT = pr.sb(phS, "btT", [128, 32 * 512], BF16)
        b_btT = sch.buf("btT")
        msk_scr = pr.sb(phS, "mskscr", [128, 4096], U8)
        b_mskscr = sch.buf("mskscr")
        relu_sb = [pr.sb(phS, f"relu{j}", [128, 512], BF16) for j in range(2)]
        b_relu = [sch.buf(f"relu{j}") for j in range(2)]
        mskt = [pr.sb(phS, f"cmsk{j}", [128, 1024], BF16) for j in range(2)]
        b_mskt = [sch.buf(f"cmsk{j}") for j in range(2)]
        cans = [pr.sb(phS, f"can{j}", [128, 8], F32) for j in range(6)]
        b_cans = [sch.buf(f"can{j}") for j in range(6)]
        ths = []
        b_ths = []
        for j in range(4):
            ths.append(pr.sb(phS, f"ths{j}", [128, 8], F32))
            b_ths.append(sch.buf(f"ths{j}"))
        thu = [pr.sb(phS, f"thu{j}", [128, 2], U8) for j in range(4)]
        b_thu = [sch.buf(f"thu{j}") for j in range(4)]
        for j in range(6):
            sch.step("vector", lambda e, j=j: e.memset(cans[j][:], 1234.5),
                     writes=[b_cans[j]], tag=f"can{j}")
        negmil = pr.sb(phS, "negmil", [128, 1], F32)
        b_negmil = sch.buf("negmil")
        bbf = pr.sb(phS, "bbf", [128, 4096], BF16)
        b_bbf = sch.buf("bbf")
        ktl = [pr.sb(phS, f"cktl{r}", [128, 256], BF16) for r in range(3)]
        b_ktl = [sch.buf(f"cktl{r}") for r in range(3)]
        vtl = [pr.sb(phS, f"cvtl{r}", [128, 4 * 65], BF16) for r in range(3)]
        b_vtl = [sch.buf(f"cvtl{r}") for r in range(3)]
        pt_t = [[pr.sb(phS, f"cpt{hp}_{j}", [128, 1024], BF16) for j in range(2)]
                for hp in range(2)]
        b_pt = [[sch.buf(f"cpt{hp}_{j}") for j in range(2)] for hp in range(2)]
        yfm = [pr.sb(phS, f"cyfm{c}", [128, 512], BF16) for c in range(DC)]
        b_yfm = [sch.buf(f"cyfm{c}") for c in range(DC)]
        rsum = pr.sb(phS, "crsum", [1, 512], F32)
        b_rsum = sch.buf("crsum")
        ytmp = pr.sb(phS, "cytmp", [64, 512], F32)
        b_ytmp = sch.buf("cytmp")
        ao = [pr.sb(phS, f"cao{c}", [128, 512], BF16) for c in range(DC)]
        b_ao = [sch.buf(f"cao{c}") for c in range(DC)]
        aot = [pr.sb(phS, f"caot{j}", [128, D], BF16) for j in range(2)]
        b_aot = [sch.buf(f"caot{j}") for j in range(2)]
        lnres = [pr.sb(phS, f"clnr{j}", [128, D], F32) for j in range(2)]
        b_lnres = [sch.buf(f"clnr{j}") for j in range(2)]
        xrow = [pr.sb(phS, f"cxr{j}", [128, D], F32) for j in range(2)]
        b_xrow = [sch.buf(f"cxr{j}") for j in range(2)]
        xout = [pr.sb(phS, f"cxo{j}", [128, D], F32) for j in range(2)]
        b_xout = [sch.buf(f"cxo{j}") for j in range(2)]
        qkp = pr.ps(phS, "cqkp", [128, 1024], F32)
        qk = [qkp[:, 0:512], qkp[:, 512:1024]]
        b_qk = [sch.buf(f"cqk{j}") for j in range(2)]
        pv = [pr.ps(phS, f"cpv{j}", [65, 512], F32) for j in range(4)]
        b_pv = [sch.buf(f"cpv{j}") for j in range(4)]
        ptbf = pr.ps(phS, "cptbf", [128, D], BF16)
        b_ptbf = sch.buf("cptbf")
        spare = pr.ps(phS, "cspare", [128, 512], F32)
        b_spare = sch.buf("cspare")
        sch.step("vector", lambda e: e.memset(negmil[:], -1.0e6),
                 writes=[b_negmil], tag="negmil")
        upair = pr.sb(phS, "cupair", [128, 8], F32)
        b_upair = sch.buf("cupair")
        sch.step("vector", lambda e: e.tensor_reduce(
            upair[:], ustars[:].rearrange("p (a b) -> p a b", b=2), axis=AX.X,
            op=ALU.min),
            reads=[b_ustars], writes=[b_upair], tag="cupair")
        dbgth = dbg or (fx is not None and fx.get("dbgth"))
        if dbgth:
            dbg_th = pr.dout("dbg_th", [128, 16])
            dbg_ikx = pr.dout("dbg_ikx", [128, S], BF16)
            dbg_iqx = pr.dout("dbg_iqx", [128, TOK], BF16)
            dbg_iwx = pr.dout("dbg_iwx", [128, 16 * IDXH])
            dbg_A1 = pr.dout("dbg_A1", [128, 4096])
            dbg_Aall = pr.dout("dbg_Aall", [16, 128, 4096])
            dbg_ths = pr.dout("dbg_ths", [128, 16 * 8])
            dbg_can = pr.dout("dbg_can", [128, 48])
            thsnap = pr.sb(phS, "thsnap", [128, 16 * 8], F32)
            b_thsnap = sch.buf("thsnap")
            dbg_th_sb = pr.sb(phS, "dbgth", [128, 16], F32)
            b_dbgth = sch.buf("dbgth")
            sch.step("sync", lambda e: e.dma_start(out=dbg_ikx[:], in_=ikf[:]),
                     reads=[b_ikf], dma=True, tag="dbg_ikx")
            sch.step("sync", lambda e: e.dma_start(out=dbg_iqx[:], in_=iqf[0][:]),
                     reads=[b_iqf[0]], dma=True, tag="dbg_iqx")
            sch.step("sync", lambda e: e.dma_start(out=dbg_iwx[:], in_=iwt[:]),
                     reads=[b_iwt], dma=True, tag="dbg_iwx")
        if dbg:
            dbg_us = pr.dout("dbg_us", [128, 16])
            dbg_q = pr.dout("dbg_q", [128, TOK], BF16)
            dbg_ik = pr.dout("dbg_ik", [128, S], BF16)
            dbg_iq = pr.dout("dbg_iq", [128, TOK], BF16)
            dbg_iw = pr.dout("dbg_iw", [128, 16 * IDXH])
            dbg_A = pr.dout("dbg_A", [128, 1024])
            sch.step("sync", lambda e: e.dma_start(out=dbg_us[:], in_=ustars[:]),
                     reads=[b_ustars], dma=True, tag="dbg_us")
            sch.step("sync", lambda e: e.dma_start(out=dbg_q[:], in_=qsb[0][:]),
                     reads=[b_qsb[0]], dma=True, tag="dbg_q")
            sch.step("sync", lambda e: e.dma_start(out=dbg_ik[:], in_=ikf[:]),
                     reads=[b_ikf], dma=True, tag="dbg_ik")
            sch.step("sync", lambda e: e.dma_start(out=dbg_iq[:], in_=iqf[0][:]),
                     reads=[b_iqf[0]], dma=True, tag="dbg_iq")
            sch.step("sync", lambda e: e.dma_start(out=dbg_iw[:], in_=iwt[:]),
                     reads=[b_iwt], dma=True, tag="dbg_iw")
            dbg_y = pr.dout("dbg_y", [128, 512], BF16)
            dbg_pt = pr.dout("dbg_pt", [128, 512], BF16)
            dbg_rs = pr.dout("dbg_rs", [1, 512])
            dbg_ao = pr.dout("dbg_ao", [128, 512], BF16)
            dbg_bt = pr.dout("dbg_bt", [128, 1024], BF16)

        for s_ in range(NSLOT):
            Ek = EXT[s_] * 4
            W = EXT[s_] * 512
            q0c = s_ * 512
            for qs in range(4):
                jA = 0
                At, bAt = A_sb[0], b_A[0]
                trow = s_ * 4 + qs
                jM = qs % 2
                sch.step("sync", lambda e, s_=s_, qs=qs, jM=jM: e.dma_start(
                    out=mskt[jM][:], in_=mqk_d[s_, qs * 128:(qs + 1) * 128, :]),
                    writes=[b_mskt[jM]], dma=True, tag=f"cmld{s_}_{qs}")
                for ih in range(IDXH):
                    for kt5 in range(EXT[s_]):
                        pa, ba = qk[kt5 % 2], b_qk[kt5 % 2]
                        sch.step("tensor", lambda e, ih=ih, kt5=kt5, trow=trow, pa=pa:
                                 e.matmul(pa[:],
                                          iqf[ih // 2][(ih % 2) * 64:(ih % 2) * 64 + 64,
                                                       trow * 128:(trow + 1) * 128],
                                          ikf[(ih % 2) * 64:(ih % 2) * 64 + 64,
                                              kt5 * 512:(kt5 + 1) * 512],
                                          start=True, stop=True, skip_group_check=True),
                                 reads=[b_iqf[ih // 2], b_ikf], writes=[ba],
                                 tag=f"cam{s_}_{qs}_{ih}_{kt5}")
                        jR = kt5 % 2
                        sch.step("scalar", lambda e, pa=pa, jR=jR: e.activation(
                            out=relu_sb[jR][:], in_=pa[:], func=AF.Relu),
                            reads=[ba], writes=[b_relu[jR]], tag=f"car{s_}_{qs}_{ih}_{kt5}")
                        iwcol = iwt[:, trow * IDXH + ih:trow * IDXH + ih + 1]
                        if ih == 0:
                            sch.step("vector", lambda e, jR=jR, kt5=kt5, At=At,
                                     iwcol=iwcol: e.tensor_scalar(
                                out=At[:, kt5 * 512:(kt5 + 1) * 512], in0=relu_sb[jR][:],
                                scalar1=iwcol, scalar2=None, op0=ALU.mult),
                                reads=[b_relu[jR], b_iwt], writes=[bAt],
                                tag=f"caa{s_}_{qs}_{ih}_{kt5}")
                        else:
                            sch.step("vector", lambda e, jR=jR, kt5=kt5, At=At,
                                     iwcol=iwcol: e.scalar_tensor_tensor(
                                out=At[:, kt5 * 512:(kt5 + 1) * 512], in0=relu_sb[jR][:],
                                scalar=iwcol, in1=At[:, kt5 * 512:(kt5 + 1) * 512],
                                op0=ALU.mult, op1=ALU.add),
                                reads=[b_relu[jR], b_iwt, bAt], writes=[bAt],
                                tag=f"caa{s_}_{qs}_{ih}_{kt5}")
                for mh in range(2):
                    sch.step("vector", lambda e, mh=mh, At=At, jM=jM, W=W: e.tensor_tensor(
                        out=At[:, W - 1024 + mh * 512:W - 1024 + (mh + 1) * 512],
                        in0=At[:, W - 1024 + mh * 512:W - 1024 + (mh + 1) * 512],
                        in1=mskt[jM][:, mh * 512:(mh + 1) * 512], op=ALU.add),
                        reads=[bAt, b_mskt[jM]], writes=[bAt], tag=f"cama{s_}_{qs}_{mh}")
                th, bth = ths[qs], b_ths[qs]
                tu, btu = thu[qs], b_thu[qs]
                lo, hi, mid, cnt, th_f = (th[:, 0:1], th[:, 1:2], th[:, 2:3],
                                          th[:, 3:4], th[:, 4:5])
                sch.step("vector", lambda e, At=At, hi=hi, W=W: e.reduce_max(
                    hi, At[:, 0:W], axis=AX.X),
                    reads=[bAt], writes=[bth], tag=f"cbmx{s_}_{qs}")
                jM = qs % 2
                sch.step("vector", lambda e, At=At, jM=jM, W=W: e.scalar_tensor_tensor(
                    out=bbf[:, 0:1024],
                    in0=mskt[jM][:], scalar=-2.0, in1=At[:, W - 1024:W],
                    op0=ALU.mult, op1=ALU.add),
                    reads=[bAt, b_mskt[jM]], writes=[b_bbf], tag=f"cbms{s_}_{qs}")
                sch.step("vector", lambda e, lo=lo: e.tensor_reduce(
                    lo, bbf[:, 0:1024], axis=AX.X, op=ALU.min),
                    reads=[b_bbf], writes=[bth], tag=f"cbmn{s_}_{qs}")
                if W > 1024:
                    sch.step("vector", lambda e, At=At, mid=mid, W=W: e.tensor_reduce(
                        mid, At[:, 0:W - 1024], axis=AX.X, op=ALU.min),
                        reads=[bAt], writes=[bth], tag=f"cbmn2{s_}_{qs}")
                    sch.step("vector", lambda e, lo=lo, mid=mid: e.tensor_tensor(
                        out=lo, in0=lo, in1=mid, op=ALU.min),
                        reads=[bth], writes=[bth], tag=f"cbmn3{s_}_{qs}")
                sch.step("scalar", lambda e, lo=lo, mid=mid: e.activation(
                    out=mid, in_=lo, func=AF.Abs),
                    reads=[bth], writes=[bth], tag=f"cbw{s_}_{qs}a")
                sch.step("vector", lambda e, lo=lo, mid=mid: e.scalar_tensor_tensor(
                    out=lo, in0=mid, scalar=-0.01, in1=lo, op0=ALU.mult, op1=ALU.add),
                    reads=[bth], writes=[bth], tag=f"cbw{s_}_{qs}b")
                sch.step("vector", lambda e, lo=lo: e.tensor_scalar(
                    out=lo, in0=lo, scalar1=-1e-3, scalar2=None, op0=ALU.add),
                    reads=[bth], writes=[bth], tag=f"cbw{s_}_{qs}c")
                for it in range(NIT):
                    sch.step("vector", lambda e, lo=lo, hi=hi, mid=mid: e.tensor_tensor(
                        out=mid, in0=lo, in1=hi, op=ALU.add),
                        reads=[bth], writes=[bth], tag=f"cbi{s_}_{qs}_{it}a")
                    sch.step("vector", lambda e, mid=mid: e.tensor_scalar(
                        out=mid, in0=mid, scalar1=0.5, scalar2=None, op0=ALU.mult),
                        reads=[bth], writes=[bth], tag=f"cbi{s_}_{qs}_{it}b")
                    sch.step("vector", lambda e, At=At, mid=mid, cnt=cnt, W=W:
                             e.tensor_scalar(
                        out=msk_scr[:, 0:W], in0=At[:, 0:W], scalar1=mid,
                        scalar2=0.0, op0=ALU.is_ge, op1=ALU.add, accum_out=cnt),
                        reads=[bAt, bth], writes=[b_mskscr, bth],
                        tag=f"cbi{s_}_{qs}_{it}c")
                    sch.step("vector", lambda e, cnt=cnt, tu=tu: e.tensor_scalar(
                        out=tu[:, 0:1], in0=cnt, scalar1=float(TOPK), scalar2=None,
                        op0=ALU.is_ge),
                        reads=[bth], writes=[btu], tag=f"cbi{s_}_{qs}_{it}d")
                    sch.step("vector", lambda e, lo=lo, mid=mid, tu=tu: e.select(
                        lo, tu[:, 0:1], mid, lo),
                        reads=[bth, btu], writes=[bth], tag=f"cbi{s_}_{qs}_{it}e")
                    sch.step("vector", lambda e, hi=hi, mid=mid, tu=tu: e.select(
                        hi, tu[:, 0:1], hi, mid),
                        reads=[bth, btu], writes=[bth], tag=f"cbi{s_}_{qs}_{it}f")
                sch.step("vector", lambda e, tu=tu, trow=trow: e.tensor_scalar(
                    out=tu[:, 1:2], in0=ovf[:, trow:trow + 1], scalar1=0.5,
                    scalar2=None, op0=ALU.is_gt),
                    reads=[b_ovf], writes=[btu], tag=f"cbo{s_}_{qs}")
                sch.step("vector", lambda e, th_f=th_f, lo=lo, tu=tu: e.select(
                    th_f, tu[:, 1:2], negmil[:], lo),
                    reads=[bth, btu, b_negmil], writes=[bth], tag=f"cbth{s_}_{qs}")
                if dbgth:
                    sch.step("vector", lambda e, th_f=th_f, trow=trow: e.tensor_copy(
                        out=dbg_th_sb[:, trow:trow + 1], in_=th_f),
                        reads=[bth], writes=[b_dbgth], tag=f"dbgth{s_}_{qs}")
                    sch.step("vector", lambda e, th=th, trow=trow: e.tensor_copy(
                        out=thsnap[:, trow * 8:(trow + 1) * 8], in_=th[:, 0:8]),
                        reads=[bth], writes=[b_thsnap], tag=f"dbgts{s_}_{qs}")
                    if s_ == 1 and qs == 0:
                        sch.step("sync", lambda e, At=At: e.dma_start(
                            out=dbg_A1[:], in_=At[:, :]),
                            reads=[bAt], dma=True, tag="dbg_A1")
                    sch.step("sync", lambda e, At=At, trow=trow: e.dma_start(
                        out=dbg_Aall[trow], in_=At[:, :]),
                        reads=[bAt], dma=True, tag=f"dbg_Aall{trow}")
                if dbg:
                    if s_ == 0 and qs == 0:
                        sch.step("sync", lambda e, At=At: e.dma_start(
                            out=dbg_A[:], in_=At[:, 0:1024]),
                            reads=[bAt], dma=True, tag="dbg_A")
                sch.step("vector", lambda e, At=At, th_f=th_f, W=W: e.tensor_scalar(
                    out=msk_scr[:, 0:W], in0=At[:, 0:W], scalar1=th_f, scalar2=0.0,
                    op0=ALU.is_ge, op1=ALU.add),
                    reads=[bAt, bth], writes=[b_mskscr], tag=f"cbm{s_}_{qs}")
                sch.step("vector", lambda e, W=W: e.tensor_scalar(
                    out=bbf[:, 0:W], in0=msk_scr[:, 0:W], scalar1=1.0e30,
                    scalar2=-1.0e30, op0=ALU.mult, op1=ALU.add),
                    reads=[b_mskscr], writes=[b_bbf], tag=f"cbb{s_}_{qs}")
                for kt in range(Ek):
                    c0 = (kt % 4) * 128
                    sch.step("tensor", lambda e, kt=kt, c0=c0: e.transpose(
                        ptbf[:, c0:c0 + 128], bbf[:, kt * 128:(kt + 1) * 128],
                        cx.ident_bf[:]),
                        reads=[b_bbf, cx.b_ident_bf], writes=[b_ptbf], tag=f"cbt{s_}_{qs}_{kt}")
                    sch.step("vector", lambda e, kt=kt, qs=qs, c0=c0: e.tensor_copy(
                        out=btT[:, kt * 512 + qs * 128:kt * 512 + (qs + 1) * 128],
                        in_=ptbf[:, c0:c0 + 128]),
                        reads=[b_ptbf], writes=[b_btT], tag=f"cbe{s_}_{qs}_{kt}")

            for hg in range(4):
                for kt in range(Ek):
                    r = kt % 3
                    sch.step("sync", lambda e, r=r, kt=kt, hg=hg: e.dma_start(
                        out=ktl[r][:].rearrange("p (a c) -> p a c", c=128),
                        in_=k_sp[kt, hg * 2:(hg + 1) * 2].rearrange("a p c -> p a c")),
                        reads=[b_ksp], writes=[b_ktl[r]], dma=True,
                        tag=f"cktl{s_}_{hg}_{kt}")
                    sch.step("sync", lambda e, r=r, kt=kt, hg=hg: e.dma_start(
                        out=vtl[r][:], in_=v_sp[kt, :, hg * 4 * 65:(hg + 1) * 4 * 65]),
                        reads=[b_vspb], writes=[b_vtl[r]], dma=True,
                        tag=f"cvtl{s_}_{hg}_{kt}")
                    for hl in range(4):
                        h = hg * 4 + hl
                        jq = hl % 2
                        pqk, bqk = qk[jq], b_qk[jq]
                        sch.step("tensor", lambda e, pqk=pqk, kt=kt: e.matmul(
                            pqk[:], cx.ident_bf[:], btT[:, kt * 512:(kt + 1) * 512],
                            start=True, stop=False, skip_group_check=True),
                            reads=[b_btT, cx.b_ident_bf], writes=[bqk],
                            tag=f"cbad{s_}_{hg}_{kt}_{hl}")
                        sch.step("tensor", lambda e, pqk=pqk, h=h, hl=hl, q0c=q0c,
                                 kt=kt: e.matmul(
                            pqk[:], ktl[kt % 3][(h % 2) * 64:(h % 2) * 64 + 64,
                                                (hl // 2) * 128:(hl // 2) * 128 + 128],
                            qsb[h // 2][(h % 2) * 64:(h % 2) * 64 + 64, q0c:q0c + 512],
                            start=False, stop=True, skip_group_check=True),
                            reads=[b_ktl[kt % 3], b_qsb[h // 2]], writes=[bqk],
                            tag=f"cqk{s_}_{hg}_{kt}_{hl}")
                        if hl % 2 == 1:
                            hp = hl // 2
                            ptt, bptt = pt_t[hp][kt % 2], b_pt[hp][kt % 2]
                            pairidx = hg * 2 + hp
                            sch.step("scalar", lambda e, ptt=ptt, pairidx=pairidx:
                                     e.activation(
                                out=ptt[:], in_=qkp[:], func=AF.Exp,
                                bias=upair[:, pairidx:pairidx + 1]),
                                reads=[b_qk[0], b_qk[1], b_upair], writes=[bptt],
                                tag=f"cex{s_}_{hg}_{kt}_{hl}")
                            for hh in (hl - 1, hl):
                                ppv, bpv = pv[hh], b_pv[hh]
                                sch.step("tensor", lambda e, ppv=ppv, hh=hh, kt=kt,
                                         ptt=ptt, Ek=Ek: e.matmul(
                                    ppv[:], vtl[kt % 3][:, hh * 65:(hh + 1) * 65],
                                    ptt[:, (hh % 2) * 512:(hh % 2 + 1) * 512],
                                    start=(kt == 0), stop=(kt == Ek - 1),
                                    skip_group_check=True),
                                    reads=[b_vtl[kt % 3], bptt], writes=[bpv],
                                    tag=f"cpv{s_}_{hg}_{kt}_{hh}")
                if dbg and s_ == 0 and hg == 0:
                    sch.step("sync", lambda e: e.dma_start(
                        out=dbg_pt[:], in_=pt_t[0][(EXT[0] * 4 - 1) % 2][:, 0:512]),
                        reads=[b_pt[0][(EXT[0] * 4 - 1) % 2]], dma=True, tag="dbg_pt")
                for hl in range(4):
                    h = hg * 4 + hl
                    ppv, bpv = pv[hl], b_pv[hl]
                    sch.step("vector", lambda e, ppv=ppv: e.reciprocal(
                        rsum[0:1, :], ppv[64:65, :]),
                        reads=[bpv], writes=[b_rsum], tag=f"crc{s_}_{h}")
                    if dbg and s_ == 0 and h == 0:
                        sch.step("sync", lambda e: e.dma_start(
                            out=dbg_rs[:], in_=rsum[0:1, :]),
                            reads=[b_rsum], dma=True, tag="dbg_rs")
                    sch.step("tensor", lambda e: e.matmul(
                        spare[0:64, :], cx.ones_row[0:1, 0:64], rsum[0:1, :],
                        start=True, stop=True, skip_group_check=True),
                        reads=[b_rsum, cx.b_ones_row], writes=[b_spare], tag=f"cbc{s_}_{h}")
                    sch.step("vector", lambda e, ppv=ppv: e.tensor_copy(
                        out=ytmp[:], in_=ppv[0:64, :]),
                        reads=[bpv], writes=[b_ytmp], tag=f"cyc{s_}_{h}")
                    hc, hb = h // 2, (h % 2) * 64
                    sch.step("vector", lambda e, hc=hc, hb=hb: e.tensor_tensor(
                        out=yfm[hc][hb:hb + 64, :], in0=ytmp[:], in1=spare[0:64, :],
                        op=ALU.mult),
                        reads=[b_ytmp, b_spare], writes=[b_yfm[hc]], tag=f"cyn{s_}_{h}")
            if dbg and s_ == 0:
                sch.step("sync", lambda e: e.dma_start(out=dbg_y[:], in_=yfm[0][:]),
                         reads=[b_yfm[0]], dma=True, tag="dbg_y")
                sch.step("sync", lambda e: e.dma_start(out=dbg_bt[:], in_=btT[:, 0:1024]),
                         reads=[b_btT], dma=True, tag="dbg_bt")
            for dm in range(DC):
                j = dm % 2
                pw, bw = qk[j], b_qk[j]
                for hc in range(DC):
                    sch.step("tensor", lambda e, pw=pw, hc=hc, dm=dm: e.matmul(
                        pw[:], wo[:, hc * 1024 + dm * 128:hc * 1024 + (dm + 1) * 128],
                        yfm[hc][:], start=(hc == 0), stop=(hc == DC - 1),
                        skip_group_check=True),
                        reads=[b_wo, b_yfm[hc]], writes=[bw], tag=f"cwo{s_}_{dm}_{hc}")
                sch.step("scalar", lambda e, pw=pw, dm=dm: e.activation(
                    out=ao[dm][:], in_=pw[:], func=AF.Identity),
                    reads=[bw], writes=[b_ao[dm]], tag=f"cwe{s_}_{dm}")
                if dbg and s_ == 0 and dm == 0:
                    sch.step("sync", lambda e: e.dma_start(out=dbg_ao[:], in_=ao[0][:]),
                             reads=[b_ao[0]], dma=True, tag="dbg_ao")
            for qs in range(4):
                j = qs % 2
                for dm in range(DC):
                    sch.step("tensor", lambda e, dm=dm, qs=qs: e.transpose(
                        ptbf[:, dm * 128:(dm + 1) * 128],
                        ao[dm][:, qs * 128:(qs + 1) * 128], cx.ident_bf[:]),
                        reads=[b_ao[dm], cx.b_ident_bf], writes=[b_ptbf],
                        tag=f"caot{s_}_{qs}_{dm}")
                sch.step("vector", lambda e, j=j: e.tensor_copy(
                    out=aot[j][:], in_=ptbf[:]),
                    reads=[b_ptbf], writes=[b_aot[j]], tag=f"caoe{s_}_{qs}")
                lnk.emit(s_ * 4 + qs, aot[j], b_aot[j], lnres[j], b_lnres[j],
                         f"cel{s_}_{qs}")
                row0 = s_ * 512 + qs * 128
                sch.step("sync", lambda e, j=j, row0=row0: e.dma_start(
                    out=xrow[j][:], in_=x_own_d[row0:row0 + 128, :]),
                    reads=list(XR), writes=[b_xrow[j]], dma=True, tag=f"cxl{s_}_{qs}")
                sch.step("vector", lambda e, j=j: e.tensor_tensor(
                    out=xout[j][:], in0=lnres[j][:], in1=xrow[j][:], op=ALU.add),
                    reads=[b_lnres[j], b_xrow[j]], writes=[b_xout[j]], tag=f"cad{s_}_{qs}")
                sch.step("sync", lambda e, j=j, row0=row0: e.dma_start(
                    out=xo_d[row0:row0 + 128, :], in_=xout[j][:]),
                    reads=[b_xout[j]], writes=list(XW), dma=True, tag=f"cst{s_}_{qs}")
        if dbgth:
            sch.step("sync", lambda e: e.dma_start(out=dbg_th[:], in_=dbg_th_sb[:]),
                     reads=[b_dbgth], dma=True, tag="dbg_th")
            sch.step("sync", lambda e: e.dma_start(out=dbg_ths[:], in_=thsnap[:]),
                     reads=[b_thsnap], dma=True, tag="dbg_ths")
            for j in range(6):
                sch.step("sync", lambda e, j=j: e.dma_start(
                    out=dbg_can[:, j * 8:(j + 1) * 8], in_=cans[j][:]),
                    reads=[b_cans[j]], dma=True, tag=f"dbg_can{j}")
        phS.close()

        if fx is None:
            sch.emit(block, sems)
    return pr


def stage_c_core_inputs(inputs, core, x_stage):
    b, p = core // 2, core % 2
    qkv = inputs["L1_attn_qkv_w"].astype(np.float32)
    rows = _own_rows(p)
    d = {
        "x_own": np.ascontiguousarray(x_stage[b][rows]),
        "x_batch": np.ascontiguousarray(x_stage[b]),
        "WqT": np.ascontiguousarray(qkv[0:D].T),
        "WkT": np.ascontiguousarray(qkv[D:2 * D].T),
        "WvT": np.ascontiguousarray(qkv[2 * D:3 * D].T),
        "WoT": np.ascontiguousarray(inputs["L1_attn_wo_w"].astype(np.float32).T),
        "iqT": np.ascontiguousarray(inputs["L1_attn_iq_w"].astype(np.float32).T),
        "ikT": np.ascontiguousarray(np.concatenate([inputs["L1_attn_ik_w"].astype(np.float32).T] * 2, axis=1)),
        "iwT": np.ascontiguousarray(inputs["L1_attn_iw_w"].astype(np.float32).T),
        "mask_qk": _causal_masks_qk(p),
        "th_ovr": np.ascontiguousarray(_theta_override(p).reshape(16, 128).T),
    }
    chi, clo, shi, slo = _rope_tables64(rows)
    d.update(rope_chi_o=chi, rope_clo_o=clo, rope_shi_o=shi, rope_slo_o=slo)
    chi, clo, shi, slo = _rope_tables64(np.arange(S))
    d.update(rope_chi_b=chi, rope_clo_b=clo, rope_shi_b=shi, rope_slo_b=slo)
    d.update(const_inputs())
    return d



def build_stage_d(fx=None):
    if fx is None:
        pr = Prog()
        st = pr.stack
    else:
        pr = fx["pr"]
        st = fx["st"]
    sch = pr.sch
    nc = pr.nc
    XR = [fx["x_in_buf"]] if fx else []
    XW = [fx["x_out_buf"]] if fx else []

    x_own_d = fx["x_in"] if fx else pr.din("x_own", [TOK, D])
    gwT_d = pr.din("gwT", [D, 8])
    ewT_d = pr.din("ewT", [8, D, D])
    xo_d = fx["x_out"] if fx else pr.dout("x_out", [TOK, D])
    if fx is None:
        sems = {e: pr.stack.enter_context(nc.semaphore("sem_" + e)) for e in ENGS}
        blk_cm = nc.Block()
    else:
        sems = fx["sems"]
        blk_cm = _UseBlock(fx["block"])

    with blk_cm as block:
        cx = fx["cx"] if fx else emit_consts(pr, st)
        lnk = LNKit(pr, st, "dlk")
        gw = pr.sb(st, "gw", [128, DC * 8], BF16)
        b_gw = sch.buf("gw")
        ew = [pr.sb(st, f"ew{e}", [128, DC * 1024], BF16) for e in range(8)]
        b_ew = [sch.buf(f"ew{e}") for e in range(8)]
        for c in range(DC):
            sch.step("gpsimd", lambda e, c=c: e.dma_start(
                out=gw[:, c * 8:(c + 1) * 8], in_=gwT_d[c * 128:(c + 1) * 128, :]),
                writes=[b_gw], dma=True, tag=f"dgw{c}")
            for ex in range(8):
                sch.step("gpsimd", lambda e, c=c, ex=ex: e.dma_start(
                    out=ew[ex][:, c * 1024:(c + 1) * 1024],
                    in_=ewT_d[ex, c * 128:(c + 1) * 128, :]),
                    writes=[b_ew[ex]], dma=True, tag=f"dew{ex}_{c}")

        xb = [pr.sb(st, f"dx{j}", [128, D], F32) for j in range(2)]
        bxb = [sch.buf(f"dx{j}") for j in range(2)]
        lnf = [pr.sb(st, f"dln{j}", [128, D], F32) for j in range(2)]
        b_lnf = [sch.buf(f"dln{j}") for j in range(2)]
        lnb = [pr.sb(st, f"dlb{j}", [128, D], BF16) for j in range(2)]
        b_lnb = [sch.buf(f"dlb{j}") for j in range(2)]
        lfm = [pr.sb(st, f"dfm{j}", [128, D], BF16) for j in range(2)]
        b_lfm = [sch.buf(f"dfm{j}") for j in range(2)]
        sco = pr.sb(st, "dsco", [128, 12], F32)
        b_sco = sch.buf("dsco")
        ohu = pr.sb(st, "dohu", [128, 24], U8)
        b_ohu = sch.buf("dohu")
        oh = pr.sb(st, "doh", [128, 12], F32)
        b_oh = sch.buf("doh")
        acc = [pr.sb(st, f"dacc{j}", [128, D], F32) for j in range(2)]
        b_acc = [sch.buf(f"dacc{j}") for j in range(2)]
        ffo = [pr.sb(st, f"dffo{j}", [128, D], F32) for j in range(2)]
        b_ffo = [sch.buf(f"dffo{j}") for j in range(2)]
        lnr = [pr.sb(st, f"dlr{j}", [128, D], F32) for j in range(2)]
        b_lnr = [sch.buf(f"dlr{j}") for j in range(2)]
        xou = [pr.sb(st, f"dxo{j}", [128, D], F32) for j in range(2)]
        b_xou = [sch.buf(f"dxo{j}") for j in range(2)]
        ptbf = pr.ps(st, "dptbf", [128, D], BF16)
        b_ptbf = sch.buf("dptbf")
        psc = pr.ps(st, "dpsc", [128, 8], F32)
        b_psc = sch.buf("dpsc")
        py = [pr.ps(st, f"dpy{j}", [128, 1024], F32) for j in range(2)]
        b_py = [sch.buf(f"dpy{j}") for j in range(2)]

        for i in range(16):
            j = i % 2
            sch.step("sync", lambda e, i=i, j=j: e.dma_start(
                out=xb[j][:], in_=x_own_d[i * 128:(i + 1) * 128, :]),
                reads=list(XR), writes=[bxb[j]], dma=True, tag=f"dld{i}")
            lnk.emit(i, xb[j], bxb[j], lnf[j], b_lnf[j], f"d1_{i}")
            sch.step("vector", lambda e, j=j: e.tensor_copy(out=lnb[j][:], in_=lnf[j][:]),
                     reads=[b_lnf[j]], writes=[b_lnb[j]], tag=f"dbf{i}")
            for c in range(DC):
                sch.step("tensor", lambda e, c=c, j=j: e.transpose(
                    ptbf[:, c * 128:(c + 1) * 128], lnb[j][:, c * 128:(c + 1) * 128],
                    cx.ident_bf[:]),
                    reads=[b_lnb[j], cx.b_ident_bf], writes=[b_ptbf], tag=f"dtr{i}_{c}")
            sch.step("vector", lambda e, j=j: e.tensor_copy(out=lfm[j][:], in_=ptbf[:]),
                     reads=[b_ptbf], writes=[b_lfm[j]], tag=f"dte{i}")
            for c in range(DC):
                sch.step("tensor", lambda e, c=c, j=j: e.matmul(
                    psc[:], lfm[j][:, c * 128:(c + 1) * 128], gw[:, c * 8:(c + 1) * 8],
                    start=(c == 0), stop=(c == DC - 1), skip_group_check=True),
                    reads=[b_lfm[j], b_gw], writes=[b_psc], tag=f"dsm{i}_{c}")
            sch.step("vector", lambda e, j=j: e.tensor_copy(
                out=sco[:, 0:8], in_=psc[:]),
                reads=[b_psc], writes=[b_sco], tag=f"dse{i}")
            sch.step("vector", lambda e: e.tensor_reduce(
                sco[:, 8:9], sco[:, 0:8], axis=AX.X, op=ALU.max),
                reads=[b_sco], writes=[b_sco], tag=f"dmx{i}")
            sch.step("vector", lambda e: e.tensor_scalar(
                out=ohu[:, 0:8], in0=sco[:, 0:8], scalar1=sco[:, 8:9], scalar2=None,
                op0=ALU.is_ge),
                reads=[b_sco], writes=[b_ohu], tag=f"deq{i}")
            sch.step("vector", lambda e: e.tensor_copy(out=oh[:, 0:8], in_=ohu[:, 0:8]),
                     reads=[b_ohu], writes=[b_oh], tag=f"deqf{i}")
            sch.step("vector", lambda e: e.memset(sco[:, 9:10], 0.0),
                     writes=[b_sco], tag=f"dr0{i}")
            for ex in range(8):
                if ex > 0:
                    sch.step("vector", lambda e, ex=ex: e.tensor_tensor(
                        out=sco[:, 9:10], in0=sco[:, 9:10], in1=oh[:, ex - 1:ex],
                        op=ALU.add),
                        reads=[b_oh, b_sco], writes=[b_sco], tag=f"dra{i}_{ex}")
                    sch.step("vector", lambda e, ex=ex: e.tensor_scalar(
                        out=ohu[:, 8 + ex:9 + ex], in0=sco[:, 9:10], scalar1=0.5,
                        scalar2=None, op0=ALU.is_lt),
                        reads=[b_sco], writes=[b_ohu], tag=f"drl{i}_{ex}")
                    sch.step("vector", lambda e, ex=ex: e.tensor_tensor(
                        out=ohu[:, 16 + ex:17 + ex], in0=ohu[:, ex:ex + 1],
                        in1=ohu[:, 8 + ex:9 + ex], op=ALU.logical_and),
                        reads=[b_ohu], writes=[b_ohu], tag=f"dan{i}_{ex}")
                else:
                    sch.step("vector", lambda e: e.tensor_copy(
                        out=ohu[:, 16:17], in_=ohu[:, 0:1]),
                        reads=[b_ohu], writes=[b_ohu], tag=f"dan{i}_0")
            sch.step("vector", lambda e: e.tensor_copy(
                out=oh[:, 0:8], in_=ohu[:, 16:24]),
                reads=[b_ohu], writes=[b_oh], tag=f"dohf{i}")
            for ex in range(8):
                jp = ex % 2
                pa, ba = py[jp], b_py[jp]
                for half in range(2):
                    for c in range(DC):
                        sch.step("tensor", lambda e, c=c, ex=ex, half=half, j=j, pa=pa:
                                 e.matmul(pa[:, half * 512:(half + 1) * 512],
                                          lfm[j][:, c * 128:(c + 1) * 128],
                                          ew[ex][:, c * 1024 + half * 512:c * 1024 + (half + 1) * 512],
                                          start=(c == 0), stop=(c == DC - 1),
                                          skip_group_check=True),
                                 reads=[b_lfm[j], b_ew[ex]], writes=[ba],
                                 tag=f"dym{i}_{ex}_{half}_{c}")
                ohc = oh[:, ex:ex + 1]
                if ex == 0:
                    sch.step("vector", lambda e, pa=pa, ohc=ohc, j=j: e.tensor_scalar(
                        out=acc[j][:], in0=pa[:], scalar1=ohc, scalar2=None,
                        op0=ALU.mult),
                        reads=[ba, b_oh], writes=[b_acc[j]], tag=f"dac{i}_{ex}")
                else:
                    sch.step("vector", lambda e, pa=pa, ohc=ohc, j=j:
                             e.scalar_tensor_tensor(
                        out=acc[j][:], in0=pa[:], scalar=ohc, in1=acc[j][:],
                        op0=ALU.mult, op1=ALU.add),
                        reads=[ba, b_oh, b_acc[j]], writes=[b_acc[j]], tag=f"dac{i}_{ex}")
            sch.step("vector", lambda e, j=j: e.tensor_tensor(
                out=ffo[j][:], in0=lnf[j][:], in1=acc[j][:], op=ALU.add),
                reads=[b_lnf[j], b_acc[j]], writes=[b_ffo[j]], tag=f"dff{i}")
            lnk.emit(i + 16, ffo[j], b_ffo[j], lnr[j], b_lnr[j], f"d2_{i}")
            sch.step("vector", lambda e, j=j: e.tensor_tensor(
                out=xou[j][:], in0=lnr[j][:], in1=xb[j][:], op=ALU.add),
                reads=[b_lnr[j], bxb[j]], writes=[b_xou[j]], tag=f"dad{i}")
            sch.step("sync", lambda e, i=i, j=j: e.dma_start(
                out=xo_d[i * 128:(i + 1) * 128, :], in_=xou[j][:]),
                reads=[b_xou[j]], writes=list(XW), dma=True, tag=f"dst{i}")

        if fx is None:
            sch.emit(block, sems)
    return pr


def stage_d_core_inputs(inputs, core, x_stage):
    b, p = core // 2, core % 2
    ew = inputs["L1_ffn_expert_w"].astype(np.float32)
    d = {
        "x_own": np.ascontiguousarray(x_stage[b][_own_rows(p)]),
        "gwT": np.ascontiguousarray(inputs["L1_ffn_gate_w"].astype(np.float32).T),
        "ewT": np.ascontiguousarray(ew.transpose(0, 2, 1)),
    }
    d.update(const_inputs())
    return d



_PROGS = {}


def _get_prog(name, builder):
    if name not in _PROGS:
        _PROGS[name] = builder()
    return _PROGS[name]


def kernel(**inputs):
    return kernel_fused(**inputs)


def kernel_unfused(**inputs):
    inputs = dict(inputs)
    x = np.ascontiguousarray(np.asarray(inputs["L0_attn_x_bld"], np.float32))
    inputs["x_full"] = x

    prA = _get_prog("A", build_stage_a)
    res = run_bass_kernel_spmd(prA.nc, [stage_a_core_inputs(inputs, c) for c in range(8)],
                               core_ids=list(range(8)))
    x1 = _unshard([res.results[c]["x_out"] for c in range(8)])

    prB = _get_prog("B", build_stage_b)
    res = run_bass_kernel_spmd(prB.nc, [stage_b_core_inputs(inputs, c, x1) for c in range(8)],
                               core_ids=list(range(8)))
    x2 = _unshard([res.results[c]["x_out"] for c in range(8)])

    prC = _get_prog("C", build_stage_c)
    res = run_bass_kernel_spmd(prC.nc, [stage_c_core_inputs(inputs, c, x2) for c in range(8)],
                               core_ids=list(range(8)))
    x3 = _unshard([res.results[c]["x_out"] for c in range(8)])

    prD = _get_prog("D", build_stage_d)
    res = run_bass_kernel_spmd(prD.nc, [stage_d_core_inputs(inputs, c, x3) for c in range(8)],
                               core_ids=list(range(8)))
    return _unshard([res.results[c]["x_out"] for c in range(8)])



def build_fused(dbg=False):
    pr = Prog()
    nc = pr.nc
    sch = pr.sch
    sems = {e: pr.stack.enter_context(nc.semaphore("sem_" + e)) for e in ENGS}

    with nc.Block() as block:
        cx = emit_consts(pr, pr.stack)
        x1_dr = nc.dram_tensor("x1_dr", [TOK, D], F32)
        x2_dr = nc.dram_tensor("x2_dr", [TOK, D], F32)
        x3_dr = nc.dram_tensor("x3_dr", [TOK, D], F32)
        ag_out = nc.dram_tensor("ag_out", [8 * TOK, D], F32)
        b_x1 = sch.buf("x1_dr", kind="d")
        b_x2 = sch.buf("x2_dr", kind="d")
        b_x3 = sch.buf("x3_dr", kind="d")
        b_ag = sch.buf("ag_out", kind="d")
        b_x0 = sch.buf("x0_dummy", kind="d")

        pr.ns = "a_"
        x0_d = pr.din("x_own", [TOK, D])
        st_a = ExitStack()
        build_stage_a(fx=dict(pr=pr, st=st_a, block=block, sems=sems, cx=cx,
                              x_in=x0_d, x_in_buf=b_x0, x_out=x1_dr, x_out_buf=b_x1,
                              dbga=dbg))
        st_a.close()
        sch.barrier()

        pr.ns = "b_"
        st_b = ExitStack()
        build_stage_b(fx=dict(pr=pr, st=st_b, block=block, sems=sems, cx=cx,
                              x_in=x1_dr, x_in_buf=b_x1, x_out=x2_dr, x_out_buf=b_x2))
        st_b.close()
        sch.barrier()

        sch.step("gpsimd", lambda e: e.collective_compute(
            "AllGather", ALU.bypass, replica_groups=[list(range(8))],
            ins=[x2_dr.ap().opt()], outs=[ag_out.ap().opt()]),
            reads=[b_x2], writes=[b_ag], collective=True, tag="ag_x2", gsem="cc")

        pr.ns = "c_"
        st_c = ExitStack()
        build_stage_c(fx=dict(pr=pr, st=st_c, block=block, sems=sems, cx=cx,
                              dbgth=dbg,
                              x_in=x2_dr, x_in_buf=b_x2, x_out=x3_dr, x_out_buf=b_x3,
                              ag_out=ag_out, ag_out_buf=b_ag))
        st_c.close()
        sch.barrier()

        pr.ns = "d_"
        xo_d = pr.dout("x_out", [TOK, D])
        st_d = ExitStack()
        build_stage_d(fx=dict(pr=pr, st=st_d, block=block, sems=sems, cx=cx,
                              x_in=x3_dr, x_in_buf=b_x3, x_out=xo_d,
                              x_out_buf=sch.buf("xo_dummy", kind="d")))
        st_d.close()

        pr.ns = ""
        if dbg:
            for nm, t, bb in (("dbg_x1", x1_dr, b_x1), ("dbg_x2", x2_dr, b_x2),
                              ("dbg_x3", x3_dr, b_x3), ("dbg_ag", ag_out, b_ag)):
                od = pr.dout(nm, list(t.shape))
                sch.step("gpsimd", lambda e, od=od, t=t: e.dma_start(
                    out=od[:, :], in_=t[:, :]),
                    reads=[bb], dma=True, tag=nm)
        sch.emit(block, sems)
    return pr


def fused_core_inputs(inputs, core):
    x = inputs["x_full"]
    b = core // 2
    d = dict(const_inputs())
    for pfx, fn, xs in (("a_", stage_a_core_inputs, None),
                        ("b_", stage_b_core_inputs, x),
                        ("c_", stage_c_core_inputs, x),
                        ("d_", stage_d_core_inputs, x)):
        sub = fn(inputs, core) if xs is None else fn(inputs, core, xs)
        for k, v in sub.items():
            if k in ("ident_bf", "ident_f", "ones_bf", "ones_f", "ones_row"):
                continue
            if pfx != "a_" and k in ("x_own", "x_batch"):
                continue
            d[pfx + k] = v
    bfl = np.zeros((128, 4), np.float32)
    bfl[:, b] = 1.0
    d["c_bflags"] = bfl
    return d


def kernel_fused(**inputs):
    inputs = dict(inputs)
    inputs["x_full"] = np.ascontiguousarray(np.asarray(inputs["L0_attn_x_bld"], np.float32))
    prF = _get_prog("F", build_fused)
    res = run_bass_kernel_spmd(prF.nc, [fused_core_inputs(inputs, c) for c in range(8)],
                               core_ids=list(range(8)))
    return _unshard([res.results[c]["d_x_out"] for c in range(8)])
